# version 01
# baseline (speedup 1.0000x reference)
import numpy as np
import ml_dtypes
from contextlib import ExitStack

import concourse.bass as bass
import concourse.mybir as mybir
from concourse.bass_utils import run_bass_kernel_spmd
from concourse.aot_env import allow_ici_all_to_all

F32 = mybir.dt.float32
BF16 = mybir.dt.bfloat16
I32 = mybir.dt.int32
AF = mybir.ActivationFunctionType
ALU = mybir.AluOpType
AX = mybir.AxisListType

NCORE = 8
B, S, D, Fm = 2, 4096, 4096, 11008
H, KVH, HD = 32, 8, 128
TLOC = 1024
NTT = 8
FSL = Fm // NCORE
WDSL = D // NCORE
FCH = Fm // 128
NEG = -1e30
QW = 1024
I_ABS = int(np.int64(0x7FFFFFFF))
I_TRUNC = int(np.int64(0xFFC00000) - (1 << 32))
I_EXP = int(np.int64(0xFF800000) - (1 << 32))
I_SIGN = int(np.int64(0x80000000) - (1 << 32))


class Sem:
    def __init__(self, stk, nc, name):
        self.h = stk.enter_context(nc.semaphore(name))
        self.v = 0

    def bump(self, n=1):
        self.v += n
        return self.v


class Bld:
    ENGS = ("sync", "scalar", "vector", "tensor", "gpsimd")

    def __init__(self, nc, stk):
        self.nc, self.stk = nc, stk
        self.q = {e: [] for e in self.ENGS}

    def op(self, eng, meth, *args, waits=(), incs=(), a2a=False, **kwargs):
        self.q[eng].append((meth, args, kwargs, list(waits), list(incs), a2a))
        for sem, n in incs:
            sem.bump(n)

    def wait(self, eng, *waits):
        self.q[eng].append((None, (), {}, list(waits), [], False))

    def emit(self, block):
        for eng in self.ENGS:
            ops = self.q[eng]

            def make(ops_):
                def run(e):
                    for meth, args, kwargs, waits, incs, a2a in ops_:
                        for sem, val in waits:
                            e.wait_ge(sem.h, val)
                        if meth is None:
                            continue
                        if a2a:
                            with allow_ici_all_to_all():
                                inst = getattr(e, meth)(*args, **kwargs)
                        else:
                            inst = getattr(e, meth)(*args, **kwargs)
                        for sem, n in incs:
                            inst.then_inc(sem.h, n)
                return run
            getattr(block, eng)(make(ops))


def build(stage=10):
    nc = bass.Bass("TRN2", target_bir_lowering=False, debug=False)
    stk = ExitStack()
    bd = Bld(nc, stk)
    T = {}

    def dram(name, shape, dtype, kind="Internal"):
        T[name] = nc.dram_tensor(name, shape, dtype, kind=kind)

    dram("xp", [TLOC, D], F32, "ExternalInput")
    dram("xres", [TLOC, D], F32, "ExternalInput")
    dram("wq", [48, 128, 32, 128], F32, "ExternalInput")
    dram("wo", [8, 32, 128, 512], F32, "ExternalInput")
    dram("wg", [FSL, D], F32, "ExternalInput")
    dram("wu", [FSL, D], F32, "ExternalInput")
    dram("wd", [WDSL, Fm], F32, "ExternalInput")
    dram("msc", [128, 4], F32, "ExternalInput")
    dram("lra", [128, 32, 64], F32, "ExternalInput")
    dram("lrb", [64, D], F32, "ExternalInput")
    dram("cosq", [64, TLOC], F32, "ExternalInput")
    dram("sinq", [64, TLOC], F32, "ExternalInput")
    dram("cosk", [64, TLOC], F32, "ExternalInput")
    dram("sink", [64, TLOC], F32, "ExternalInput")
    dram("ident", [128, 128], BF16, "ExternalInput")
    dram("icon", [128, 4], I32, "ExternalInput")
    dram("fcon", [128, 4], F32, "ExternalInput")
    dram("ones", [1, 128], BF16, "ExternalInput")
    dram("onesf", [1, 128], F32, "ExternalInput")
    dram("cmask", [128, 128], F32, "ExternalInput")
    dram("out", [TLOC, D], F32, "ExternalOutput")

    dram("wgt_c", [D, FSL], BF16)
    dram("wut_c", [D, FSL], BF16)
    dram("wdt_c", [Fm, WDSL], BF16)
    dram("wgt_g", [NCORE * D, FSL], BF16)
    dram("wut_g", [NCORE * D, FSL], BF16)
    dram("wdt_g", [NCORE * Fm, WDSL], BF16)
    dram("qkv_c", [NCORE * 768, TLOC], BF16)
    dram("qkv_x", [NCORE * 768, TLOC], BF16)
    dram("at_c", [NCORE * 512, TLOC], BF16)
    dram("at_x", [NCORE * 512, TLOC], BF16)
    dram("mx_c", [1, 8], F32)
    dram("mx_g", [1, 8], F32)
    dram("hm_c", [1, 8], F32)
    dram("hm_g", [1, 8], F32)
    dram("hsp", [TLOC, Fm], F32)
    dram("sc1", [128, 1], F32)
    if stage < 10:
        dram("dbg_xq", [128, D], BF16, "ExternalOutput")
        dram("dbg_xn", [128, D], BF16, "ExternalOutput")
        dram("dbg_wgt", [256, FSL], BF16, "ExternalOutput")
        dram("dbg_wdt", [256, WDSL], BF16, "ExternalOutput")
        dram("dbg_qkv", [768, TLOC], BF16, "ExternalOutput")
        dram("dbg_mx", [1, 8], F32, "ExternalOutput")
    if stage < 10 and stage >= 2:
        dram("dbg_h", [128, Fm], F32, "ExternalOutput")
        dram("dbg_hm", [1, 8], F32, "ExternalOutput")
    if stage < 10 and stage >= 3:
        dram("dbg_at", [512, TLOC], BF16, "ExternalOutput")

    ap = {k: v.ap() for k, v in T.items()}

    sX = Sem(stk, nc, "sX")
    sXa = Sem(stk, nc, "sXa")
    sXb = Sem(stk, nc, "sXb")
    sXc = Sem(stk, nc, "sXc")
    sXh = Sem(stk, nc, "sXh")
    sXat = Sem(stk, nc, "sXat")
    sXr = Sem(stk, nc, "sXr")
    sXw = [Sem(stk, nc, f"sXw{i}") for i in range(4)]
    sS = Sem(stk, nc, "sS")
    sSh = [Sem(stk, nc, f"sSh{i}") for i in range(4)]
    sSq = [Sem(stk, nc, f"sSq{i}") for i in range(2)]
    sSo = [Sem(stk, nc, f"sSo{i}") for i in range(2)]
    sSt2 = [Sem(stk, nc, f"sSt2{i}") for i in range(2)]
    sG = Sem(stk, nc, "sG")
    sGa = Sem(stk, nc, "sGa")
    sGb = Sem(stk, nc, "sGb")
    sGc = Sem(stk, nc, "sGc")
    sGw = [Sem(stk, nc, f"sGw{i}") for i in range(4)]
    sE = Sem(stk, nc, "sE")
    sT = Sem(stk, nc, "sT")
    sV = Sem(stk, nc, "sV")
    sA = Sem(stk, nc, "sA")
    sCC = [Sem(stk, nc, f"sCC{i}") for i in range(7)]
    CC_MAX, CC_WG, CC_WU, CC_WD, CC_QKV, CC_HM, CC_AT = range(7)

    def sb(name, shape, dtype):
        return stk.enter_context(nc.sbuf_tensor(name, shape, dtype))

    ident = sb("ident_s", [128, 128], BF16)
    icon = sb("icon_s", [128, 4], I32)
    fcon = sb("fcon_s", [128, 4], F32)
    cmask = sb("cmask_s", [128, 128], F32)
    ones = sb("ones_s", [1, 128], BF16)
    onesf = sb("onesf_s", [1, 128], F32)
    msc_sb = sb("msc_s", [128, 4], F32)
    lnstat = sb("lnstat_s", [128, 16], F32)
    lnsm = sb("lnsm_s", [128, 8], F32)
    scl = sb("scl_s", [128, 10], F32)
    smax = sb("smax_s", [128, 6], F32)
    srow = sb("srow_s", [1, 16], F32)
    prow = sb("prow_s", [1, 128], F32)
    tfm = sb("tfm_s", [64, TLOC], BF16)

    q_bm = sb("q_bm", [128, QW // 16], F32)
    q_sbf = sb("q_sbf", [128, QW // 16], F32)
    q_v = sb("q_v", [128, QW // 16], F32)
    q_rv = sb("q_rv", [128, QW // 16], F32)
    q_a = sb("q_a", [128, QW], F32)
    q_b = sb("q_b", [128, QW], F32)
    q_l = sb("q_l", [128, QW], F32)
    q_m = sb("q_m", [128, QW], F32)

    qstate = {"last_cast": 0}

    def qdq_tile(src, dst_bf, rc, cw, s_ap, rs6_ap, first_waits, extra_scale=None,
                 cast_waits=()):
        nb = cw // 16
        srcb = src.rearrange("p (n b) -> p n b", b=16)
        bd.op("vector", "tensor_reduce",
              out=q_bm[0:rc, 0:nb].rearrange("p (n o) -> p n o", o=1),
              in_=srcb, axis=AX.X, op=ALU.max, apply_absolute_value=True,
              waits=list(first_waits) + [(sA, qstate["last_cast"])], incs=[(sE, 1)])
        bd.op("vector", "tensor_scalar", out=q_sbf[0:rc, 0:nb], in0=q_bm[0:rc, 0:nb],
              scalar1=rs6_ap, scalar2=1e-12, op0=ALU.mult, op1=ALU.max,
              waits=[(sE, sE.v)], incs=[(sE, 1)])
        bd.op("vector", "tensor_scalar", out=q_sbf[0:rc, 0:nb], in0=q_sbf[0:rc, 0:nb],
              scalar1=448.0, scalar2=None, op0=ALU.min, waits=[(sE, sE.v)], incs=[(sE, 1)])
        bd.op("vector", "tensor_scalar", out=q_v[0:rc, 0:nb], in0=q_sbf[0:rc, 0:nb],
              scalar1=s_ap, scalar2=None, op0=ALU.mult, waits=[(sE, sE.v)], incs=[(sE, 1)])
        bd.op("vector", "reciprocal", out=q_rv[0:rc, 0:nb], in_=q_v[0:rc, 0:nb],
              waits=[(sE, sE.v)], incs=[(sE, 1)])
        rv_b = q_rv[0:rc, 0:nb].rearrange("p (n o) -> p n o", o=1).broadcast_to((rc, nb, 16))
        v_b = q_v[0:rc, 0:nb].rearrange("p (n o) -> p n o", o=1).broadcast_to((rc, nb, 16))
        av = q_a[0:rc, 0:cw]
        bd.op("vector", "tensor_tensor", av.rearrange("p (n b) -> p n b", b=16), srcb, rv_b,
              ALU.mult, waits=[(sE, sE.v)])
        bd.op("vector", "tensor_scalar", out=q_b[0:rc, 0:cw].bitcast(I32), in0=av.bitcast(I32),
              scalar1=icon[0:rc, 0:1], scalar2=None, op0=ALU.bitwise_and, incs=[(sV, 1)])
        vb = sV.v
        bd.op("scalar", "activation", out=q_l[0:rc, 0:cw], in_=q_b[0:rc, 0:cw],
              func=AF.Identity, bias=fcon[0:rc, 0:1], waits=[(sV, vb)], incs=[(sA, 1)])
        al = sA.v
        bd.op("vector", "tensor_scalar", out=q_m[0:rc, 0:cw].bitcast(I32), in0=av.bitcast(I32),
              scalar1=icon[0:rc, 2:3], scalar2=None, op0=ALU.bitwise_and)
        bd.op("vector", "scalar_tensor_tensor", out=q_m[0:rc, 0:cw], in0=q_m[0:rc, 0:cw],
              scalar=0.25, in1=av, op0=ALU.mult, op1=ALU.add)
        bd.op("vector", "tensor_scalar", out=q_m[0:rc, 0:cw].bitcast(I32),
              in0=q_m[0:rc, 0:cw].bitcast(I32), scalar1=icon[0:rc, 1:2],
              scalar2=icon[0:rc, 0:1], op0=ALU.bitwise_and, op1=ALU.bitwise_and)
        bd.op("vector", "tensor_single_scalar", out=q_b[0:rc, 0:cw].bitcast(I32),
              in_=q_b[0:rc, 0:cw], scalar=1.0, op=ALU.is_lt)
        bd.op("vector", "tensor_scalar", out=q_l[0:rc, 0:cw].bitcast(I32),
              in0=q_l[0:rc, 0:cw].bitcast(I32), scalar1=icon[0:rc, 1:2], scalar2=None,
              op0=ALU.bitwise_and, waits=[(sA, al)])
        bd.op("vector", "tensor_scalar", out=q_l[0:rc, 0:cw], in0=q_l[0:rc, 0:cw],
              scalar1=1.0, scalar2=None, op0=ALU.subtract)
        bd.op("vector", "select", out=q_m[0:rc, 0:cw], mask=q_b[0:rc, 0:cw].bitcast(I32),
              on_true=q_l[0:rc, 0:cw], on_false=q_m[0:rc, 0:cw])
        bd.op("vector", "tensor_tensor", q_m[0:rc, 0:cw].rearrange("p (n b) -> p n b", b=16),
              q_m[0:rc, 0:cw].rearrange("p (n b) -> p n b", b=16), v_b, ALU.mult)
        if extra_scale is not None:
            bd.op("vector", "tensor_scalar", out=q_m[0:rc, 0:cw], in0=q_m[0:rc, 0:cw],
                  scalar1=extra_scale, scalar2=None, op0=ALU.mult)
        bd.op("vector", "scalar_tensor_tensor", out=q_m[0:rc, 0:cw].bitcast(I32),
              in0=src.bitcast(I32), scalar=icon[0:rc, 3:4], in1=q_m[0:rc, 0:cw].bitcast(I32),
              op0=ALU.bitwise_and, op1=ALU.bitwise_or, incs=[(sV, 1)])
        v_done = sV.v
        bd.op("scalar", "activation", out=dst_bf, in_=q_m[0:rc, 0:cw], func=AF.Copy,
              waits=[(sV, v_done)] + list(cast_waits), incs=[(sA, 1)])
        qstate["last_cast"] = sA.v
        return v_done, sA.v

    def pcol_to_srow(col_ap, dstc, first_waits):
        bd.op("scalar", "dma_start", out=ap["sc1"], in_=col_ap,
              waits=list(first_waits) + [(sXc, sXc.v)], incs=[(sS, 16)])
        bd.op("sync", "dma_start",
              out=prow[0:1, 0:128].rearrange("o (a b) -> o a b", a=1),
              in_=ap["sc1"].rearrange("(a p) o -> a o p", a=1),
              waits=[(sS, sS.v)], incs=[(sXc, 16)])
        bd.op("vector", "tensor_reduce", out=srow[0:1, dstc:dstc + 1],
              in_=prow[0:1, 0:128].rearrange("o (a b) -> o a b", a=1),
              axis=AX.X, op=ALU.max, waits=[(sXc, sXc.v)], incs=[(sV, 1)])

    for name, dst in (("ident", ident), ("icon", icon), ("fcon", fcon), ("cmask", cmask),
                      ("ones", ones), ("onesf", onesf), ("msc", msc_sb)):
        bd.op("sync", "dma_start", out=dst[:], in_=ap[name], incs=[(sX, 16)])
    bd.op("vector", "memset", smax[:], 0.0, waits=[(sX, sX.v)], incs=[(sV, 1)])

    with ExitStack() as ph12:
        sbA = lambda n, s, d: ph12.enter_context(nc.sbuf_tensor(n, s, d))
        xsc = sbA("xsc", [128, D], F32)
        lscr = sbA("lscr", [128, D], F32)
        w1buf = sbA("w1buf", [128, 2 * 2048], F32)

        for tt in range(NTT):
            w = [(sV, sV.v)] if tt > 0 else []
            bd.op("sync", "dma_start", out=xsc[:], in_=ap["xp"][tt * 128:(tt + 1) * 128, :],
                  waits=w, incs=[(sXc, 16)])
            ld = sXc.v
            bd.op("vector", "tensor_reduce", out=lnsm[:, 0:1], in_=xsc[:], axis=AX.X,
                  op=ALU.add, waits=[(sXc, ld)], incs=[(sE, 1)])
            bd.op("scalar", "activation", out=lscr[:], in_=xsc[:], func=AF.Square,
                  accum_out=lnsm[:, 1:2], waits=[(sXc, ld)], incs=[(sA, 1)])
            bd.op("vector", "tensor_scalar", out=lnstat[:, 2 * tt:2 * tt + 1], in0=lnsm[:, 0:1],
                  scalar1=1.0 / D, scalar2=None, op0=ALU.mult,
                  waits=[(sA, sA.v), (sE, sE.v)], incs=[(sE, 1)])
            bd.op("vector", "tensor_scalar", out=lnsm[:, 3:4], in0=lnsm[:, 1:2],
                  scalar1=1.0 / D, scalar2=None, op0=ALU.mult, incs=[(sE, 1)])
            bd.op("vector", "tensor_scalar", out=lnsm[:, 4:5], in0=lnstat[:, 2 * tt:2 * tt + 1],
                  scalar1=lnstat[:, 2 * tt:2 * tt + 1], scalar2=None, op0=ALU.mult,
                  waits=[(sE, sE.v)], incs=[(sE, 1)])
            bd.op("vector", "tensor_sub", lnsm[:, 5:6], lnsm[:, 3:4], lnsm[:, 4:5],
                  waits=[(sE, sE.v)], incs=[(sE, 1)])
            bd.op("vector", "tensor_scalar", out=lnsm[:, 5:6], in0=lnsm[:, 5:6],
                  scalar1=1e-5, scalar2=None, op0=ALU.add, waits=[(sE, sE.v)], incs=[(sE, 1)])
            bd.op("vector", "reciprocal", out=lnsm[:, 7:8], in_=lnsm[:, 5:6],
                  waits=[(sE, sE.v)], incs=[(sV, 1)])
            bd.op("scalar", "activation", out=lnstat[:, 2 * tt + 1:2 * tt + 2], in_=lnsm[:, 7:8],
                  func=AF.Sqrt, waits=[(sV, sV.v)], incs=[(sA, 1)])
            bd.op("vector", "tensor_scalar", out=xsc[:], in0=xsc[:],
                  scalar1=lnstat[:, 2 * tt:2 * tt + 1], scalar2=lnstat[:, 2 * tt + 1:2 * tt + 2],
                  op0=ALU.subtract, op1=ALU.mult, waits=[(sA, sA.v)])
            bd.op("vector", "tensor_reduce", out=lnsm[:, 0:1], in_=xsc[:], axis=AX.X,
                  op=ALU.max, apply_absolute_value=True, incs=[(sE, 1)])
            bd.op("vector", "tensor_max", smax[:, 3:4], smax[:, 3:4], lnsm[:, 0:1],
                  waits=[(sE, sE.v)], incs=[(sV, 1)])

        def wtiles(colw):
            out = []
            for name, rows, cols, smcol in (("wg", FSL, D, 0), ("wu", FSL, D, 1), ("wd", WDSL, Fm, 2)):
                r0 = 0
                while r0 < rows:
                    rc = min(128, rows - r0)
                    c0 = 0
                    while c0 < cols:
                        cc = min(colw, cols - c0)
                        out.append((name, r0, rc, c0, cc, smcol))
                        c0 += cc
                    r0 += rc
            return out

        WT1 = wtiles(2048)
        p2red = []
        for i, (name, r0, rc, c0, cc, smcol) in enumerate(WT1):
            buf = w1buf[0:rc, (i % 2) * 2048:(i % 2) * 2048 + cc]
            sp = sXa if i % 2 == 0 else sXb
            w = [] if i < 2 else [(sV, p2red[i - 2])]
            bd.op("sync", "dma_start", out=buf, in_=ap[name][r0:r0 + rc, c0:c0 + cc],
                  waits=w, incs=[(sp, 16)])
            bd.op("vector", "tensor_reduce", out=lnsm[0:rc, 0:1], in_=buf, axis=AX.X,
                  op=ALU.max, apply_absolute_value=True, waits=[(sp, sp.v)], incs=[(sE, 1)])
            bd.op("vector", "tensor_max", smax[0:rc, smcol:smcol + 1],
                  smax[0:rc, smcol:smcol + 1], lnsm[0:rc, 0:1], waits=[(sE, sE.v)], incs=[(sV, 1)])
            p2red.append(sV.v)
        w1_done = sV.v

        for t in range(4):
            pcol_to_srow(smax[:, t:t + 1], t, [(sV, w1_done)] if t == 0 else [])
        bd.op("vector", "memset", srow[0:1, 4:8], 0.0, incs=[(sV, 1)])
        bd.op("scalar", "dma_start", out=ap["mx_c"], in_=srow[0:1, 0:8],
              waits=[(sV, sV.v)], incs=[(sS, 16)])
        bd.wait("gpsimd", (sS, sS.v))
        bd.op("gpsimd", "collective_compute", "AllReduce", ALU.max,
              replica_groups=[list(range(NCORE))], ins=[ap["mx_c"].opt()],
              outs=[ap["mx_g"].opt()], incs=[(sCC[CC_MAX], 1)])
        bd.op("sync", "dma_start", out=srow[0:1, 0:8], in_=ap["mx_g"],
              waits=[(sCC[CC_MAX], 1)], incs=[(sXc, 16)])
        mxg_ld = sXc.v
        if stage < 10:
            bd.op("gpsimd", "dma_start", out=ap["dbg_mx"], in_=ap["mx_g"],
                  waits=[(sCC[CC_MAX], 1)], incs=[(sG, 16)])


    ph57 = stk.enter_context(ExitStack())
    xqfm = ph57.enter_context(nc.sbuf_tensor("xqfm", [128, 32 * TLOC], BF16))
    with ExitStack() as phA:
        sbB = lambda n, s, d: phA.enter_context(nc.sbuf_tensor(n, s, d))
        psB = lambda n, s, d: phA.enter_context(nc.psum_tensor(n, s, d))
        pbc = psB("pbc", [128, 512], F32)
        ptr = psB("ptr", [128, 1024], BF16)
        ptr2 = psB("ptr2", [128, 1024], BF16)
        qkps = [psB(f"qkps{i}", [128, 512], F32) for i in range(4)]
        tps = psB("tps", [64, 512], F32)

        bd.op("tensor", "matmul", pbc[:, 0:4], onesf[:], srow[0:1, 0:4],
              start=True, stop=True, waits=[(sXc, mxg_ld)], incs=[(sT, 1)])
        bd.op("vector", "tensor_scalar", out=scl[:, 0:4], in0=pbc[:, 0:4],
              scalar1=1.0 / 2688.0, scalar2=1e-12, op0=ALU.mult, op1=ALU.max,
              waits=[(sT, sT.v)], incs=[(sE, 1)])
        bd.op("vector", "tensor_scalar", out=scl[:, 4:8], in0=scl[:, 0:4],
              scalar1=6.0, scalar2=None, op0=ALU.mult, waits=[(sE, sE.v)], incs=[(sE, 1)])
        bd.op("vector", "reciprocal", out=scl[:, 4:8], in_=scl[:, 4:8],
              waits=[(sE, sE.v)], incs=[(sV, 1)])
        scales_v = sV.v

        xnfm = sbB("xnfm", [128, 32 * TLOC], BF16)
        phA5 = phA.enter_context(ExitStack())
        xsc2 = phA5.enter_context(nc.sbuf_tensor("xsc2", [128, D], F32))
        xqt = phA5.enter_context(nc.sbuf_tensor("xqt", [128, D], BF16))
        xnt = phA5.enter_context(nc.sbuf_tensor("xnt", [128, D], BF16))
        xnfm3 = xnfm[:].rearrange("p (c t) -> p c t", t=TLOC)
        xqfm3 = xqfm[:].rearrange("p (c t) -> p c t", t=TLOC)
        last_sign_v = scales_v
        prev_cast_a = 0
        tpose_prev = 0
        tb_ev = {0: (sV, 0), 1: (sV, 0)}
        for tt in range(NTT):
            bd.op("sync", "dma_start", out=xsc2[:], in_=ap["xp"][tt * 128:(tt + 1) * 128, :],
                  waits=[(sV, last_sign_v), (sA, prev_cast_a)], incs=[(sXc, 16)])
            ld = sXc.v
            bd.op("vector", "tensor_scalar", out=xsc2[:], in0=xsc2[:],
                  scalar1=lnstat[:, 2 * tt:2 * tt + 1], scalar2=lnstat[:, 2 * tt + 1:2 * tt + 2],
                  op0=ALU.subtract, op1=ALU.mult, waits=[(sXc, ld), (sV, scales_v)])
            a_last = None
            for c0, cw in ((0, 1024), (1024, 1024), (2048, 1024), (3072, 896)):
                vd, a_last = qdq_tile(xsc2[:, c0:c0 + cw], xqt[:, c0:c0 + cw], 128, cw,
                                      scl[:, 3:4], scl[:, 7:8], [],
                                      cast_waits=[(sT, tpose_prev)])
                last_sign_v = vd
            bd.op("scalar", "activation", out=xqt[:, D - 128:D], in_=xsc2[:, D - 128:D],
                  func=AF.Copy, incs=[(sA, 1)])
            bd.op("scalar", "activation", out=xnt[:], in_=xsc2[:], func=AF.Copy, incs=[(sA, 1)])
            cast_a = sA.v
            prev_cast_a = cast_a
            g = 0
            for srcb, dst3 in ((xnt, xnfm3), (xqt, xqfm3)):
                for dc4 in range(8):
                    bank = g % 2
                    pt = ptr if bank == 0 else ptr2
                    for j in range(4):
                        dc = dc4 * 4 + j
                        w = []
                        if j == 0:
                            w.append((sA, cast_a))
                            w.append(tb_ev[bank])
                        bd.op("tensor", "transpose", pt[:, j * 128:(j + 1) * 128],
                              srcb[:, dc * 128:(dc + 1) * 128], ident[:], waits=w,
                              incs=[(sT, 1)] if j == 3 else [])
                    bd.op("vector", "tensor_copy",
                          out=dst3[:, dc4 * 4:dc4 * 4 + 4, tt * 128:tt * 128 + 128],
                          in_=pt[:, 0:512].rearrange("p (a b) -> p a b", a=4),
                          waits=[(sT, sT.v)], incs=[(sV, 1)])
                    tb_ev[bank] = (sV, sV.v)
                    g += 1
            tpose_prev = sT.v
        xq_done_v = sV.v
        xq_done_a = sA.v
        if stage < 10:
            bd.op("scalar", "dma_start", out=ap["dbg_xq"], in_=xqt[:],
                  waits=[(sV, xq_done_v), (sA, xq_done_a)], incs=[(sS, 16)])
            bd.op("scalar", "dma_start", out=ap["dbg_xn"], in_=xnt[:], incs=[(sS, 16)])
        dbg_xq_st = sS.v

        phA5.close()
        phA4 = phA.enter_context(ExitStack())
        w2buf = phA4.enter_context(nc.sbuf_tensor("w2buf", [128, 2 * QW], F32))
        wdq = phA4.enter_context(nc.sbuf_tensor("wdq", [128, QW], BF16))
        wst = phA4.enter_context(nc.sbuf_tensor("wst", [128, 512], BF16))
        WT2 = wtiles(QW)
        CONTRIB = {"wg": "wgt_c", "wu": "wut_c", "wd": "wdt_c"}
        tile_done = []
        store_v = []
        wdq_tp_prev = 0
        for i, (name, r0, rc, c0, cc, smcol) in enumerate(WT2):
            buf = w2buf[0:rc, (i % 2) * QW:(i % 2) * QW + cc]
            sp = sXa if i % 2 == 0 else sXb
            w = [(sV, xq_done_v), (sA, xq_done_a)] if i < 2 else [(sV, tile_done[i - 2])]
            bd.op("sync", "dma_start", out=buf, in_=ap[name][r0:r0 + rc, c0:c0 + cc],
                  waits=w, incs=[(sp, 16)])
            ldv = (sp, sp.v)
            totc = D if name in ("wg", "wu") else Fm
            pt_cols = max(0, (c0 + cc) - (totc - 128))
            qc = cc - pt_cols
            a_last = xq_done_a
            if qc > 0:
                esc = msc_sb[0:rc, r0 // 128:r0 // 128 + 1] if name == "wd" else None
                _, a_last = qdq_tile(buf[:, 0:qc], wdq[0:rc, 0:qc], rc, qc,
                                     scl[0:rc, smcol:smcol + 1], scl[0:rc, 4 + smcol:5 + smcol],
                                     [ldv], extra_scale=esc,
                                     cast_waits=[(sT, wdq_tp_prev)])
            if pt_cols > 0:
                if name == "wd":
                    bd.op("vector", "tensor_scalar", out=wdq[0:rc, qc:cc], in0=buf[:, qc:cc],
                          scalar1=msc_sb[0:rc, r0 // 128:r0 // 128 + 1], scalar2=None,
                          op0=ALU.mult, waits=[ldv])
                else:
                    bd.op("vector", "tensor_copy", out=wdq[0:rc, qc:cc], in_=buf[:, qc:cc],
                          waits=[ldv])
                bd.op("scalar", "activation", out=wdq[0:rc, qc:cc].bitcast(BF16),
                      in_=wdq[0:rc, qc:cc], func=AF.Copy, waits=[(sT, wdq_tp_prev)],
                      incs=[(sA, 1)])
                a_last = sA.v
            bd.op("vector", "tensor_copy", out=lnsm[0:1, 0:1], in_=lnsm[0:1, 0:1], incs=[(sV, 1)])
            tile_done.append(sV.v)
            nch = cc // 128
            cn = CONTRIB[name]
            colw = FSL if name in ("wg", "wu") else WDSL
            for g0 in range(0, nch, 4):
                gn = min(4, nch - g0)
                bank = (g0 // 4) % 2
                pt = ptr if bank == 0 else ptr2
                for j in range(gn):
                    ch = g0 + j
                    w = []
                    if j == 0:
                        w.append((sA, a_last))
                        w.append(tb_ev[bank])
                    bd.op("tensor", "transpose", pt[0:128, j * 128:j * 128 + rc],
                          wdq[0:rc, ch * 128:(ch + 1) * 128], ident[0:rc, 0:rc], waits=w,
                          incs=[(sT, 1)] if j == gn - 1 else [])
                wstw = [(sT, sT.v)]
                if store_v:
                    wstw.append((sS, store_v[-1]))
                bd.op("vector", "tensor_copy",
                      out=wst[:, 0:gn * 128].rearrange("p (a b) -> p a b", a=gn)[:, :, 0:rc],
                      in_=pt[:, 0:gn * 128].rearrange("p (a b) -> p a b", a=gn)[:, :, 0:rc],
                      waits=wstw, incs=[(sV, 1)])
                tb_ev[bank] = (sV, sV.v)
                dst = bass.AP(T[cn], (c0 + g0 * 128) * colw + r0,
                              [[colw, 128], [128 * colw, gn], [1, rc]])
                bd.op("scalar", "dma_start", out=dst,
                      in_=wst[:, 0:gn * 128].rearrange("p (a b) -> p a b", a=gn)[:, :, 0:rc],
                      waits=[(sV, sV.v)], incs=[(sS, 16)])
                store_v.append(sS.v)
            wdq_tp_prev = sT.v
        w2_store = sS.v
        bd.wait("gpsimd", (sS, w2_store))
        for cc_i, cin, cout in ((CC_WG, "wgt_c", "wgt_g"), (CC_WU, "wut_c", "wut_g"),
                                (CC_WD, "wdt_c", "wdt_g")):
            bd.op("gpsimd", "collective_compute", "AllGather", ALU.bypass,
                  replica_groups=[list(range(NCORE))], ins=[ap[cin].opt()],
                  outs=[ap[cout].opt()], incs=[(sCC[cc_i], 1)])
        if stage < 10:
            bd.op("gpsimd", "dma_start", out=ap["dbg_wgt"], in_=ap["wgt_g"][D:D + 256, :],
                  waits=[(sCC[CC_WG], 1)], incs=[(sG, 16)])
            bd.op("gpsimd", "dma_start", out=ap["dbg_wdt"], in_=ap["wdt_g"][Fm:Fm + 256, :],
                  waits=[(sCC[CC_WD], 1)], incs=[(sG, 16)])

        phA4.close()
        trigq = sbB("trigq", [128, TLOC], F32)
        trigk = sbB("trigk", [128, TLOC], F32)
        qkslab = sbB("qkslab", [128, 2 * 4096], BF16)
        ro1 = sbB("ro1", [64, 512], F32)
        ro2 = sbB("ro2", [64, 512], F32)
        qstg = sbB("qstg", [128, 2 * TLOC], BF16)
        for i, (name, dstt) in enumerate((("cosq", trigq[0:64, :]), ("sinq", trigq[64:128, :]),
                                          ("cosk", trigk[0:64, :]), ("sink", trigk[64:128, :]))):
            bd.op("sync", "dma_start", out=dstt, in_=ap[name],
                  waits=([(sV, sV.v), (sA, sA.v)] if i == 0 else []), incs=[(sXc, 16)])
        trig_ld = sXc.v
        oc_ev = []
        st_v = {}
        mm_done = {}
        for oc in range(48):
            par = oc % 2
            sp = sGa if par == 0 else sGb
            slab = qkslab[:, par * 4096:par * 4096 + 4096]
            w = [(sV, xq_done_v), (sA, xq_done_a)] if oc < 2 else [(sT, mm_done[oc - 2])]
            bd.op("gpsimd", "dma_start", out=slab, in_=ap["wq"][oc], waits=w, incs=[(sp, 16)])
            ldg = (sp, sp.v)
            for th in range(2):
                ps = qkps[par * 2 + th]
                for dc in range(32):
                    w = []
                    if dc == 0:
                        w.append(ldg)
                        if oc >= 2:
                            w.append((sV, oc_ev[oc - 2]))
                        if th == 0 and oc == 0:
                            w.append((sV, xq_done_v))
                    bd.op("tensor", "matmul", ps[:], slab[:, dc * 128:(dc + 1) * 128],
                          xnfm3[:, dc, th * 512:th * 512 + 512],
                          start=(dc == 0), stop=(dc == 31), waits=w,
                          incs=[(sT, 1)] if dc == 31 else [])
            mm_done[oc] = sT.v
            stg = qstg[:, par * TLOC:par * TLOC + TLOC]
            wstg = [(sSq[par], st_v[oc - 2])] if oc >= 2 else []
            if oc < 40:
                trg = trigq if oc < 32 else trigk
                for th in range(2):
                    ps = qkps[par * 2 + th]
                    tv = mm_done[oc] - 1 + th
                    t0 = th * 512
                    cosT = trg[0:64, t0:t0 + 512]
                    sinT = trg[64:128, t0:t0 + 512]
                    w0 = [(sT, tv), (sXc, trig_ld)] + (wstg if th == 0 else [])
                    bd.op("vector", "tensor_tensor", ro1[:, 0:512], ps[0:64, :], cosT,
                          ALU.mult, waits=w0)
                    bd.op("vector", "tensor_tensor", ro2[:, 0:512], ps[64:128, :], sinT, ALU.mult)
                    bd.op("vector", "tensor_sub", stg[0:64, t0:t0 + 512],
                          ro1[:, 0:512], ro2[:, 0:512])
                    bd.op("vector", "tensor_tensor", ro1[:, 0:512], ps[0:64, :], sinT, ALU.mult)
                    bd.op("vector", "tensor_tensor", ro2[:, 0:512], ps[64:128, :], cosT, ALU.mult)
                    bd.op("vector", "tensor_add", stg[64:128, t0:t0 + 512], ro1[:, 0:512],
                          ro2[:, 0:512], incs=[(sV, 1)] if th == 1 else [])
            else:
                bd.op("vector", "tensor_copy", out=stg[:, 0:512], in_=qkps[par * 2][:],
                      waits=[(sT, mm_done[oc] - 1)] + wstg)
                bd.op("vector", "tensor_copy", out=stg[:, 512:1024], in_=qkps[par * 2 + 1][:],
                      waits=[(sT, mm_done[oc])], incs=[(sV, 1)])
            oc_ev.append(sV.v)
            if oc < 32:
                row = 768 * (oc // 4) + (oc % 4) * 128
            elif oc < 40:
                row = 768 * (oc - 32) + 512
            else:
                row = 768 * (oc - 40) + 640
            bd.op("scalar", "dma_start", out=ap["qkv_c"][row:row + 128, :], in_=stg[:],
                  waits=[(sV, sV.v)], incs=[(sSq[par], 16)])
            st_v[oc] = sSq[par].v
        lra_sb = sbB("lra_sb", [128, 32 * 64], BF16)
        bd.op("gpsimd", "dma_start", out=lra_sb[:], in_=ap["lra"], incs=[(sGc, 16)])
        lrg = sGc.v
        for th in range(2):
            for dc in range(32):
                if dc == 0:
                    w = ([(sGc, lrg), (sV, xq_done_v), (sV, oc_ev[46])] if th == 0
                         else [(sV, sV.v)])
                else:
                    w = []
                bd.op("tensor", "matmul", tps[:], lra_sb[:, dc * 64:(dc + 1) * 64],
                      xnfm3[:, dc, th * 512:th * 512 + 512],
                      start=(dc == 0), stop=(dc == 31), waits=w,
                      incs=[(sT, 1)] if dc == 31 else [])
            bd.op("vector", "tensor_copy", out=tfm[:, th * 512:th * 512 + 512], in_=tps[:],
                  waits=[(sT, sT.v)], incs=[(sV, 1)])
        qkv_store_w = [(sSq[0], sSq[0].v), (sSq[1], sSq[1].v)]
        pA_done_v = sV.v
        pA_done_t = sT.v
        bd.wait("gpsimd", *qkv_store_w)
        bd.op("gpsimd", "collective_compute", "AllToAll", ALU.bypass,
              replica_groups=[list(range(NCORE))], ins=[ap["qkv_c"].opt()],
              outs=[ap["qkv_x"].opt()], a2a=True, incs=[(sCC[CC_QKV], 1)])
        if stage < 10:
            bd.op("gpsimd", "dma_start", out=ap["dbg_qkv"], in_=ap["qkv_x"][768:1536, :],
                  waits=[(sCC[CC_QKV], 1)], incs=[(sG, 16)])

    if stage < 2:
        return _close(nc, bd, stk, sS, sX, sG, sCC, stage)

    with ExitStack() as phB:
        sbC = lambda n, s, d: phB.enter_context(nc.sbuf_tensor(n, s, d))
        psC = lambda n, s, d: phB.enter_context(nc.psum_tensor(n, s, d))
        gslab = sbC("gslab", [128, 2 * 32 * 512], BF16)
        gact = sbC("gact", [128, 8 * 512], BF16)
        hev = sbC("hev", [128, 4 * 512], F32)
        gps = [psC(f"gps{i}", [128, 512], F32) for i in range(8)]

        FBS = [(r, c0, w) for r in range(NCORE) for (c0, w) in ((0, 512), (512, 512), (1024, 352))]
        bank_free = {tt: (sV, pA_done_v) for tt in range(NTT)}
        hev_store = {}
        mmv = {}
        spill_s = []
        for kp, (r, c0w, ww) in enumerate(FBS):
            for t_i, gname, ccs in ((0, "wgt_g", CC_WG), (1, "wut_g", CC_WU)):
                k = kp * 2 + t_i
                par = k % 2
                sp = sXa if par == 0 else sXb
                slab = gslab[:, par * 16384:par * 16384 + 16384]
                w = [(sCC[ccs], 1), (sT, pA_done_t)]
                if k >= 2:
                    w = [(sT, mmv[k - 2])]
                for sd in range(8):
                    src = bass.AP(T[gname], (r * D + sd * 512) * FSL + c0w,
                                  [[FSL, 128], [128 * FSL, 4], [1, ww]])
                    bd.op("sync", "dma_start",
                          out=slab[:, sd * 2048:sd * 2048 + 2048].rearrange("p (a b) -> p a b", a=4)[:, :, 0:ww],
                          in_=src, waits=(w if sd == 0 else []), incs=[(sp, 16)])
                ldv = (sp, sp.v)
                for dc in range(32):
                    for tt in range(NTT):
                        w2 = []
                        if dc == 0:
                            w2.append(ldv)
                            w2.append(bank_free[tt])
                        bd.op("tensor", "matmul", gps[tt][:, 0:ww],
                              xqfm[:, dc * TLOC + tt * 128:dc * TLOC + tt * 128 + 128],
                              gslab[:, par * 16384 + dc * 512:par * 16384 + dc * 512 + ww],
                              start=(dc == 0), stop=(dc == 31), waits=w2,
                              incs=[(sT, 1)] if dc == 31 else [])
                        if dc == 31:
                            mmv[(k, tt)] = sT.v
                mmv[k] = sT.v
                if t_i == 0:
                    for tt in range(NTT):
                        bd.op("scalar", "activation", out=gact[:, tt * 512:tt * 512 + ww],
                              in_=gps[tt][:, 0:ww], func=AF.Silu,
                              waits=[(sT, mmv[(k, tt)]), bank_free[tt]], incs=[(sA, 1)])
                        bank_free[tt] = (sA, sA.v)
                else:
                    foff = r * FSL + c0w
                    for tt in range(NTT):
                        slot = tt % 4
                        hbuf = hev[:, slot * 512:slot * 512 + ww]
                        w3 = [(sT, mmv[(k, tt)])]
                        if (kp, tt - 4) in hev_store:
                            w3.append((sSh[slot], hev_store[(kp, tt - 4)]))
                        elif kp > 0 and (kp - 1, tt + 4) in hev_store:
                            w3.append((sSh[slot], hev_store[(kp - 1, tt + 4)]))
                        bd.op("vector", "tensor_tensor", hbuf, gps[tt][:, 0:ww],
                              gact[:, tt * 512:tt * 512 + ww], ALU.mult, waits=w3)
                        bd.op("vector", "tensor_reduce", out=lnsm[:, 0:1],
                              in_=hbuf, axis=AX.X, op=ALU.max,
                              apply_absolute_value=True, incs=[(sE, 1)])
                        bd.op("vector", "tensor_max", smax[:, 4:5], smax[:, 4:5],
                              lnsm[:, 0:1], waits=[(sE, sE.v)], incs=[(sV, 1)])
                        bank_free[tt] = (sV, sV.v)
                        bd.op("scalar", "dma_start",
                              out=ap["hsp"][tt * 128:(tt + 1) * 128, foff:foff + ww],
                              in_=hbuf, waits=[(sV, sV.v)], incs=[(sSh[slot], 16)])
                        hev_store[(kp, tt)] = sSh[slot].v
                        spill_s.append(sS.v)
        p7_spill_w = [(sSh[i], sSh[i].v) for i in range(4)]
        p7_done_t = sT.v
        pcol_to_srow(smax[:, 4:5], 8, [bank_free[tt] for tt in range(NTT)])
        bd.op("vector", "memset", srow[0:1, 9:16], 0.0, incs=[(sV, 1)])
        bd.op("scalar", "dma_start", out=ap["hm_c"], in_=srow[0:1, 8:16],
              waits=[(sV, sV.v)], incs=[(sS, 16)])
        bd.wait("gpsimd", (sS, sS.v))
        bd.op("gpsimd", "collective_compute", "AllReduce", ALU.max,
              replica_groups=[list(range(NCORE))], ins=[ap["hm_c"].opt()],
              outs=[ap["hm_g"].opt()], incs=[(sCC[CC_HM], 1)])
        p7_done_v = sV.v
        if stage < 10 and stage >= 2:
            bd.op("gpsimd", "dma_start", out=ap["dbg_h"], in_=ap["hsp"][0:128, :],
                  waits=p7_spill_w, incs=[(sG, 16)])
            bd.op("gpsimd", "dma_start", out=ap["dbg_hm"], in_=ap["hm_g"],
                  waits=[(sCC[CC_HM], 1)], incs=[(sG, 16)])

    ph57.close()

    if stage < 3:
        return _close(nc, bd, stk, sS, sX, sG, sCC, stage)

    with ExitStack() as phC:
        sbD = lambda n, s, d: phC.enter_context(nc.sbuf_tensor(n, s, d))
        psD = lambda n, s, d: phC.enter_context(nc.psum_tensor(n, s, d))
        Kb = sbD("Kb", [128, 2 * S], BF16)
        Vb = sbD("Vb", [128, 2 * S], BF16)
        Qb = sbD("Qb", [128, 2 * S], BF16)
        VT = sbD("VT", [128, S], BF16)
        Ssb = sbD("Ssb", [128, 2 * S], F32)
        Psb = sbD("Psb", [128, 2 * S], BF16)
        PTs = sbD("PTs", [128, 2 * S], BF16)
        otb = sbD("otb", [128, 2 * S], BF16)
        amx = sbD("amx", [128, 8], F32)
        scps = [psD(f"scps{i}", [128, 512], F32) for i in range(2)]
        ptps = [psD(f"ptps{i}", [128, 1024], BF16) for i in range(2)]
        ovps = [psD(f"ovps{i}", [128, 512], F32) for i in range(2)]

        pair_pe_done = {}
        exp_done = {0: 0, 1: 0}
        pv_done = {0: 0, 1: 0}
        scev = {0: (sA, 0), 1: (sA, 0)}
        ptev = {0: (sV, p7_done_v), 1: (sV, p7_done_v)}
        ovev = {0: (sV, p7_done_v), 1: (sV, p7_done_v)}
        otst = {0: 0, 1: 0}
        for p in range(8):
            b, hh = p // 4, p % 4
            par = p % 2
            w = [(sCC[CC_QKV], 1), (sT, p7_done_t), (sV, p7_done_v)] + p7_spill_w
            if p >= 2:
                w = [(sT, pair_pe_done[p - 2])]
            sp = sGa if par == 0 else sGb
            for r2 in range(4):
                base = 768 * (4 * b + r2)
                bd.op("gpsimd", "dma_start", out=Kb[:, par * S + r2 * TLOC:par * S + (r2 + 1) * TLOC],
                      in_=ap["qkv_x"][base + 512:base + 640, :], waits=(w if r2 == 0 else []),
                      incs=[(sp, 16)])
                bd.op("gpsimd", "dma_start", out=Vb[:, par * S + r2 * TLOC:par * S + (r2 + 1) * TLOC],
                      in_=ap["qkv_x"][base + 640:base + 768, :], incs=[(sp, 16)])
                bd.op("gpsimd", "dma_start", out=Qb[:, par * S + r2 * TLOC:par * S + (r2 + 1) * TLOC],
                      in_=ap["qkv_x"][base + hh * 128:base + hh * 128 + 128, :], incs=[(sp, 16)])
            ld = (sp, sp.v)
            v0 = sV.v
            for g4 in range(8):
                pt = ptps[g4 % 2]
                for j in range(4):
                    kt = g4 * 4 + j
                    w2 = []
                    if j == 0:
                        w2.append(ld)
                        if g4 < 2:
                            w2.append(ptev[g4 % 2])
                        else:
                            w2.append((sV, v0 + g4 - 1))
                    bd.op("tensor", "transpose", pt[:, j * 128:(j + 1) * 128],
                          Vb[:, par * S + kt * 128:par * S + (kt + 1) * 128], ident[:],
                          waits=w2, incs=[(sT, 1)] if j == 3 else [])
                bd.op("vector", "tensor_copy",
                      out=VT[:].rearrange("p (c t) -> p c t", t=128)[:, g4 * 4:g4 * 4 + 4, :],
                      in_=pt[:, 0:512].rearrange("p (a b) -> p a b", a=4),
                      waits=[(sT, sT.v)], incs=[(sV, 1)])
                ptev[g4 % 2] = (sV, sV.v)
            for gq in range(32):
                qpar = gq % 2
                row = (gq + 1) * 128
                nbk = (row + 511) // 512
                evs = []
                for j in range(nbk):
                    wj = min(512, row - j * 512)
                    w2 = [scev[j % 2]]
                    if p == 0 and gq == 0 and j < 2:
                        w2.append((sV, p7_done_v))
                    bd.op("tensor", "matmul", scps[j % 2][:, 0:wj],
                          Qb[:, par * S + gq * 128:par * S + (gq + 1) * 128],
                          Kb[:, par * S + j * 512:par * S + j * 512 + wj],
                          start=True, stop=True, waits=w2, incs=[(sT, 1)])
                    mmsv = sT.v
                    w3 = [(sT, mmsv)]
                    if gq >= 2 and j == 0:
                        w3.append((sA, exp_done[qpar]))
                    bd.op("scalar", "activation", out=Ssb[:, qpar * S + j * 512:qpar * S + j * 512 + wj],
                          in_=scps[j % 2][:, 0:wj], func=AF.Copy, waits=w3, incs=[(sA, 1)])
                    scev[j % 2] = (sA, sA.v)
                    evs.append(sA.v)
                bd.op("vector", "tensor_add", Ssb[:, qpar * S + gq * 128:qpar * S + row],
                      Ssb[:, qpar * S + gq * 128:qpar * S + row], cmask[:],
                      waits=[(sA, evs[-1])])
                bd.op("vector", "tensor_reduce", out=amx[:, 0:1],
                      in_=Ssb[:, qpar * S:qpar * S + row], axis=AX.X, op=ALU.max,
                      incs=[(sE, 1)])
                bd.op("vector", "tensor_scalar", out=amx[:, 1:2], in0=amx[:, 0:1],
                      scalar1=-1.0, scalar2=None, op0=ALU.mult, waits=[(sE, sE.v)],
                      incs=[(sV, 1)])
                nmv = sV.v
                w4 = [(sV, nmv)]
                if gq >= 2:
                    w4.append((sT, pv_done[qpar]))
                bd.op("scalar", "activation", out=Psb[:, qpar * S:qpar * S + row],
                      in_=Ssb[:, qpar * S:qpar * S + row], func=AF.Exp,
                      bias=amx[:, 1:2], accum_out=amx[:, 2:3], waits=w4, incs=[(sA, 1)])
                exp_done[qpar] = sA.v
                bd.op("vector", "reciprocal", out=amx[:, 3:4], in_=amx[:, 2:3],
                      waits=[(sA, sA.v)], incs=[(sE, 1)])
                bd.op("vector", "tensor_scalar", out=Psb[:, qpar * S:qpar * S + row],
                      in0=Psb[:, qpar * S:qpar * S + row], scalar1=amx[:, 3:4],
                      scalar2=None, op0=ALU.mult, waits=[(sE, sE.v)], incs=[(sV, 1)])
                resc_v = sV.v
                ntk = gq + 1
                gev = []
                v1 = sV.v
                for g4 in range(0, ntk, 4):
                    gn = min(4, ntk - g4)
                    pt = ptps[(g4 // 4) % 2]
                    for j in range(gn):
                        kt = g4 + j
                        w5 = []
                        if j == 0:
                            w5.append((sV, resc_v))
                            if g4 >= 8:
                                w5.append((sV, v1 + g4 // 4 - 1))
                            else:
                                w5.append(ptev[(g4 // 4) % 2])
                        bd.op("tensor", "transpose", pt[:, j * 128:(j + 1) * 128],
                              Psb[:, qpar * S + kt * 128:qpar * S + (kt + 1) * 128],
                              ident[:], waits=w5, incs=[(sT, 1)] if j == gn - 1 else [])
                    bd.op("vector", "tensor_copy",
                          out=PTs[:, qpar * S + g4 * 128:qpar * S + (g4 + gn) * 128].rearrange("p (a b) -> p a b", b=128),
                          in_=pt[:, 0:gn * 128].rearrange("p (a b) -> p a b", a=gn),
                          waits=[(sT, sT.v)], incs=[(sV, 1)])
                    gev.append(sV.v)
                for idx in range(0, ntk, 4):
                    ptev[(idx // 4) % 2] = (sV, gev[idx // 4])
                for kt in range(ntk):
                    w6 = []
                    if kt % 4 == 0:
                        w6.append((sV, gev[kt // 4]))
                    if kt == 0:
                        w6.append(ovev[qpar])
                    bd.op("tensor", "matmul", ovps[qpar][:, 0:128],
                          VT[:, kt * 128:(kt + 1) * 128],
                          PTs[:, qpar * S + kt * 128:qpar * S + (kt + 1) * 128],
                          start=(kt == 0), stop=(kt == ntk - 1), waits=w6,
                          incs=[(sT, 1)] if kt == ntk - 1 else [])
                pv_done[qpar] = sT.v
                w7 = [(sT, sT.v)]
                if gq < 2:
                    w7.append((sSt2[par], otst[par]))
                bd.op("vector", "tensor_copy", out=otb[:, par * S + gq * 128:par * S + row],
                      in_=ovps[qpar][:, 0:128], waits=w7, incs=[(sV, 1)])
                ovev[qpar] = (sV, sV.v)
            pair_pe_done[p] = sT.v
            ot_v = sV.v
            for r2 in range(4):
                bd.op("scalar", "dma_start",
                      out=ap["at_c"][512 * (4 * b + r2) + hh * 128:512 * (4 * b + r2) + (hh + 1) * 128, :],
                      in_=otb[:, par * S + r2 * TLOC:par * S + (r2 + 1) * TLOC],
                      waits=[(sV, ot_v)], incs=[(sSt2[par], 16)])
            otst[par] = sSt2[par].v
        at_store_w = [(sSt2[0], sSt2[0].v), (sSt2[1], sSt2[1].v)]
        p9_done_t = sT.v
        p9_done_v = sV.v
        bd.wait("gpsimd", *at_store_w)
        bd.op("gpsimd", "collective_compute", "AllToAll", ALU.bypass,
              replica_groups=[list(range(NCORE))], ins=[ap["at_c"].opt()],
              outs=[ap["at_x"].opt()], a2a=True, incs=[(sCC[CC_AT], 1)])
        if stage < 10 and stage >= 3:
            bd.op("gpsimd", "dma_start", out=ap["dbg_at"], in_=ap["at_x"][0:512, :],
                  waits=[(sCC[CC_AT], 1)], incs=[(sG, 16)])

    if stage < 4:
        return _close(nc, bd, stk, sS, sX, sG, sCC, stage)

    with ExitStack() as phD:
        sbE = lambda n, s, d: phD.enter_context(nc.sbuf_tensor(n, s, d))
        psE = lambda n, s, d: phD.enter_context(nc.psum_tensor(n, s, d))
        hbuf = sbE("hbuf", [128, 6144], F32)
        hqt = sbE("hqt", [128, QW], BF16)
        hqfm = sbE("hqfm", [128, FCH * 512], BF16)
        atsb = sbE("atsb", [128, 32 * 512], BF16)
        wdr = sbE("wdr", [128, 4 * 512], BF16)
        wor = sbE("wor", [128, 4 * 512], BF16)
        xrr = sbE("xrr", [128, 4 * 512], F32)
        osb = sbE("osb", [128, 2 * 512], F32)
        lrb_sb = sbE("lrb_sb2", [64, D], BF16)
        fps = [psE(f"fps{i}", [128, 512], F32) for i in range(4)]
        ptF = [psE(f"ptF{i}", [128, 1024], BF16) for i in range(2)]
        pbc2 = psE("pbc2", [128, 512], F32)

        bd.op("gpsimd", "dma_start", out=lrb_sb[:], in_=ap["lrb"],
              waits=[(sT, p9_done_t)] + at_store_w, incs=[(sGc, 16)])
        lrbg = sGc.v
        bd.op("sync", "dma_start", out=srow[0:1, 8:16], in_=ap["hm_g"],
              waits=[(sCC[CC_HM], 1)], incs=[(sXc, 16)])
        bd.op("tensor", "matmul", pbc2[:, 0:1], onesf[:], srow[0:1, 8:9], start=True, stop=True,
              waits=[(sXc, sXc.v), (sT, p9_done_t)], incs=[(sT, 1)])
        bd.op("vector", "tensor_scalar", out=scl[:, 8:9], in0=pbc2[:, 0:1],
              scalar1=1.0 / 2688.0, scalar2=1e-12, op0=ALU.mult, op1=ALU.max,
              waits=[(sT, sT.v), (sV, p9_done_v)], incs=[(sE, 1)])
        bd.op("vector", "tensor_scalar", out=scl[:, 9:10], in0=scl[:, 8:9],
              scalar1=6.0, scalar2=None, op0=ALU.mult, waits=[(sE, sE.v)], incs=[(sE, 1)])
        bd.op("vector", "reciprocal", out=scl[:, 9:10], in_=scl[:, 9:10],
              waits=[(sE, sE.v)], incs=[(sV, 1)])
        hscale_v = sV.v

        HCH = ((0, 1024), (1024, 1024), (2048, 1024), (3072, 1024), (4096, 1024),
               (5120, 1024), (6144, 1024), (7168, 1024), (8192, 1024), (9216, 1024),
               (10240, 768))
        evac_prev = {tr: (sV, hscale_v) for tr in range(4)}
        ptF_ev = {0: (sV, p9_done_v), 1: (sV, p9_done_v)}
        hq_tp_prev = [p9_done_t]
        osb_st = {0: 0, 1: 0}
        for hf in range(2):
            watt = ([(sCC[CC_AT], 1), (sT, p9_done_t)] + at_store_w) if hf == 0 else [(sT, half_done_t)]
            for dc in range(32):
                bd.op("sync", "dma_start", out=atsb[:, dc * 512:(dc + 1) * 512],
                      in_=ap["at_x"][(dc // 4) * 512 + (dc % 4) * 128:(dc // 4) * 512 + (dc % 4) * 128 + 128,
                                     hf * 512:hf * 512 + 512],
                      waits=(watt if dc == 0 else []), incs=[(sXat, 16)])
            at_ld = sXat.v
            hq_v = hscale_v if hf == 0 else hq_v
            for tr in range(4):
                tt = hf * 4 + tr
                for hs0, hw in ((0, 6144), (6144, 4864)):
                    w = [(sV, sV.v), (sA, qstate["last_cast"])] + p7_spill_w + at_store_w
                    if hf == 1 and tr == 0 and hs0 == 0:
                        w.append((sT, half_done_t))
                    bd.op("sync", "dma_start", out=hbuf[:, 0:hw],
                          in_=ap["hsp"][tt * 128:(tt + 1) * 128, hs0:hs0 + hw],
                          waits=w, incs=[(sXh, 16)])
                    hld = (sXh, sXh.v)
                    for (c0, cw) in HCH:
                        if c0 < hs0 or c0 >= hs0 + hw:
                            continue
                        qc = cw - (128 if c0 + cw == Fm else 0)
                        a_last = None
                        if qc > 0:
                            _, a_last = qdq_tile(hbuf[:, c0 - hs0:c0 - hs0 + qc], hqt[:, 0:qc],
                                                 128, qc, scl[:, 8:9], scl[:, 9:10], [hld],
                                                 cast_waits=[(sT, hq_tp_prev[0])])
                        if qc < cw:
                            bd.op("scalar", "activation", out=hqt[:, qc:cw],
                                  in_=hbuf[:, c0 - hs0 + qc:c0 - hs0 + cw],
                                  func=AF.Copy, waits=[(sT, hq_tp_prev[0])], incs=[(sA, 1)])
                            a_last = sA.v
                            qstate["last_cast"] = sA.v
                        nch = cw // 128
                        for g0 in range(0, nch, 4):
                            gn = min(4, nch - g0)
                            bank = (g0 // 4 + c0 // 512) % 2
                            pt = ptF[bank]
                            for j in range(gn):
                                w5 = []
                                if j == 0:
                                    w5.append((sA, a_last))
                                    w5.append(ptF_ev[bank])
                                bd.op("tensor", "transpose", pt[:, j * 128:(j + 1) * 128],
                                      hqt[:, (g0 + j) * 128:(g0 + j + 1) * 128], ident[:],
                                      waits=w5, incs=[(sT, 1)] if j == gn - 1 else [])
                            fc0 = c0 // 128 + g0
                            bd.op("vector", "tensor_copy",
                                  out=hqfm[:].rearrange("p (f q) -> p f q", q=512)[:, fc0:fc0 + gn, tr * 128:tr * 128 + 128],
                                  in_=pt[:, 0:gn * 128].rearrange("p (a b) -> p a b", a=gn),
                                  waits=[(sT, sT.v)], incs=[(sV, 1)])
                            ptF_ev[bank] = (sV, sV.v)
                        hq_tp_prev[0] = sT.v
                hq_v = sV.v
            half_hq_v = sV.v
            for ob in range(8):
                wd_ld = {}
                wo_ld = {}
                for fc in range(4):
                    w = [(sV, half_hq_v)] if (hf == 0 and ob == 0 and fc == 0) else \
                        ([(sT, sT.v)] if fc == 0 else [])
                    bd.op("sync", "dma_start", out=wdr[:, fc * 512:(fc + 1) * 512],
                          in_=ap["wdt_g"][ob * Fm + fc * 128:ob * Fm + (fc + 1) * 128, :],
                          waits=w, incs=[(sXw[fc % 4], 16)])
                    wd_ld[fc] = (sXw[fc % 4], sXw[fc % 4].v)
                mm_slot = {}
                for fc in range(FCH):
                    for tr in range(4):
                        w2 = [wd_ld[fc]] if tr == 0 else []
                        if fc == 0:
                            w2.append(evac_prev[tr])
                            if tr == 0:
                                w2.append((sV, half_hq_v))
                        bd.op("tensor", "matmul", fps[tr][:],
                              hqfm[:, fc * 512 + tr * 128:fc * 512 + tr * 128 + 128],
                              wdr[:, (fc % 4) * 512:(fc % 4) * 512 + 512],
                              start=(fc == 0), stop=False, waits=w2, incs=[(sT, 1)] if tr == 3 else [])
                    mm_slot[fc % 4] = sT.v
                    nfc = fc + 4
                    if nfc < FCH:
                        bd.op("sync", "dma_start", out=wdr[:, (nfc % 4) * 512:(nfc % 4) * 512 + 512],
                              in_=ap["wdt_g"][ob * Fm + nfc * 128:ob * Fm + (nfc + 1) * 128, :],
                              waits=[(sT, mm_slot[nfc % 4])], incs=[(sXw[nfc % 4], 16)])
                        wd_ld[nfc] = (sXw[nfc % 4], sXw[nfc % 4].v)
                for dc in range(4):
                    w = [(sXat, at_ld), (sT, sT.v)] if dc == 0 else []
                    bd.op("gpsimd", "dma_start", out=wor[:, dc * 512:(dc + 1) * 512],
                          in_=ap["wo"][ob, dc], waits=w, incs=[(sGw[dc % 4], 16)])
                    wo_ld[dc] = (sGw[dc % 4], sGw[dc % 4].v)
                for dc in range(32):
                    for tr in range(4):
                        w2 = [wo_ld[dc]] if tr == 0 else []
                        bd.op("tensor", "matmul", fps[tr][:],
                              atsb[:, dc * 512 + tr * 128:dc * 512 + tr * 128 + 128],
                              wor[:, (dc % 4) * 512:(dc % 4) * 512 + 512],
                              start=False, stop=False, waits=w2, incs=[(sT, 1)] if tr == 3 else [])
                    mm_slot[dc % 4] = sT.v
                    ndc = dc + 4
                    if ndc < 32:
                        bd.op("gpsimd", "dma_start", out=wor[:, (ndc % 4) * 512:(ndc % 4) * 512 + 512],
                              in_=ap["wo"][ob, ndc], waits=[(sT, mm_slot[ndc % 4])],
                              incs=[(sGw[ndc % 4], 16)])
                        wo_ld[ndc] = (sGw[ndc % 4], sGw[ndc % 4].v)
                for tr in range(4):
                    tt = hf * 4 + tr
                    bd.op("sync", "dma_start", out=xrr[:, tr * 512:(tr + 1) * 512],
                          in_=ap["xres"][tt * 128:(tt + 1) * 128, ob * 512:(ob + 1) * 512],
                          waits=([(sSo[tr % 2], osb_st[tr % 2])] if (hf > 0 or ob > 0) else []),
                          incs=[(sXr, 16)])
                xr_ld = sXr.v
                for tr in range(4):
                    w2 = [(sGc, lrbg)]
                    bd.op("tensor", "matmul", fps[tr][:],
                          tfm[:, hf * 512 + tr * 128:hf * 512 + tr * 128 + 128],
                          lrb_sb[:, ob * 512:(ob + 1) * 512],
                          start=False, stop=True, waits=w2, incs=[(sT, 1)])
                    lr_mm = sT.v
                    bd.op("vector", "tensor_add", osb[:, (tr % 2) * 512:(tr % 2) * 512 + 512],
                          fps[tr][:], xrr[:, tr * 512:(tr + 1) * 512],
                          waits=[(sT, lr_mm), (sXr, xr_ld),
                                 (sSo[tr % 2], osb_st[tr % 2])], incs=[(sV, 1)])
                    evac_prev[tr] = (sV, sV.v)
                    tt = hf * 4 + tr
                    bd.op("scalar", "dma_start",
                          out=ap["out"][tt * 128:(tt + 1) * 128, ob * 512:(ob + 1) * 512],
                          in_=osb[:, (tr % 2) * 512:(tr % 2) * 512 + 512],
                          waits=[(sV, sV.v)], incs=[(sSo[tr % 2], 16)])
                    osb_st[tr % 2] = sSo[tr % 2].v
            half_done_t = sT.v
        out_w = [(sSo[0], sSo[0].v), (sSo[1], sSo[1].v)]

    return _close(nc, bd, stk, sS, sX, sG, sCC, stage, extra=out_w)


def _close(nc, bd, stk, sS, sX, sG, sCC, stage, extra=()):
    bd.wait("sync", (sS, sS.v), (sG, sG.v), *extra)
    with nc.Block() as block:
        bd.emit(block)
    stk.close()
    return nc


def _host_prepare(inputs):
    f32 = np.float32
    x = np.asarray(inputs["L0_attn_x"], f32).reshape(B * S, D)
    Wq = np.asarray(inputs["L0_attn_Wq"], f32)
    Wk = np.asarray(inputs["L0_attn_Wk"], f32)
    Wv = np.asarray(inputs["L0_attn_Wv"], f32)
    Wo = np.asarray(inputs["L0_attn_Wo"], f32)
    wg = np.asarray(inputs["L0_ffn_w_gate"], f32)
    wu = np.asarray(inputs["L0_ffn_w_up"], f32)
    wd = np.asarray(inputs["L0_ffn_w_down"], f32)
    msc = np.asarray(inputs["L0_ffn_mlp_output_scale"], f32)
    lrA = np.asarray(inputs["L0_ffn_lr_A"], f32)
    lrB = np.asarray(inputs["L0_ffn_lr_B"], f32)
    up = np.asarray(inputs["L0_ffn_up_reorder_index"], np.int64)
    dn = np.asarray(inputs["L0_ffn_down_reorder_index"], np.int64)

    rope = np.concatenate([np.concatenate([h * HD + np.arange(0, HD, 2),
                                           h * HD + np.arange(1, HD, 2)]) for h in range(H)])
    rope_kv = rope[:KVH * HD]
    WQKV = np.concatenate([Wq[up][:, rope], Wk[up][:, rope_kv[:KVH * HD]], Wv[up]], axis=1)
    wq_t = np.ascontiguousarray(WQKV.reshape(32, 128, 48, 128).transpose(2, 1, 0, 3))
    wo_t = np.ascontiguousarray(Wo.reshape(32, 128, 8, 512).transpose(2, 0, 1, 3))
    wg_p = wg[dn][:, up]
    wu_p = wu[dn][:, up]
    wd_p = wd[:, dn]
    lra_t = np.ascontiguousarray(lrA[up].reshape(32, 128, 64).transpose(1, 0, 2))

    inv_freq = 1.0 / (10000.0 ** (np.arange(0, HD, 2, dtype=np.float64) / HD))
    ident = np.eye(128, dtype=ml_dtypes.bfloat16)
    icon = np.tile(np.array([[I_ABS, I_TRUNC, I_EXP, I_SIGN]], dtype=np.int32), (128, 1))
    ones_bf = np.ones((1, 128), dtype=ml_dtypes.bfloat16)
    ones_f = np.ones((1, 128), dtype=np.float32)
    qi, ki = np.meshgrid(np.arange(128), np.arange(128), indexing="ij")
    cmask = np.where(ki > qi, np.float32(NEG), np.float32(0.0)).astype(np.float32)

    in_maps = []
    for c in range(NCORE):
        pos = np.arange(1024 * (c % 4), 1024 * (c % 4) + TLOC, dtype=np.float64)
        ang = pos[:, None] * inv_freq[None, :]
        cos = np.cos(ang).T.astype(f32)
        sin = np.sin(ang).T.astype(f32)
        sc = np.float32(1.0 / np.sqrt(HD))
        m = {
            "xp": np.ascontiguousarray(x[c * TLOC:(c + 1) * TLOC][:, up]),
            "xres": np.ascontiguousarray(x[c * TLOC:(c + 1) * TLOC]),
            "wq": wq_t, "wo": wo_t,
            "wg": np.ascontiguousarray(wg_p[c * FSL:(c + 1) * FSL]),
            "wu": np.ascontiguousarray(wu_p[c * FSL:(c + 1) * FSL]),
            "wd": np.ascontiguousarray(wd_p[c * WDSL:(c + 1) * WDSL]),
            "msc": np.ascontiguousarray(msc[c * WDSL:(c + 1) * WDSL].reshape(4, 128).T),
            "lra": lra_t, "lrb": lrB,
            "cosq": np.ascontiguousarray(cos * sc), "sinq": np.ascontiguousarray(sin * sc),
            "cosk": cos, "sink": sin,
            "ident": ident, "icon": icon, "ones": ones_bf, "onesf": ones_f,
            "fcon": np.tile(np.array([[1.25, 0.0, 0.0, 0.0]], dtype=np.float32), (128, 1)),
            "cmask": cmask,
        }
        in_maps.append(m)
    return in_maps


def kernel(**inputs):
    in_maps = _host_prepare(inputs)
    nc = build(stage=10)
    res = run_bass_kernel_spmd(nc, in_maps, core_ids=list(range(NCORE)))
    out = np.concatenate([np.asarray(res.results[c]["out"]) for c in range(NCORE)], axis=0)
    return out.reshape(B, S, D).astype(np.float32)


# version 02
# speedup vs baseline: 1.0180x; 1.0180x over previous
import numpy as np
import ml_dtypes
from contextlib import ExitStack

import concourse.bass as bass
import concourse.mybir as mybir
from concourse.bass_utils import run_bass_kernel_spmd
from concourse.aot_env import allow_ici_all_to_all

F32 = mybir.dt.float32
BF16 = mybir.dt.bfloat16
I32 = mybir.dt.int32
AF = mybir.ActivationFunctionType
ALU = mybir.AluOpType
AX = mybir.AxisListType

NCORE = 8
B, S, D, Fm = 2, 4096, 4096, 11008
H, KVH, HD = 32, 8, 128
TLOC = 1024
NTT = 8
FSL = Fm // NCORE
WDSL = D // NCORE
FCH = Fm // 128
NEG = -1e30
QW = 1024
I_ABS = int(np.int64(0x7FFFFFFF))
I_TRUNC = int(np.int64(0xFFC00000) - (1 << 32))
I_EXP = int(np.int64(0xFF800000) - (1 << 32))
I_SIGN = int(np.int64(0x80000000) - (1 << 32))


class Sem:
    def __init__(self, stk, nc, name):
        self.h = stk.enter_context(nc.semaphore(name))
        self.v = 0

    def bump(self, n=1):
        self.v += n
        return self.v


class Bld:
    ENGS = ("sync", "scalar", "vector", "tensor", "gpsimd")

    def __init__(self, nc, stk):
        self.nc, self.stk = nc, stk
        self.q = {e: [] for e in self.ENGS}

    def op(self, eng, meth, *args, waits=(), incs=(), a2a=False, **kwargs):
        self.q[eng].append((meth, args, kwargs, list(waits), list(incs), a2a))
        for sem, n in incs:
            sem.bump(n)

    def wait(self, eng, *waits):
        self.q[eng].append((None, (), {}, list(waits), [], False))

    def emit(self, block):
        for eng in self.ENGS:
            ops = self.q[eng]

            def make(ops_):
                def run(e):
                    for meth, args, kwargs, waits, incs, a2a in ops_:
                        for sem, val in waits:
                            e.wait_ge(sem.h, val)
                        if meth is None:
                            continue
                        if a2a:
                            with allow_ici_all_to_all():
                                inst = getattr(e, meth)(*args, **kwargs)
                        else:
                            inst = getattr(e, meth)(*args, **kwargs)
                        for sem, n in incs:
                            inst.then_inc(sem.h, n)
                return run
            getattr(block, eng)(make(ops))


def build(stage=10):
    nc = bass.Bass("TRN2", target_bir_lowering=False, debug=False)
    stk = ExitStack()
    bd = Bld(nc, stk)
    T = {}

    def dram(name, shape, dtype, kind="Internal"):
        T[name] = nc.dram_tensor(name, shape, dtype, kind=kind)

    dram("xp", [TLOC, D], F32, "ExternalInput")
    dram("xres", [TLOC, D], F32, "ExternalInput")
    dram("wq", [48, 128, 32, 128], F32, "ExternalInput")
    dram("wo", [8, 32, 128, 512], F32, "ExternalInput")
    dram("wg", [FSL, D], F32, "ExternalInput")
    dram("wu", [FSL, D], F32, "ExternalInput")
    dram("wd", [WDSL, Fm], F32, "ExternalInput")
    dram("msc", [128, 4], F32, "ExternalInput")
    dram("lra", [128, 32, 64], F32, "ExternalInput")
    dram("lrb", [64, D], F32, "ExternalInput")
    dram("cosq", [64, TLOC], F32, "ExternalInput")
    dram("sinq", [64, TLOC], F32, "ExternalInput")
    dram("cosk", [64, TLOC], F32, "ExternalInput")
    dram("sink", [64, TLOC], F32, "ExternalInput")
    dram("ident", [128, 128], BF16, "ExternalInput")
    dram("icon", [128, 4], I32, "ExternalInput")
    dram("fcon", [128, 4], F32, "ExternalInput")
    dram("ones", [1, 128], BF16, "ExternalInput")
    dram("onesf", [1, 128], F32, "ExternalInput")
    dram("cmask", [128, 128], F32, "ExternalInput")
    dram("out", [TLOC, D], F32, "ExternalOutput")

    dram("wgt_c", [D, FSL], BF16)
    dram("wut_c", [D, FSL], BF16)
    dram("wdt_c", [Fm, WDSL], BF16)
    dram("wgt_g", [NCORE * D, FSL], BF16)
    dram("wut_g", [NCORE * D, FSL], BF16)
    dram("wdt_g", [NCORE * Fm, WDSL], BF16)
    dram("qkv_c", [NCORE * 768, TLOC], BF16)
    dram("qkv_x", [NCORE * 768, TLOC], BF16)
    dram("at_c", [NCORE * 512, TLOC], BF16)
    dram("at_x", [NCORE * 512, TLOC], BF16)
    dram("mx_c", [1, 8], F32)
    dram("mx_g", [1, 8], F32)
    dram("hm_c", [1, 8], F32)
    dram("hm_g", [1, 8], F32)
    dram("hsp", [TLOC, Fm], F32)
    dram("sc1", [128, 1], F32)
    if stage < 10:
        dram("dbg_xq", [128, D], BF16, "ExternalOutput")
        dram("dbg_xn", [128, D], BF16, "ExternalOutput")
        dram("dbg_wgt", [256, FSL], BF16, "ExternalOutput")
        dram("dbg_wdt", [256, WDSL], BF16, "ExternalOutput")
        dram("dbg_qkv", [768, TLOC], BF16, "ExternalOutput")
        dram("dbg_mx", [1, 8], F32, "ExternalOutput")
    if stage < 10 and stage >= 2:
        dram("dbg_h", [128, Fm], F32, "ExternalOutput")
        dram("dbg_hm", [1, 8], F32, "ExternalOutput")
    if stage < 10 and stage >= 3:
        dram("dbg_at", [512, TLOC], BF16, "ExternalOutput")

    ap = {k: v.ap() for k, v in T.items()}

    sX = Sem(stk, nc, "sX")
    sXa = Sem(stk, nc, "sXa")
    sXb = Sem(stk, nc, "sXb")
    sXc = Sem(stk, nc, "sXc")
    sXh = Sem(stk, nc, "sXh")
    sXat = Sem(stk, nc, "sXat")
    sXr = Sem(stk, nc, "sXr")
    sXw = [Sem(stk, nc, f"sXw{i}") for i in range(4)]
    sS = Sem(stk, nc, "sS")
    sSh = [Sem(stk, nc, f"sSh{i}") for i in range(4)]
    sSq = [Sem(stk, nc, f"sSq{i}") for i in range(2)]
    sSo = [Sem(stk, nc, f"sSo{i}") for i in range(2)]
    sSt2 = [Sem(stk, nc, f"sSt2{i}") for i in range(2)]
    sG = Sem(stk, nc, "sG")
    sGa = Sem(stk, nc, "sGa")
    sGb = Sem(stk, nc, "sGb")
    sGc = Sem(stk, nc, "sGc")
    sGw = [Sem(stk, nc, f"sGw{i}") for i in range(4)]
    sE = Sem(stk, nc, "sE")
    sT = Sem(stk, nc, "sT")
    sV = Sem(stk, nc, "sV")
    sA = Sem(stk, nc, "sA")
    sCC = [Sem(stk, nc, f"sCC{i}") for i in range(7)]
    CC_MAX, CC_WG, CC_WU, CC_WD, CC_QKV, CC_HM, CC_AT = range(7)

    def sb(name, shape, dtype):
        return stk.enter_context(nc.sbuf_tensor(name, shape, dtype))

    ident = sb("ident_s", [128, 128], BF16)
    icon = sb("icon_s", [128, 4], I32)
    fcon = sb("fcon_s", [128, 4], F32)
    cmask = sb("cmask_s", [128, 128], F32)
    ones = sb("ones_s", [1, 128], BF16)
    onesf = sb("onesf_s", [1, 128], F32)
    msc_sb = sb("msc_s", [128, 4], F32)
    lnstat = sb("lnstat_s", [128, 16], F32)
    lnsm = sb("lnsm_s", [128, 8], F32)
    scl = sb("scl_s", [128, 10], F32)
    smax = sb("smax_s", [128, 6], F32)
    srow = sb("srow_s", [1, 16], F32)
    prow = sb("prow_s", [1, 128], F32)
    tfm = sb("tfm_s", [64, TLOC], BF16)

    q_bm = sb("q_bm", [128, QW // 16], F32)
    q_sbf = sb("q_sbf", [128, QW // 16], F32)
    q_v = sb("q_v", [128, QW // 16], F32)
    q_rv = sb("q_rv", [128, QW // 16], F32)
    q_a = sb("q_a", [128, QW], F32)
    q_b = sb("q_b", [128, QW], F32)
    q_l = sb("q_l", [128, QW], F32)
    q_m = sb("q_m", [128, QW], F32)

    qstate = {"last_cast": 0}

    def qdq_tile(src, dst_bf, rc, cw, s_ap, rs6_ap, first_waits, extra_scale=None,
                 cast_waits=()):
        nb = cw // 16
        srcb = src.rearrange("p (n b) -> p n b", b=16)
        bd.op("vector", "tensor_reduce",
              out=q_bm[0:rc, 0:nb].rearrange("p (n o) -> p n o", o=1),
              in_=srcb, axis=AX.X, op=ALU.max, apply_absolute_value=True,
              waits=list(first_waits) + [(sA, qstate["last_cast"])], incs=[(sE, 1)])
        bd.op("vector", "tensor_scalar", out=q_sbf[0:rc, 0:nb], in0=q_bm[0:rc, 0:nb],
              scalar1=rs6_ap, scalar2=1e-12, op0=ALU.mult, op1=ALU.max,
              waits=[(sE, sE.v)], incs=[(sE, 1)])
        bd.op("vector", "tensor_scalar", out=q_sbf[0:rc, 0:nb], in0=q_sbf[0:rc, 0:nb],
              scalar1=448.0, scalar2=None, op0=ALU.min, waits=[(sE, sE.v)], incs=[(sE, 1)])
        bd.op("vector", "tensor_scalar", out=q_v[0:rc, 0:nb], in0=q_sbf[0:rc, 0:nb],
              scalar1=s_ap, scalar2=None, op0=ALU.mult, waits=[(sE, sE.v)], incs=[(sE, 1)])
        bd.op("vector", "reciprocal", out=q_rv[0:rc, 0:nb], in_=q_v[0:rc, 0:nb],
              waits=[(sE, sE.v)], incs=[(sE, 1)])
        rv_b = q_rv[0:rc, 0:nb].rearrange("p (n o) -> p n o", o=1).broadcast_to((rc, nb, 16))
        v_b = q_v[0:rc, 0:nb].rearrange("p (n o) -> p n o", o=1).broadcast_to((rc, nb, 16))
        av = q_a[0:rc, 0:cw]
        bd.op("vector", "tensor_tensor", av.rearrange("p (n b) -> p n b", b=16), srcb, rv_b,
              ALU.mult, waits=[(sE, sE.v)])
        bd.op("vector", "tensor_scalar", out=q_b[0:rc, 0:cw].bitcast(I32), in0=av.bitcast(I32),
              scalar1=icon[0:rc, 0:1], scalar2=None, op0=ALU.bitwise_and, incs=[(sV, 1)])
        vb = sV.v
        bd.op("scalar", "activation", out=q_l[0:rc, 0:cw], in_=q_b[0:rc, 0:cw],
              func=AF.Identity, bias=fcon[0:rc, 0:1], waits=[(sV, vb)], incs=[(sA, 1)])
        al = sA.v
        bd.op("vector", "tensor_scalar", out=q_m[0:rc, 0:cw].bitcast(I32), in0=av.bitcast(I32),
              scalar1=icon[0:rc, 2:3], scalar2=None, op0=ALU.bitwise_and)
        bd.op("vector", "scalar_tensor_tensor", out=q_m[0:rc, 0:cw], in0=q_m[0:rc, 0:cw],
              scalar=0.25, in1=av, op0=ALU.mult, op1=ALU.add)
        bd.op("vector", "tensor_scalar", out=q_m[0:rc, 0:cw].bitcast(I32),
              in0=q_m[0:rc, 0:cw].bitcast(I32), scalar1=icon[0:rc, 1:2],
              scalar2=icon[0:rc, 0:1], op0=ALU.bitwise_and, op1=ALU.bitwise_and)
        bd.op("vector", "tensor_single_scalar", out=q_b[0:rc, 0:cw].bitcast(I32),
              in_=q_b[0:rc, 0:cw], scalar=1.0, op=ALU.is_lt)
        bd.op("vector", "tensor_scalar", out=q_l[0:rc, 0:cw].bitcast(I32),
              in0=q_l[0:rc, 0:cw].bitcast(I32), scalar1=icon[0:rc, 1:2], scalar2=None,
              op0=ALU.bitwise_and, waits=[(sA, al)])
        bd.op("vector", "tensor_scalar", out=q_l[0:rc, 0:cw], in0=q_l[0:rc, 0:cw],
              scalar1=1.0, scalar2=None, op0=ALU.subtract)
        bd.op("vector", "select", out=q_m[0:rc, 0:cw], mask=q_b[0:rc, 0:cw].bitcast(I32),
              on_true=q_l[0:rc, 0:cw], on_false=q_m[0:rc, 0:cw])
        bd.op("vector", "tensor_tensor", q_m[0:rc, 0:cw].rearrange("p (n b) -> p n b", b=16),
              q_m[0:rc, 0:cw].rearrange("p (n b) -> p n b", b=16), v_b, ALU.mult)
        if extra_scale is not None:
            bd.op("vector", "tensor_scalar", out=q_m[0:rc, 0:cw], in0=q_m[0:rc, 0:cw],
                  scalar1=extra_scale, scalar2=None, op0=ALU.mult)
        bd.op("vector", "scalar_tensor_tensor", out=q_m[0:rc, 0:cw].bitcast(I32),
              in0=src.bitcast(I32), scalar=icon[0:rc, 3:4], in1=q_m[0:rc, 0:cw].bitcast(I32),
              op0=ALU.bitwise_and, op1=ALU.bitwise_or, incs=[(sV, 1)])
        v_done = sV.v
        bd.op("scalar", "activation", out=dst_bf, in_=q_m[0:rc, 0:cw], func=AF.Copy,
              waits=[(sV, v_done)] + list(cast_waits), incs=[(sA, 1)])
        qstate["last_cast"] = sA.v
        return v_done, sA.v

    def pcol_to_srow(col_ap, dstc, first_waits):
        bd.op("scalar", "dma_start", out=ap["sc1"], in_=col_ap,
              waits=list(first_waits) + [(sXc, sXc.v)], incs=[(sS, 16)])
        bd.op("sync", "dma_start",
              out=prow[0:1, 0:128].rearrange("o (a b) -> o a b", a=1),
              in_=ap["sc1"].rearrange("(a p) o -> a o p", a=1),
              waits=[(sS, sS.v)], incs=[(sXc, 16)])
        bd.op("vector", "tensor_reduce", out=srow[0:1, dstc:dstc + 1],
              in_=prow[0:1, 0:128].rearrange("o (a b) -> o a b", a=1),
              axis=AX.X, op=ALU.max, waits=[(sXc, sXc.v)], incs=[(sV, 1)])

    for name, dst in (("ident", ident), ("icon", icon), ("fcon", fcon), ("cmask", cmask),
                      ("ones", ones), ("onesf", onesf), ("msc", msc_sb)):
        bd.op("sync", "dma_start", out=dst[:], in_=ap[name], incs=[(sX, 16)])
    bd.op("vector", "memset", smax[:], 0.0, waits=[(sX, sX.v)], incs=[(sV, 1)])

    with ExitStack() as ph12:
        sbA = lambda n, s, d: ph12.enter_context(nc.sbuf_tensor(n, s, d))
        xsc = sbA("xsc", [128, D], F32)
        lscr = sbA("lscr", [128, D], F32)
        w1buf = sbA("w1buf", [128, 2 * 2048], F32)

        for tt in range(NTT):
            w = [(sV, sV.v)] if tt > 0 else []
            bd.op("sync", "dma_start", out=xsc[:], in_=ap["xp"][tt * 128:(tt + 1) * 128, :],
                  waits=w, incs=[(sXc, 16)])
            ld = sXc.v
            bd.op("vector", "tensor_reduce", out=lnsm[:, 0:1], in_=xsc[:], axis=AX.X,
                  op=ALU.add, waits=[(sXc, ld)], incs=[(sE, 1)])
            bd.op("scalar", "activation", out=lscr[:], in_=xsc[:], func=AF.Square,
                  accum_out=lnsm[:, 1:2], waits=[(sXc, ld)], incs=[(sA, 1)])
            bd.op("vector", "tensor_scalar", out=lnstat[:, 2 * tt:2 * tt + 1], in0=lnsm[:, 0:1],
                  scalar1=1.0 / D, scalar2=None, op0=ALU.mult,
                  waits=[(sA, sA.v), (sE, sE.v)], incs=[(sE, 1)])
            bd.op("vector", "tensor_scalar", out=lnsm[:, 3:4], in0=lnsm[:, 1:2],
                  scalar1=1.0 / D, scalar2=None, op0=ALU.mult, incs=[(sE, 1)])
            bd.op("vector", "tensor_scalar", out=lnsm[:, 4:5], in0=lnstat[:, 2 * tt:2 * tt + 1],
                  scalar1=lnstat[:, 2 * tt:2 * tt + 1], scalar2=None, op0=ALU.mult,
                  waits=[(sE, sE.v)], incs=[(sE, 1)])
            bd.op("vector", "tensor_sub", lnsm[:, 5:6], lnsm[:, 3:4], lnsm[:, 4:5],
                  waits=[(sE, sE.v)], incs=[(sE, 1)])
            bd.op("vector", "tensor_scalar", out=lnsm[:, 5:6], in0=lnsm[:, 5:6],
                  scalar1=1e-5, scalar2=None, op0=ALU.add, waits=[(sE, sE.v)], incs=[(sE, 1)])
            bd.op("vector", "reciprocal", out=lnsm[:, 7:8], in_=lnsm[:, 5:6],
                  waits=[(sE, sE.v)], incs=[(sV, 1)])
            bd.op("scalar", "activation", out=lnstat[:, 2 * tt + 1:2 * tt + 2], in_=lnsm[:, 7:8],
                  func=AF.Sqrt, waits=[(sV, sV.v)], incs=[(sA, 1)])
            bd.op("vector", "tensor_scalar", out=xsc[:], in0=xsc[:],
                  scalar1=lnstat[:, 2 * tt:2 * tt + 1], scalar2=lnstat[:, 2 * tt + 1:2 * tt + 2],
                  op0=ALU.subtract, op1=ALU.mult, waits=[(sA, sA.v)])
            bd.op("vector", "tensor_reduce", out=lnsm[:, 0:1], in_=xsc[:], axis=AX.X,
                  op=ALU.max, apply_absolute_value=True, incs=[(sE, 1)])
            bd.op("vector", "tensor_max", smax[:, 3:4], smax[:, 3:4], lnsm[:, 0:1],
                  waits=[(sE, sE.v)], incs=[(sV, 1)])

        def wtiles(colw):
            out = []
            for name, rows, cols, smcol in (("wg", FSL, D, 0), ("wu", FSL, D, 1), ("wd", WDSL, Fm, 2)):
                r0 = 0
                while r0 < rows:
                    rc = min(128, rows - r0)
                    c0 = 0
                    while c0 < cols:
                        cc = min(colw, cols - c0)
                        out.append((name, r0, rc, c0, cc, smcol))
                        c0 += cc
                    r0 += rc
            return out

        WT1 = wtiles(2048)
        p2red = []
        for i, (name, r0, rc, c0, cc, smcol) in enumerate(WT1):
            buf = w1buf[0:rc, (i % 2) * 2048:(i % 2) * 2048 + cc]
            sp = sXa if i % 2 == 0 else sXb
            w = [] if i < 2 else [(sV, p2red[i - 2])]
            bd.op("sync", "dma_start", out=buf, in_=ap[name][r0:r0 + rc, c0:c0 + cc],
                  waits=w, incs=[(sp, 16)])
            bd.op("vector", "tensor_reduce", out=lnsm[0:rc, 0:1], in_=buf, axis=AX.X,
                  op=ALU.max, apply_absolute_value=True, waits=[(sp, sp.v)], incs=[(sE, 1)])
            bd.op("vector", "tensor_max", smax[0:rc, smcol:smcol + 1],
                  smax[0:rc, smcol:smcol + 1], lnsm[0:rc, 0:1], waits=[(sE, sE.v)], incs=[(sV, 1)])
            p2red.append(sV.v)
        w1_done = sV.v

        for t in range(4):
            pcol_to_srow(smax[:, t:t + 1], t, [(sV, w1_done)] if t == 0 else [])
        bd.op("vector", "memset", srow[0:1, 4:8], 0.0, incs=[(sV, 1)])
        bd.op("scalar", "dma_start", out=ap["mx_c"], in_=srow[0:1, 0:8],
              waits=[(sV, sV.v)], incs=[(sS, 16)])
        bd.wait("gpsimd", (sS, sS.v))
        bd.op("gpsimd", "collective_compute", "AllReduce", ALU.max,
              replica_groups=[list(range(NCORE))], ins=[ap["mx_c"].opt()],
              outs=[ap["mx_g"].opt()], incs=[(sCC[CC_MAX], 1)])
        bd.op("sync", "dma_start", out=srow[0:1, 0:8], in_=ap["mx_g"],
              waits=[(sCC[CC_MAX], 1)], incs=[(sXc, 16)])
        mxg_ld = sXc.v
        if stage < 10:
            bd.op("gpsimd", "dma_start", out=ap["dbg_mx"], in_=ap["mx_g"],
                  waits=[(sCC[CC_MAX], 1)], incs=[(sG, 16)])


    ph57 = stk.enter_context(ExitStack())
    xqfm = ph57.enter_context(nc.sbuf_tensor("xqfm", [128, 32 * TLOC], BF16))
    with ExitStack() as phA:
        sbB = lambda n, s, d: phA.enter_context(nc.sbuf_tensor(n, s, d))
        psB = lambda n, s, d: phA.enter_context(nc.psum_tensor(n, s, d))
        pbc = psB("pbc", [128, 512], F32)
        ptr = psB("ptr", [128, 1024], BF16)
        ptr2 = psB("ptr2", [128, 1024], BF16)
        qkps = [psB(f"qkps{i}", [128, 512], F32) for i in range(4)]
        tps = psB("tps", [64, 512], F32)

        bd.op("tensor", "matmul", pbc[:, 0:4], onesf[:], srow[0:1, 0:4],
              start=True, stop=True, waits=[(sXc, mxg_ld)], incs=[(sT, 1)])
        bd.op("vector", "tensor_scalar", out=scl[:, 0:4], in0=pbc[:, 0:4],
              scalar1=1.0 / 2688.0, scalar2=1e-12, op0=ALU.mult, op1=ALU.max,
              waits=[(sT, sT.v)], incs=[(sE, 1)])
        bd.op("vector", "tensor_scalar", out=scl[:, 4:8], in0=scl[:, 0:4],
              scalar1=6.0, scalar2=None, op0=ALU.mult, waits=[(sE, sE.v)], incs=[(sE, 1)])
        bd.op("vector", "reciprocal", out=scl[:, 4:8], in_=scl[:, 4:8],
              waits=[(sE, sE.v)], incs=[(sV, 1)])
        scales_v = sV.v

        xnfm = sbB("xnfm", [128, 32 * TLOC], BF16)
        phA5 = phA.enter_context(ExitStack())
        xsc2 = phA5.enter_context(nc.sbuf_tensor("xsc2", [128, D], F32))
        xqt = phA5.enter_context(nc.sbuf_tensor("xqt", [128, D], BF16))
        xnt = phA5.enter_context(nc.sbuf_tensor("xnt", [128, D], BF16))
        xnfm3 = xnfm[:].rearrange("p (c t) -> p c t", t=TLOC)
        xqfm3 = xqfm[:].rearrange("p (c t) -> p c t", t=TLOC)
        last_sign_v = scales_v
        prev_cast_a = 0
        tpose_prev = 0
        tb_ev = {0: (sV, 0), 1: (sV, 0)}
        for tt in range(NTT):
            bd.op("sync", "dma_start", out=xsc2[:], in_=ap["xp"][tt * 128:(tt + 1) * 128, :],
                  waits=[(sV, last_sign_v), (sA, prev_cast_a)], incs=[(sXc, 16)])
            ld = sXc.v
            bd.op("vector", "tensor_scalar", out=xsc2[:], in0=xsc2[:],
                  scalar1=lnstat[:, 2 * tt:2 * tt + 1], scalar2=lnstat[:, 2 * tt + 1:2 * tt + 2],
                  op0=ALU.subtract, op1=ALU.mult, waits=[(sXc, ld), (sV, scales_v)])
            a_last = None
            for c0, cw in ((0, 1024), (1024, 1024), (2048, 1024), (3072, 896)):
                vd, a_last = qdq_tile(xsc2[:, c0:c0 + cw], xqt[:, c0:c0 + cw], 128, cw,
                                      scl[:, 3:4], scl[:, 7:8], [],
                                      cast_waits=[(sT, tpose_prev)])
                last_sign_v = vd
            bd.op("scalar", "activation", out=xqt[:, D - 128:D], in_=xsc2[:, D - 128:D],
                  func=AF.Copy, incs=[(sA, 1)])
            bd.op("scalar", "activation", out=xnt[:], in_=xsc2[:], func=AF.Copy, incs=[(sA, 1)])
            cast_a = sA.v
            prev_cast_a = cast_a
            g = 0
            for srcb, dst3 in ((xnt, xnfm3), (xqt, xqfm3)):
                for dc4 in range(8):
                    bank = g % 2
                    pt = ptr if bank == 0 else ptr2
                    for j in range(4):
                        dc = dc4 * 4 + j
                        w = []
                        if j == 0:
                            w.append((sA, cast_a))
                            w.append(tb_ev[bank])
                        bd.op("tensor", "transpose", pt[:, j * 128:(j + 1) * 128],
                              srcb[:, dc * 128:(dc + 1) * 128], ident[:], waits=w,
                              incs=[(sT, 1)] if j == 3 else [])
                    bd.op("vector", "tensor_copy",
                          out=dst3[:, dc4 * 4:dc4 * 4 + 4, tt * 128:tt * 128 + 128],
                          in_=pt[:, 0:512].rearrange("p (a b) -> p a b", a=4),
                          waits=[(sT, sT.v)], incs=[(sV, 1)])
                    tb_ev[bank] = (sV, sV.v)
                    g += 1
            tpose_prev = sT.v
        xq_done_v = sV.v
        xq_done_a = sA.v
        if stage < 10:
            bd.op("scalar", "dma_start", out=ap["dbg_xq"], in_=xqt[:],
                  waits=[(sV, xq_done_v), (sA, xq_done_a)], incs=[(sS, 16)])
            bd.op("scalar", "dma_start", out=ap["dbg_xn"], in_=xnt[:], incs=[(sS, 16)])
        dbg_xq_st = sS.v

        phA5.close()
        phA4 = phA.enter_context(ExitStack())
        w2buf = phA4.enter_context(nc.sbuf_tensor("w2buf", [128, 2 * QW], F32))
        wdq = phA4.enter_context(nc.sbuf_tensor("wdq", [128, QW], BF16))
        wst = phA4.enter_context(nc.sbuf_tensor("wst", [128, 512], BF16))
        WT2 = wtiles(QW)
        CONTRIB = {"wg": "wgt_c", "wu": "wut_c", "wd": "wdt_c"}
        tile_done = []
        store_v = []
        wdq_tp_prev = 0
        for i, (name, r0, rc, c0, cc, smcol) in enumerate(WT2):
            buf = w2buf[0:rc, (i % 2) * QW:(i % 2) * QW + cc]
            sp = sXa if i % 2 == 0 else sXb
            w = [(sV, xq_done_v), (sA, xq_done_a)] if i < 2 else [(sV, tile_done[i - 2])]
            bd.op("sync", "dma_start", out=buf, in_=ap[name][r0:r0 + rc, c0:c0 + cc],
                  waits=w, incs=[(sp, 16)])
            ldv = (sp, sp.v)
            totc = D if name in ("wg", "wu") else Fm
            pt_cols = max(0, (c0 + cc) - (totc - 128))
            qc = cc - pt_cols
            a_last = xq_done_a
            if qc > 0:
                esc = msc_sb[0:rc, r0 // 128:r0 // 128 + 1] if name == "wd" else None
                _, a_last = qdq_tile(buf[:, 0:qc], wdq[0:rc, 0:qc], rc, qc,
                                     scl[0:rc, smcol:smcol + 1], scl[0:rc, 4 + smcol:5 + smcol],
                                     [ldv], extra_scale=esc,
                                     cast_waits=[(sT, wdq_tp_prev)])
            if pt_cols > 0:
                if name == "wd":
                    bd.op("vector", "tensor_scalar", out=wdq[0:rc, qc:cc], in0=buf[:, qc:cc],
                          scalar1=msc_sb[0:rc, r0 // 128:r0 // 128 + 1], scalar2=None,
                          op0=ALU.mult, waits=[ldv])
                else:
                    bd.op("vector", "tensor_copy", out=wdq[0:rc, qc:cc], in_=buf[:, qc:cc],
                          waits=[ldv])
                bd.op("scalar", "activation", out=wdq[0:rc, qc:cc].bitcast(BF16),
                      in_=wdq[0:rc, qc:cc], func=AF.Copy, waits=[(sT, wdq_tp_prev)],
                      incs=[(sA, 1)])
                a_last = sA.v
            bd.op("vector", "tensor_copy", out=lnsm[0:1, 0:1], in_=lnsm[0:1, 0:1], incs=[(sV, 1)])
            tile_done.append(sV.v)
            nch = cc // 128
            cn = CONTRIB[name]
            colw = FSL if name in ("wg", "wu") else WDSL
            for g0 in range(0, nch, 4):
                gn = min(4, nch - g0)
                bank = (g0 // 4) % 2
                pt = ptr if bank == 0 else ptr2
                for j in range(gn):
                    ch = g0 + j
                    w = []
                    if j == 0:
                        w.append((sA, a_last))
                        w.append(tb_ev[bank])
                    bd.op("tensor", "transpose", pt[0:128, j * 128:j * 128 + rc],
                          wdq[0:rc, ch * 128:(ch + 1) * 128], ident[0:rc, 0:rc], waits=w,
                          incs=[(sT, 1)] if j == gn - 1 else [])
                wstw = [(sT, sT.v)]
                if store_v:
                    wstw.append((sS, store_v[-1]))
                bd.op("vector", "tensor_copy",
                      out=wst[:, 0:gn * 128].rearrange("p (a b) -> p a b", a=gn)[:, :, 0:rc],
                      in_=pt[:, 0:gn * 128].rearrange("p (a b) -> p a b", a=gn)[:, :, 0:rc],
                      waits=wstw, incs=[(sV, 1)])
                tb_ev[bank] = (sV, sV.v)
                dst = bass.AP(T[cn], (c0 + g0 * 128) * colw + r0,
                              [[colw, 128], [128 * colw, gn], [1, rc]])
                bd.op("scalar", "dma_start", out=dst,
                      in_=wst[:, 0:gn * 128].rearrange("p (a b) -> p a b", a=gn)[:, :, 0:rc],
                      waits=[(sV, sV.v)], incs=[(sS, 16)])
                store_v.append(sS.v)
            wdq_tp_prev = sT.v
        w2_store = sS.v
        bd.wait("gpsimd", (sS, w2_store))
        for cc_i, cin, cout in ((CC_WG, "wgt_c", "wgt_g"), (CC_WU, "wut_c", "wut_g"),
                                (CC_WD, "wdt_c", "wdt_g")):
            bd.op("gpsimd", "collective_compute", "AllGather", ALU.bypass,
                  replica_groups=[list(range(NCORE))], ins=[ap[cin].opt()],
                  outs=[ap[cout].opt()], incs=[(sCC[cc_i], 1)])
        if stage < 10:
            bd.op("gpsimd", "dma_start", out=ap["dbg_wgt"], in_=ap["wgt_g"][D:D + 256, :],
                  waits=[(sCC[CC_WG], 1)], incs=[(sG, 16)])
            bd.op("gpsimd", "dma_start", out=ap["dbg_wdt"], in_=ap["wdt_g"][Fm:Fm + 256, :],
                  waits=[(sCC[CC_WD], 1)], incs=[(sG, 16)])

        phA4.close()
        trigq = sbB("trigq", [128, TLOC], F32)
        trigk = sbB("trigk", [128, TLOC], F32)
        qkslab = sbB("qkslab", [128, 2 * 4096], BF16)
        ro1 = sbB("ro1", [64, 512], F32)
        ro2 = sbB("ro2", [64, 512], F32)
        qstg = sbB("qstg", [128, 2 * TLOC], BF16)
        for i, (name, dstt) in enumerate((("cosq", trigq[0:64, :]), ("sinq", trigq[64:128, :]),
                                          ("cosk", trigk[0:64, :]), ("sink", trigk[64:128, :]))):
            bd.op("sync", "dma_start", out=dstt, in_=ap[name],
                  waits=([(sV, sV.v), (sA, sA.v)] if i == 0 else []), incs=[(sXc, 16)])
        trig_ld = sXc.v
        oc_ev = []
        st_v = {}
        mm_done = {}
        for oc in range(48):
            par = oc % 2
            sp = sGa if par == 0 else sGb
            slab = qkslab[:, par * 4096:par * 4096 + 4096]
            w = [(sV, xq_done_v), (sA, xq_done_a)] if oc < 2 else [(sT, mm_done[oc - 2])]
            bd.op("gpsimd", "dma_start", out=slab, in_=ap["wq"][oc], waits=w, incs=[(sp, 16)])
            ldg = (sp, sp.v)
            for th in range(2):
                ps = qkps[par * 2 + th]
                for dc in range(32):
                    w = []
                    if dc == 0:
                        w.append(ldg)
                        if oc >= 2:
                            w.append((sV, oc_ev[oc - 2]))
                        if th == 0 and oc == 0:
                            w.append((sV, xq_done_v))
                    bd.op("tensor", "matmul", ps[:], slab[:, dc * 128:(dc + 1) * 128],
                          xnfm3[:, dc, th * 512:th * 512 + 512],
                          start=(dc == 0), stop=(dc == 31), waits=w,
                          incs=[(sT, 1)] if dc == 31 else [])
            mm_done[oc] = sT.v
            stg = qstg[:, par * TLOC:par * TLOC + TLOC]
            wstg = [(sSq[par], st_v[oc - 2])] if oc >= 2 else []
            if oc < 40:
                trg = trigq if oc < 32 else trigk
                for th in range(2):
                    ps = qkps[par * 2 + th]
                    tv = mm_done[oc] - 1 + th
                    t0 = th * 512
                    cosT = trg[0:64, t0:t0 + 512]
                    sinT = trg[64:128, t0:t0 + 512]
                    w0 = [(sT, tv), (sXc, trig_ld)] + (wstg if th == 0 else [])
                    bd.op("vector", "tensor_tensor", ro1[:, 0:512], ps[0:64, :], cosT,
                          ALU.mult, waits=w0)
                    bd.op("vector", "tensor_tensor", ro2[:, 0:512], ps[64:128, :], sinT, ALU.mult)
                    bd.op("vector", "tensor_sub", stg[0:64, t0:t0 + 512],
                          ro1[:, 0:512], ro2[:, 0:512])
                    bd.op("vector", "tensor_tensor", ro1[:, 0:512], ps[0:64, :], sinT, ALU.mult)
                    bd.op("vector", "tensor_tensor", ro2[:, 0:512], ps[64:128, :], cosT, ALU.mult)
                    bd.op("vector", "tensor_add", stg[64:128, t0:t0 + 512], ro1[:, 0:512],
                          ro2[:, 0:512], incs=[(sV, 1)] if th == 1 else [])
            else:
                bd.op("vector", "tensor_copy", out=stg[:, 0:512], in_=qkps[par * 2][:],
                      waits=[(sT, mm_done[oc] - 1)] + wstg)
                bd.op("vector", "tensor_copy", out=stg[:, 512:1024], in_=qkps[par * 2 + 1][:],
                      waits=[(sT, mm_done[oc])], incs=[(sV, 1)])
            oc_ev.append(sV.v)
            if oc < 32:
                row = 768 * (oc // 4) + (oc % 4) * 128
            elif oc < 40:
                row = 768 * (oc - 32) + 512
            else:
                row = 768 * (oc - 40) + 640
            bd.op("scalar", "dma_start", out=ap["qkv_c"][row:row + 128, :], in_=stg[:],
                  waits=[(sV, sV.v)], incs=[(sSq[par], 16)])
            st_v[oc] = sSq[par].v
        lra_sb = sbB("lra_sb", [128, 32 * 64], BF16)
        bd.op("gpsimd", "dma_start", out=lra_sb[:], in_=ap["lra"], incs=[(sGc, 16)])
        lrg = sGc.v
        for th in range(2):
            for dc in range(32):
                if dc == 0:
                    w = ([(sGc, lrg), (sV, xq_done_v), (sV, oc_ev[46])] if th == 0
                         else [(sV, sV.v)])
                else:
                    w = []
                bd.op("tensor", "matmul", tps[:], lra_sb[:, dc * 64:(dc + 1) * 64],
                      xnfm3[:, dc, th * 512:th * 512 + 512],
                      start=(dc == 0), stop=(dc == 31), waits=w,
                      incs=[(sT, 1)] if dc == 31 else [])
            bd.op("vector", "tensor_copy", out=tfm[:, th * 512:th * 512 + 512], in_=tps[:],
                  waits=[(sT, sT.v)], incs=[(sV, 1)])
        qkv_store_w = [(sSq[0], sSq[0].v), (sSq[1], sSq[1].v)]
        pA_done_v = sV.v
        pA_done_t = sT.v
        bd.wait("gpsimd", *qkv_store_w)
        bd.op("gpsimd", "collective_compute", "AllToAll", ALU.bypass,
              replica_groups=[list(range(NCORE))], ins=[ap["qkv_c"].opt()],
              outs=[ap["qkv_x"].opt()], a2a=True, incs=[(sCC[CC_QKV], 1)])
        if stage < 10:
            bd.op("gpsimd", "dma_start", out=ap["dbg_qkv"], in_=ap["qkv_x"][768:1536, :],
                  waits=[(sCC[CC_QKV], 1)], incs=[(sG, 16)])

    if stage < 2:
        return _close(nc, bd, stk, sS, sX, sG, sCC, stage)

    with ExitStack() as phB:
        sbC = lambda n, s, d: phB.enter_context(nc.sbuf_tensor(n, s, d))
        psC = lambda n, s, d: phB.enter_context(nc.psum_tensor(n, s, d))
        gslab = sbC("gslab", [128, 2 * 32 * 512], BF16)
        gact = sbC("gact", [128, 8 * 512], BF16)
        hev = sbC("hev", [128, 4 * 512], F32)
        gps = [psC(f"gps{i}", [128, 512], F32) for i in range(8)]

        FBS = [(r, c0, w) for r in range(NCORE) for (c0, w) in ((0, 512), (512, 512), (1024, 352))]
        bank_free = {tt: (sV, pA_done_v) for tt in range(NTT)}
        hev_store = {}
        mmv = {}
        spill_s = []
        for kp, (r, c0w, ww) in enumerate(FBS):
            for t_i, gname, ccs in ((0, "wgt_g", CC_WG), (1, "wut_g", CC_WU)):
                k = kp * 2 + t_i
                par = k % 2
                sp = sXa if par == 0 else sXb
                slab = gslab[:, par * 16384:par * 16384 + 16384]
                w = [(sCC[ccs], 1), (sT, pA_done_t)]
                if k >= 2:
                    w = [(sT, mmv[k - 2])]
                for sd in range(8):
                    src = bass.AP(T[gname], (r * D + sd * 512) * FSL + c0w,
                                  [[FSL, 128], [128 * FSL, 4], [1, ww]])
                    bd.op("sync", "dma_start",
                          out=slab[:, sd * 2048:sd * 2048 + 2048].rearrange("p (a b) -> p a b", a=4)[:, :, 0:ww],
                          in_=src, waits=(w if sd == 0 else []), incs=[(sp, 16)])
                ldv = (sp, sp.v)
                for dc in range(32):
                    for tt in range(NTT):
                        w2 = []
                        if dc == 0:
                            w2.append(ldv)
                            w2.append(bank_free[tt])
                        bd.op("tensor", "matmul", gps[tt][:, 0:ww],
                              xqfm[:, dc * TLOC + tt * 128:dc * TLOC + tt * 128 + 128],
                              gslab[:, par * 16384 + dc * 512:par * 16384 + dc * 512 + ww],
                              start=(dc == 0), stop=(dc == 31), waits=w2,
                              incs=[(sT, 1)] if dc == 31 else [])
                        if dc == 31:
                            mmv[(k, tt)] = sT.v
                mmv[k] = sT.v
                if t_i == 0:
                    for tt in range(NTT):
                        bd.op("scalar", "activation", out=gact[:, tt * 512:tt * 512 + ww],
                              in_=gps[tt][:, 0:ww], func=AF.Silu,
                              waits=[(sT, mmv[(k, tt)]), bank_free[tt]], incs=[(sA, 1)])
                        bank_free[tt] = (sA, sA.v)
                else:
                    foff = r * FSL + c0w
                    for tt in range(NTT):
                        slot = tt % 4
                        hbuf = hev[:, slot * 512:slot * 512 + ww]
                        w3 = [(sT, mmv[(k, tt)])]
                        if (kp, tt - 4) in hev_store:
                            w3.append((sSh[slot], hev_store[(kp, tt - 4)]))
                        elif kp > 0 and (kp - 1, tt + 4) in hev_store:
                            w3.append((sSh[slot], hev_store[(kp - 1, tt + 4)]))
                        bd.op("vector", "tensor_tensor", hbuf, gps[tt][:, 0:ww],
                              gact[:, tt * 512:tt * 512 + ww], ALU.mult, waits=w3)
                        bd.op("vector", "tensor_reduce", out=lnsm[:, 0:1],
                              in_=hbuf, axis=AX.X, op=ALU.max,
                              apply_absolute_value=True, incs=[(sE, 1)])
                        bd.op("vector", "tensor_max", smax[:, 4:5], smax[:, 4:5],
                              lnsm[:, 0:1], waits=[(sE, sE.v)], incs=[(sV, 1)])
                        bank_free[tt] = (sV, sV.v)
                        bd.op("scalar", "dma_start",
                              out=ap["hsp"][tt * 128:(tt + 1) * 128, foff:foff + ww],
                              in_=hbuf, waits=[(sV, sV.v)], incs=[(sSh[slot], 16)])
                        hev_store[(kp, tt)] = sSh[slot].v
                        spill_s.append(sS.v)
        p7_spill_w = [(sSh[i], sSh[i].v) for i in range(4)]
        p7_done_t = sT.v
        pcol_to_srow(smax[:, 4:5], 8, [bank_free[tt] for tt in range(NTT)])
        bd.op("vector", "memset", srow[0:1, 9:16], 0.0, incs=[(sV, 1)])
        bd.op("scalar", "dma_start", out=ap["hm_c"], in_=srow[0:1, 8:16],
              waits=[(sV, sV.v)], incs=[(sS, 16)])
        bd.wait("gpsimd", (sS, sS.v))
        bd.op("gpsimd", "collective_compute", "AllReduce", ALU.max,
              replica_groups=[list(range(NCORE))], ins=[ap["hm_c"].opt()],
              outs=[ap["hm_g"].opt()], incs=[(sCC[CC_HM], 1)])
        p7_done_v = sV.v
        if stage < 10 and stage >= 2:
            bd.op("gpsimd", "dma_start", out=ap["dbg_h"], in_=ap["hsp"][0:128, :],
                  waits=p7_spill_w, incs=[(sG, 16)])
            bd.op("gpsimd", "dma_start", out=ap["dbg_hm"], in_=ap["hm_g"],
                  waits=[(sCC[CC_HM], 1)], incs=[(sG, 16)])

    ph57.close()

    if stage < 3:
        return _close(nc, bd, stk, sS, sX, sG, sCC, stage)

    with ExitStack() as phC:
        sbD = lambda n, s, d: phC.enter_context(nc.sbuf_tensor(n, s, d))
        psD = lambda n, s, d: phC.enter_context(nc.psum_tensor(n, s, d))
        Kb = sbD("Kb", [128, 2 * S], BF16)
        Vb = sbD("Vb", [128, 2 * S], BF16)
        Qb = sbD("Qb", [128, 2 * S], BF16)
        VT = sbD("VT", [128, S], BF16)
        Ssb = sbD("Ssb", [128, 2 * S], F32)
        Psb = sbD("Psb", [128, 2 * S], BF16)
        PTs = sbD("PTs", [128, 2 * S], BF16)
        otb = sbD("otb", [128, 2 * S], BF16)
        amx = sbD("amx", [128, 8], F32)
        scps = [psD(f"scps{i}", [128, 512], F32) for i in range(2)]
        ptps = [psD(f"ptps{i}", [128, 1024], BF16) for i in range(2)]
        ovps = [psD(f"ovps{i}", [128, 512], F32) for i in range(2)]

        pair_pe_done = {}
        exp_done = {0: 0, 1: 0}
        pv_done = {0: 0, 1: 0}
        scev = {0: (sA, 0), 1: (sA, 0)}
        ptev = {0: (sV, p7_done_v), 1: (sV, p7_done_v)}
        ovev = {0: (sV, p7_done_v), 1: (sV, p7_done_v)}
        otst = {0: 0, 1: 0}
        for p in range(8):
            b, hh = p // 4, p % 4
            par = p % 2
            w = [(sCC[CC_QKV], 1), (sT, p7_done_t), (sV, p7_done_v)] + p7_spill_w
            if p >= 2:
                w = [(sT, pair_pe_done[p - 2])]
            sp = sGa if par == 0 else sGb
            for r2 in range(4):
                base = 768 * (4 * b + r2)
                bd.op("gpsimd", "dma_start", out=Kb[:, par * S + r2 * TLOC:par * S + (r2 + 1) * TLOC],
                      in_=ap["qkv_x"][base + 512:base + 640, :], waits=(w if r2 == 0 else []),
                      incs=[(sp, 16)])
                bd.op("gpsimd", "dma_start", out=Vb[:, par * S + r2 * TLOC:par * S + (r2 + 1) * TLOC],
                      in_=ap["qkv_x"][base + 640:base + 768, :], incs=[(sp, 16)])
                bd.op("gpsimd", "dma_start", out=Qb[:, par * S + r2 * TLOC:par * S + (r2 + 1) * TLOC],
                      in_=ap["qkv_x"][base + hh * 128:base + hh * 128 + 128, :], incs=[(sp, 16)])
            ld = (sp, sp.v)
            v0 = sV.v
            for g4 in range(8):
                pt = ptps[g4 % 2]
                for j in range(4):
                    kt = g4 * 4 + j
                    w2 = []
                    if j == 0:
                        w2.append(ld)
                        if g4 < 2:
                            w2.append(ptev[g4 % 2])
                        else:
                            w2.append((sV, v0 + g4 - 1))
                    bd.op("tensor", "transpose", pt[:, j * 128:(j + 1) * 128],
                          Vb[:, par * S + kt * 128:par * S + (kt + 1) * 128], ident[:],
                          waits=w2, incs=[(sT, 1)] if j == 3 else [])
                bd.op("vector", "tensor_copy",
                      out=VT[:].rearrange("p (c t) -> p c t", t=128)[:, g4 * 4:g4 * 4 + 4, :],
                      in_=pt[:, 0:512].rearrange("p (a b) -> p a b", a=4),
                      waits=[(sT, sT.v)], incs=[(sV, 1)])
                ptev[g4 % 2] = (sV, sV.v)
            def emit_back(ctx):
                gq, qpar, row, resc_v = ctx
                ntk = gq + 1
                gev = []
                for g4 in range(0, ntk, 4):
                    gn = min(4, ntk - g4)
                    bank = (g4 // 4) % 2
                    pt = ptps[bank]
                    for j in range(gn):
                        kt = g4 + j
                        w5 = []
                        if j == 0:
                            w5.append((sV, resc_v))
                            w5.append(ptev[bank])
                        bd.op("tensor", "transpose", pt[:, j * 128:(j + 1) * 128],
                              Psb[:, qpar * S + kt * 128:qpar * S + (kt + 1) * 128],
                              ident[:], waits=w5, incs=[(sT, 1)] if j == gn - 1 else [])
                    bd.op("vector", "tensor_copy",
                          out=PTs[:, qpar * S + g4 * 128:qpar * S + (g4 + gn) * 128].rearrange("p (a b) -> p a b", b=128),
                          in_=pt[:, 0:gn * 128].rearrange("p (a b) -> p a b", a=gn),
                          waits=[(sT, sT.v)], incs=[(sV, 1)])
                    gev.append(sV.v)
                    ptev[bank] = (sV, sV.v)
                for kt in range(ntk):
                    w6 = []
                    if kt % 4 == 0:
                        w6.append((sV, gev[kt // 4]))
                    if kt == 0:
                        w6.append(ovev[qpar])
                    bd.op("tensor", "matmul", ovps[qpar][:, 0:128],
                          VT[:, kt * 128:(kt + 1) * 128],
                          PTs[:, qpar * S + kt * 128:qpar * S + (kt + 1) * 128],
                          start=(kt == 0), stop=(kt == ntk - 1), waits=w6,
                          incs=[(sT, 1)] if kt == ntk - 1 else [])
                pv_done[qpar] = sT.v
                w7 = [(sT, sT.v)]
                if gq < 2:
                    w7.append((sSt2[par], otst[par]))
                bd.op("vector", "tensor_copy", out=otb[:, par * S + gq * 128:par * S + row],
                      in_=ovps[qpar][:, 0:128], waits=w7, incs=[(sV, 1)])
                ovev[qpar] = (sV, sV.v)

            prev_ctx = None
            for gq in range(32):
                qpar = gq % 2
                row = (gq + 1) * 128
                nbk = (row + 511) // 512
                evs = []
                for j in range(nbk):
                    wj = min(512, row - j * 512)
                    w2 = [scev[j % 2]]
                    if p == 0 and gq == 0 and j < 2:
                        w2.append((sV, p7_done_v))
                    bd.op("tensor", "matmul", scps[j % 2][:, 0:wj],
                          Qb[:, par * S + gq * 128:par * S + (gq + 1) * 128],
                          Kb[:, par * S + j * 512:par * S + j * 512 + wj],
                          start=True, stop=True, waits=w2, incs=[(sT, 1)])
                    mmsv = sT.v
                    w3 = [(sT, mmsv)]
                    if gq >= 2 and j == 0:
                        w3.append((sA, exp_done[qpar]))
                    bd.op("scalar", "activation", out=Ssb[:, qpar * S + j * 512:qpar * S + j * 512 + wj],
                          in_=scps[j % 2][:, 0:wj], func=AF.Copy, waits=w3, incs=[(sA, 1)])
                    scev[j % 2] = (sA, sA.v)
                    evs.append(sA.v)
                bd.op("vector", "tensor_add", Ssb[:, qpar * S + gq * 128:qpar * S + row],
                      Ssb[:, qpar * S + gq * 128:qpar * S + row], cmask[:],
                      waits=[(sA, evs[-1])])
                bd.op("vector", "tensor_reduce", out=amx[:, 0:1],
                      in_=Ssb[:, qpar * S:qpar * S + row], axis=AX.X, op=ALU.max,
                      incs=[(sE, 1)])
                bd.op("vector", "tensor_scalar", out=amx[:, 1:2], in0=amx[:, 0:1],
                      scalar1=-1.0, scalar2=None, op0=ALU.mult, waits=[(sE, sE.v)],
                      incs=[(sV, 1)])
                nmv = sV.v
                w4 = [(sV, nmv)]
                if gq >= 2:
                    w4.append((sT, pv_done[qpar]))
                bd.op("scalar", "activation", out=Psb[:, qpar * S:qpar * S + row],
                      in_=Ssb[:, qpar * S:qpar * S + row], func=AF.Exp,
                      bias=amx[:, 1:2], accum_out=amx[:, 2:3], waits=w4, incs=[(sA, 1)])
                exp_done[qpar] = sA.v
                bd.op("vector", "reciprocal", out=amx[:, 3:4], in_=amx[:, 2:3],
                      waits=[(sA, sA.v)], incs=[(sE, 1)])
                bd.op("vector", "tensor_scalar", out=Psb[:, qpar * S:qpar * S + row],
                      in0=Psb[:, qpar * S:qpar * S + row], scalar1=amx[:, 3:4],
                      scalar2=None, op0=ALU.mult, waits=[(sE, sE.v)], incs=[(sV, 1)])
                resc_v = sV.v
                if prev_ctx is not None:
                    emit_back(prev_ctx)
                prev_ctx = (gq, qpar, row, resc_v)
            emit_back(prev_ctx)
            pair_pe_done[p] = sT.v
            ot_v = sV.v
            for r2 in range(4):
                bd.op("scalar", "dma_start",
                      out=ap["at_c"][512 * (4 * b + r2) + hh * 128:512 * (4 * b + r2) + (hh + 1) * 128, :],
                      in_=otb[:, par * S + r2 * TLOC:par * S + (r2 + 1) * TLOC],
                      waits=[(sV, ot_v)], incs=[(sSt2[par], 16)])
            otst[par] = sSt2[par].v
        at_store_w = [(sSt2[0], sSt2[0].v), (sSt2[1], sSt2[1].v)]
        p9_done_t = sT.v
        p9_done_v = sV.v
        bd.wait("gpsimd", *at_store_w)
        bd.op("gpsimd", "collective_compute", "AllToAll", ALU.bypass,
              replica_groups=[list(range(NCORE))], ins=[ap["at_c"].opt()],
              outs=[ap["at_x"].opt()], a2a=True, incs=[(sCC[CC_AT], 1)])
        if stage < 10 and stage >= 3:
            bd.op("gpsimd", "dma_start", out=ap["dbg_at"], in_=ap["at_x"][0:512, :],
                  waits=[(sCC[CC_AT], 1)], incs=[(sG, 16)])

    if stage < 4:
        return _close(nc, bd, stk, sS, sX, sG, sCC, stage)

    with ExitStack() as phD:
        sbE = lambda n, s, d: phD.enter_context(nc.sbuf_tensor(n, s, d))
        psE = lambda n, s, d: phD.enter_context(nc.psum_tensor(n, s, d))
        hbuf = sbE("hbuf", [128, 6144], F32)
        hqt = sbE("hqt", [128, QW], BF16)
        hqfm = sbE("hqfm", [128, FCH * 512], BF16)
        atsb = sbE("atsb", [128, 32 * 512], BF16)
        wdr = sbE("wdr", [128, 4 * 512], BF16)
        wor = sbE("wor", [128, 4 * 512], BF16)
        xrr = sbE("xrr", [128, 4 * 512], F32)
        osb = sbE("osb", [128, 2 * 512], F32)
        lrb_sb = sbE("lrb_sb2", [64, D], BF16)
        fps = [psE(f"fps{i}", [128, 512], F32) for i in range(4)]
        ptF = [psE(f"ptF{i}", [128, 1024], BF16) for i in range(2)]
        pbc2 = psE("pbc2", [128, 512], F32)

        bd.op("gpsimd", "dma_start", out=lrb_sb[:], in_=ap["lrb"],
              waits=[(sT, p9_done_t)] + at_store_w, incs=[(sGc, 16)])
        lrbg = sGc.v
        bd.op("sync", "dma_start", out=srow[0:1, 8:16], in_=ap["hm_g"],
              waits=[(sCC[CC_HM], 1)], incs=[(sXc, 16)])
        bd.op("tensor", "matmul", pbc2[:, 0:1], onesf[:], srow[0:1, 8:9], start=True, stop=True,
              waits=[(sXc, sXc.v), (sT, p9_done_t)], incs=[(sT, 1)])
        bd.op("vector", "tensor_scalar", out=scl[:, 8:9], in0=pbc2[:, 0:1],
              scalar1=1.0 / 2688.0, scalar2=1e-12, op0=ALU.mult, op1=ALU.max,
              waits=[(sT, sT.v), (sV, p9_done_v)], incs=[(sE, 1)])
        bd.op("vector", "tensor_scalar", out=scl[:, 9:10], in0=scl[:, 8:9],
              scalar1=6.0, scalar2=None, op0=ALU.mult, waits=[(sE, sE.v)], incs=[(sE, 1)])
        bd.op("vector", "reciprocal", out=scl[:, 9:10], in_=scl[:, 9:10],
              waits=[(sE, sE.v)], incs=[(sV, 1)])
        hscale_v = sV.v

        HCH = ((0, 1024), (1024, 1024), (2048, 1024), (3072, 1024), (4096, 1024),
               (5120, 1024), (6144, 1024), (7168, 1024), (8192, 1024), (9216, 1024),
               (10240, 768))
        evac_prev = {tr: (sV, hscale_v) for tr in range(4)}
        ptF_ev = {0: (sV, p9_done_v), 1: (sV, p9_done_v)}
        hq_tp_prev = [p9_done_t]
        osb_st = {0: 0, 1: 0}
        for hf in range(2):
            watt = ([(sCC[CC_AT], 1), (sT, p9_done_t)] + at_store_w) if hf == 0 else [(sT, half_done_t)]
            for dc in range(32):
                bd.op("sync", "dma_start", out=atsb[:, dc * 512:(dc + 1) * 512],
                      in_=ap["at_x"][(dc // 4) * 512 + (dc % 4) * 128:(dc // 4) * 512 + (dc % 4) * 128 + 128,
                                     hf * 512:hf * 512 + 512],
                      waits=(watt if dc == 0 else []), incs=[(sXat, 16)])
            at_ld = sXat.v
            hq_v = hscale_v if hf == 0 else hq_v
            for tr in range(4):
                tt = hf * 4 + tr
                for hs0, hw in ((0, 6144), (6144, 4864)):
                    w = [(sV, sV.v), (sA, qstate["last_cast"])] + p7_spill_w + at_store_w
                    if hf == 1 and tr == 0 and hs0 == 0:
                        w.append((sT, half_done_t))
                    bd.op("sync", "dma_start", out=hbuf[:, 0:hw],
                          in_=ap["hsp"][tt * 128:(tt + 1) * 128, hs0:hs0 + hw],
                          waits=w, incs=[(sXh, 16)])
                    hld = (sXh, sXh.v)
                    for (c0, cw) in HCH:
                        if c0 < hs0 or c0 >= hs0 + hw:
                            continue
                        qc = cw - (128 if c0 + cw == Fm else 0)
                        a_last = None
                        if qc > 0:
                            _, a_last = qdq_tile(hbuf[:, c0 - hs0:c0 - hs0 + qc], hqt[:, 0:qc],
                                                 128, qc, scl[:, 8:9], scl[:, 9:10], [hld],
                                                 cast_waits=[(sT, hq_tp_prev[0])])
                        if qc < cw:
                            bd.op("scalar", "activation", out=hqt[:, qc:cw],
                                  in_=hbuf[:, c0 - hs0 + qc:c0 - hs0 + cw],
                                  func=AF.Copy, waits=[(sT, hq_tp_prev[0])], incs=[(sA, 1)])
                            a_last = sA.v
                            qstate["last_cast"] = sA.v
                        nch = cw // 128
                        for g0 in range(0, nch, 4):
                            gn = min(4, nch - g0)
                            bank = (g0 // 4 + c0 // 512) % 2
                            pt = ptF[bank]
                            for j in range(gn):
                                w5 = []
                                if j == 0:
                                    w5.append((sA, a_last))
                                    w5.append(ptF_ev[bank])
                                bd.op("tensor", "transpose", pt[:, j * 128:(j + 1) * 128],
                                      hqt[:, (g0 + j) * 128:(g0 + j + 1) * 128], ident[:],
                                      waits=w5, incs=[(sT, 1)] if j == gn - 1 else [])
                            fc0 = c0 // 128 + g0
                            bd.op("vector", "tensor_copy",
                                  out=hqfm[:].rearrange("p (f q) -> p f q", q=512)[:, fc0:fc0 + gn, tr * 128:tr * 128 + 128],
                                  in_=pt[:, 0:gn * 128].rearrange("p (a b) -> p a b", a=gn),
                                  waits=[(sT, sT.v)], incs=[(sV, 1)])
                            ptF_ev[bank] = (sV, sV.v)
                        hq_tp_prev[0] = sT.v
                hq_v = sV.v
            half_hq_v = sV.v
            for ob in range(8):
                wd_ld = {}
                wo_ld = {}
                for fc in range(4):
                    w = [(sV, half_hq_v)] if (hf == 0 and ob == 0 and fc == 0) else \
                        ([(sT, sT.v)] if fc == 0 else [])
                    bd.op("sync", "dma_start", out=wdr[:, fc * 512:(fc + 1) * 512],
                          in_=ap["wdt_g"][ob * Fm + fc * 128:ob * Fm + (fc + 1) * 128, :],
                          waits=w, incs=[(sXw[fc % 4], 16)])
                    wd_ld[fc] = (sXw[fc % 4], sXw[fc % 4].v)
                mm_slot = {}
                for fc in range(FCH):
                    for tr in range(4):
                        w2 = [wd_ld[fc]] if tr == 0 else []
                        if fc == 0:
                            w2.append(evac_prev[tr])
                            if tr == 0:
                                w2.append((sV, half_hq_v))
                        bd.op("tensor", "matmul", fps[tr][:],
                              hqfm[:, fc * 512 + tr * 128:fc * 512 + tr * 128 + 128],
                              wdr[:, (fc % 4) * 512:(fc % 4) * 512 + 512],
                              start=(fc == 0), stop=False, waits=w2, incs=[(sT, 1)] if tr == 3 else [])
                    mm_slot[fc % 4] = sT.v
                    nfc = fc + 4
                    if nfc < FCH:
                        bd.op("sync", "dma_start", out=wdr[:, (nfc % 4) * 512:(nfc % 4) * 512 + 512],
                              in_=ap["wdt_g"][ob * Fm + nfc * 128:ob * Fm + (nfc + 1) * 128, :],
                              waits=[(sT, mm_slot[nfc % 4])], incs=[(sXw[nfc % 4], 16)])
                        wd_ld[nfc] = (sXw[nfc % 4], sXw[nfc % 4].v)
                for dc in range(4):
                    w = [(sXat, at_ld), (sT, sT.v)] if dc == 0 else []
                    bd.op("gpsimd", "dma_start", out=wor[:, dc * 512:(dc + 1) * 512],
                          in_=ap["wo"][ob, dc], waits=w, incs=[(sGw[dc % 4], 16)])
                    wo_ld[dc] = (sGw[dc % 4], sGw[dc % 4].v)
                for dc in range(32):
                    for tr in range(4):
                        w2 = [wo_ld[dc]] if tr == 0 else []
                        bd.op("tensor", "matmul", fps[tr][:],
                              atsb[:, dc * 512 + tr * 128:dc * 512 + tr * 128 + 128],
                              wor[:, (dc % 4) * 512:(dc % 4) * 512 + 512],
                              start=False, stop=False, waits=w2, incs=[(sT, 1)] if tr == 3 else [])
                    mm_slot[dc % 4] = sT.v
                    ndc = dc + 4
                    if ndc < 32:
                        bd.op("gpsimd", "dma_start", out=wor[:, (ndc % 4) * 512:(ndc % 4) * 512 + 512],
                              in_=ap["wo"][ob, ndc], waits=[(sT, mm_slot[ndc % 4])],
                              incs=[(sGw[ndc % 4], 16)])
                        wo_ld[ndc] = (sGw[ndc % 4], sGw[ndc % 4].v)
                for tr in range(4):
                    tt = hf * 4 + tr
                    bd.op("sync", "dma_start", out=xrr[:, tr * 512:(tr + 1) * 512],
                          in_=ap["xres"][tt * 128:(tt + 1) * 128, ob * 512:(ob + 1) * 512],
                          waits=([(sSo[tr % 2], osb_st[tr % 2])] if (hf > 0 or ob > 0) else []),
                          incs=[(sXr, 16)])
                xr_ld = sXr.v
                for tr in range(4):
                    w2 = [(sGc, lrbg)]
                    bd.op("tensor", "matmul", fps[tr][:],
                          tfm[:, hf * 512 + tr * 128:hf * 512 + tr * 128 + 128],
                          lrb_sb[:, ob * 512:(ob + 1) * 512],
                          start=False, stop=True, waits=w2, incs=[(sT, 1)])
                    lr_mm = sT.v
                    bd.op("vector", "tensor_add", osb[:, (tr % 2) * 512:(tr % 2) * 512 + 512],
                          fps[tr][:], xrr[:, tr * 512:(tr + 1) * 512],
                          waits=[(sT, lr_mm), (sXr, xr_ld),
                                 (sSo[tr % 2], osb_st[tr % 2])], incs=[(sV, 1)])
                    evac_prev[tr] = (sV, sV.v)
                    tt = hf * 4 + tr
                    bd.op("scalar", "dma_start",
                          out=ap["out"][tt * 128:(tt + 1) * 128, ob * 512:(ob + 1) * 512],
                          in_=osb[:, (tr % 2) * 512:(tr % 2) * 512 + 512],
                          waits=[(sV, sV.v)], incs=[(sSo[tr % 2], 16)])
                    osb_st[tr % 2] = sSo[tr % 2].v
            half_done_t = sT.v
        out_w = [(sSo[0], sSo[0].v), (sSo[1], sSo[1].v)]

    return _close(nc, bd, stk, sS, sX, sG, sCC, stage, extra=out_w)


def _close(nc, bd, stk, sS, sX, sG, sCC, stage, extra=()):
    bd.wait("sync", (sS, sS.v), (sG, sG.v), *extra)
    with nc.Block() as block:
        bd.emit(block)
    stk.close()
    return nc


def _host_prepare(inputs):
    f32 = np.float32
    x = np.asarray(inputs["L0_attn_x"], f32).reshape(B * S, D)
    Wq = np.asarray(inputs["L0_attn_Wq"], f32)
    Wk = np.asarray(inputs["L0_attn_Wk"], f32)
    Wv = np.asarray(inputs["L0_attn_Wv"], f32)
    Wo = np.asarray(inputs["L0_attn_Wo"], f32)
    wg = np.asarray(inputs["L0_ffn_w_gate"], f32)
    wu = np.asarray(inputs["L0_ffn_w_up"], f32)
    wd = np.asarray(inputs["L0_ffn_w_down"], f32)
    msc = np.asarray(inputs["L0_ffn_mlp_output_scale"], f32)
    lrA = np.asarray(inputs["L0_ffn_lr_A"], f32)
    lrB = np.asarray(inputs["L0_ffn_lr_B"], f32)
    up = np.asarray(inputs["L0_ffn_up_reorder_index"], np.int64)
    dn = np.asarray(inputs["L0_ffn_down_reorder_index"], np.int64)

    rope = np.concatenate([np.concatenate([h * HD + np.arange(0, HD, 2),
                                           h * HD + np.arange(1, HD, 2)]) for h in range(H)])
    rope_kv = rope[:KVH * HD]
    WQKV = np.concatenate([Wq[up][:, rope], Wk[up][:, rope_kv[:KVH * HD]], Wv[up]], axis=1)
    wq_t = np.ascontiguousarray(WQKV.reshape(32, 128, 48, 128).transpose(2, 1, 0, 3))
    wo_t = np.ascontiguousarray(Wo.reshape(32, 128, 8, 512).transpose(2, 0, 1, 3))
    wg_p = wg[dn][:, up]
    wu_p = wu[dn][:, up]
    wd_p = wd[:, dn]
    lra_t = np.ascontiguousarray(lrA[up].reshape(32, 128, 64).transpose(1, 0, 2))

    inv_freq = 1.0 / (10000.0 ** (np.arange(0, HD, 2, dtype=np.float64) / HD))
    ident = np.eye(128, dtype=ml_dtypes.bfloat16)
    icon = np.tile(np.array([[I_ABS, I_TRUNC, I_EXP, I_SIGN]], dtype=np.int32), (128, 1))
    ones_bf = np.ones((1, 128), dtype=ml_dtypes.bfloat16)
    ones_f = np.ones((1, 128), dtype=np.float32)
    qi, ki = np.meshgrid(np.arange(128), np.arange(128), indexing="ij")
    cmask = np.where(ki > qi, np.float32(NEG), np.float32(0.0)).astype(np.float32)

    in_maps = []
    for c in range(NCORE):
        pos = np.arange(1024 * (c % 4), 1024 * (c % 4) + TLOC, dtype=np.float64)
        ang = pos[:, None] * inv_freq[None, :]
        cos = np.cos(ang).T.astype(f32)
        sin = np.sin(ang).T.astype(f32)
        sc = np.float32(1.0 / np.sqrt(HD))
        m = {
            "xp": np.ascontiguousarray(x[c * TLOC:(c + 1) * TLOC][:, up]),
            "xres": np.ascontiguousarray(x[c * TLOC:(c + 1) * TLOC]),
            "wq": wq_t, "wo": wo_t,
            "wg": np.ascontiguousarray(wg_p[c * FSL:(c + 1) * FSL]),
            "wu": np.ascontiguousarray(wu_p[c * FSL:(c + 1) * FSL]),
            "wd": np.ascontiguousarray(wd_p[c * WDSL:(c + 1) * WDSL]),
            "msc": np.ascontiguousarray(msc[c * WDSL:(c + 1) * WDSL].reshape(4, 128).T),
            "lra": lra_t, "lrb": lrB,
            "cosq": np.ascontiguousarray(cos * sc), "sinq": np.ascontiguousarray(sin * sc),
            "cosk": cos, "sink": sin,
            "ident": ident, "icon": icon, "ones": ones_bf, "onesf": ones_f,
            "fcon": np.tile(np.array([[1.25, 0.0, 0.0, 0.0]], dtype=np.float32), (128, 1)),
            "cmask": cmask,
        }
        in_maps.append(m)
    return in_maps


def kernel(**inputs):
    in_maps = _host_prepare(inputs)
    nc = build(stage=10)
    res = run_bass_kernel_spmd(nc, in_maps, core_ids=list(range(NCORE)))
    out = np.concatenate([np.asarray(res.results[c]["out"]) for c in range(NCORE)], axis=0)
    return out.reshape(B, S, D).astype(np.float32)


# version 03 (final)
# speedup vs baseline: 1.0216x; 1.0035x over previous
import numpy as np
import ml_dtypes
from contextlib import ExitStack

import concourse.bass as bass
import concourse.mybir as mybir
from concourse.bass_utils import run_bass_kernel_spmd
from concourse.aot_env import allow_ici_all_to_all

F32 = mybir.dt.float32
BF16 = mybir.dt.bfloat16
I32 = mybir.dt.int32
AF = mybir.ActivationFunctionType
ALU = mybir.AluOpType
AX = mybir.AxisListType

NCORE = 8
B, S, D, Fm = 2, 4096, 4096, 11008
H, KVH, HD = 32, 8, 128
TLOC = 1024
NTT = 8
FSL = Fm // NCORE
WDSL = D // NCORE
FCH = Fm // 128
NEG = -1e30
QW = 1024
I_ABS = int(np.int64(0x7FFFFFFF))
I_TRUNC = int(np.int64(0xFFC00000) - (1 << 32))
I_EXP = int(np.int64(0xFF800000) - (1 << 32))
I_SIGN = int(np.int64(0x80000000) - (1 << 32))


class Sem:
    def __init__(self, stk, nc, name):
        self.h = stk.enter_context(nc.semaphore(name))
        self.v = 0

    def bump(self, n=1):
        self.v += n
        return self.v


class Bld:
    ENGS = ("sync", "scalar", "vector", "tensor", "gpsimd")

    def __init__(self, nc, stk):
        self.nc, self.stk = nc, stk
        self.q = {e: [] for e in self.ENGS}

    def op(self, eng, meth, *args, waits=(), incs=(), a2a=False, **kwargs):
        self.q[eng].append((meth, args, kwargs, list(waits), list(incs), a2a))
        for sem, n in incs:
            sem.bump(n)

    def wait(self, eng, *waits):
        self.q[eng].append((None, (), {}, list(waits), [], False))

    def emit(self, block):
        for eng in self.ENGS:
            ops = self.q[eng]

            def make(ops_):
                def run(e):
                    for meth, args, kwargs, waits, incs, a2a in ops_:
                        for sem, val in waits:
                            e.wait_ge(sem.h, val)
                        if meth is None:
                            continue
                        if a2a:
                            with allow_ici_all_to_all():
                                inst = getattr(e, meth)(*args, **kwargs)
                        else:
                            inst = getattr(e, meth)(*args, **kwargs)
                        for sem, n in incs:
                            inst.then_inc(sem.h, n)
                return run
            getattr(block, eng)(make(ops))


def build(stage=10):
    nc = bass.Bass("TRN2", target_bir_lowering=False, debug=False)
    stk = ExitStack()
    bd = Bld(nc, stk)
    T = {}

    def dram(name, shape, dtype, kind="Internal"):
        T[name] = nc.dram_tensor(name, shape, dtype, kind=kind)

    dram("xp", [TLOC, D], F32, "ExternalInput")
    dram("xres", [TLOC, D], F32, "ExternalInput")
    dram("wq", [48, 128, 32, 128], F32, "ExternalInput")
    dram("wo", [8, 32, 128, 512], F32, "ExternalInput")
    dram("wg", [FSL, D], F32, "ExternalInput")
    dram("wu", [FSL, D], F32, "ExternalInput")
    dram("wd", [WDSL, Fm], F32, "ExternalInput")
    dram("msc", [128, 4], F32, "ExternalInput")
    dram("lra", [128, 32, 64], F32, "ExternalInput")
    dram("lrb", [64, D], F32, "ExternalInput")
    dram("cosq", [64, TLOC], F32, "ExternalInput")
    dram("sinq", [64, TLOC], F32, "ExternalInput")
    dram("cosk", [64, TLOC], F32, "ExternalInput")
    dram("sink", [64, TLOC], F32, "ExternalInput")
    dram("ident", [128, 128], BF16, "ExternalInput")
    dram("icon", [128, 4], I32, "ExternalInput")
    dram("fcon", [128, 4], F32, "ExternalInput")
    dram("ones", [1, 128], BF16, "ExternalInput")
    dram("onesf", [1, 128], F32, "ExternalInput")
    dram("cmask", [128, 128], F32, "ExternalInput")
    dram("out", [TLOC, D], F32, "ExternalOutput")

    dram("wgt_c", [D, FSL], BF16)
    dram("wut_c", [D, FSL], BF16)
    dram("wdt_c", [Fm, WDSL], BF16)
    dram("wgt_g", [NCORE * D, FSL], BF16)
    dram("wut_g", [NCORE * D, FSL], BF16)
    dram("wdt_g", [NCORE * Fm, WDSL], BF16)
    dram("qkv_c", [NCORE * 768, TLOC], BF16)
    dram("qkv_x", [NCORE * 768, TLOC], BF16)
    dram("at_c", [NCORE * 512, TLOC], BF16)
    dram("at_x", [NCORE * 512, TLOC], BF16)
    dram("mx_c", [1, 8], F32)
    dram("mx_g", [1, 8], F32)
    dram("hm_c", [1, 8], F32)
    dram("hm_g", [1, 8], F32)
    dram("hsp", [TLOC, Fm], F32)
    dram("sc1", [128, 1], F32)
    if stage < 10:
        dram("dbg_xq", [128, D], BF16, "ExternalOutput")
        dram("dbg_xn", [128, D], BF16, "ExternalOutput")
        dram("dbg_wgt", [256, FSL], BF16, "ExternalOutput")
        dram("dbg_wdt", [256, WDSL], BF16, "ExternalOutput")
        dram("dbg_qkv", [768, TLOC], BF16, "ExternalOutput")
        dram("dbg_mx", [1, 8], F32, "ExternalOutput")
    if stage < 10 and stage >= 2:
        dram("dbg_h", [128, Fm], F32, "ExternalOutput")
        dram("dbg_hm", [1, 8], F32, "ExternalOutput")
    if stage < 10 and stage >= 3:
        dram("dbg_at", [512, TLOC], BF16, "ExternalOutput")

    ap = {k: v.ap() for k, v in T.items()}

    sX = Sem(stk, nc, "sX")
    sXa = Sem(stk, nc, "sXa")
    sXb = Sem(stk, nc, "sXb")
    sXc = Sem(stk, nc, "sXc")
    sXh = Sem(stk, nc, "sXh")
    sXat = Sem(stk, nc, "sXat")
    sXr = Sem(stk, nc, "sXr")
    sXw = [Sem(stk, nc, f"sXw{i}") for i in range(4)]
    sS = Sem(stk, nc, "sS")
    sSh = [Sem(stk, nc, f"sSh{i}") for i in range(4)]
    sSq = [Sem(stk, nc, f"sSq{i}") for i in range(2)]
    sSo = [Sem(stk, nc, f"sSo{i}") for i in range(2)]
    sSt2 = [Sem(stk, nc, f"sSt2{i}") for i in range(2)]
    sG = Sem(stk, nc, "sG")
    sGa = Sem(stk, nc, "sGa")
    sGb = Sem(stk, nc, "sGb")
    sGc = Sem(stk, nc, "sGc")
    sGw = [Sem(stk, nc, f"sGw{i}") for i in range(4)]
    sE = Sem(stk, nc, "sE")
    sT = Sem(stk, nc, "sT")
    sV = Sem(stk, nc, "sV")
    sA = Sem(stk, nc, "sA")
    sCC = [Sem(stk, nc, f"sCC{i}") for i in range(7)]
    CC_MAX, CC_WG, CC_WU, CC_WD, CC_QKV, CC_HM, CC_AT = range(7)

    def sb(name, shape, dtype):
        return stk.enter_context(nc.sbuf_tensor(name, shape, dtype))

    ident = sb("ident_s", [128, 128], BF16)
    icon = sb("icon_s", [128, 4], I32)
    fcon = sb("fcon_s", [128, 4], F32)
    cmask = sb("cmask_s", [128, 128], F32)
    ones = sb("ones_s", [1, 128], BF16)
    onesf = sb("onesf_s", [1, 128], F32)
    msc_sb = sb("msc_s", [128, 4], F32)
    lnstat = sb("lnstat_s", [128, 16], F32)
    lnsm = sb("lnsm_s", [128, 8], F32)
    scl = sb("scl_s", [128, 10], F32)
    smax = sb("smax_s", [128, 6], F32)
    srow = sb("srow_s", [1, 16], F32)
    prow = sb("prow_s", [1, 128], F32)
    tfm = sb("tfm_s", [64, TLOC], BF16)

    q_bm = sb("q_bm", [128, QW // 16], F32)
    q_sbf = sb("q_sbf", [128, QW // 16], F32)
    q_v = sb("q_v", [128, QW // 16], F32)
    q_rv = sb("q_rv", [128, QW // 16], F32)
    q_a = sb("q_a", [128, QW], F32)
    q_b = sb("q_b", [128, QW], F32)
    q_l = sb("q_l", [128, QW], F32)
    q_m = sb("q_m", [128, QW], F32)

    qstate = {"last_cast": 0}

    def qdq_tile(src, dst_bf, rc, cw, s_ap, rs6_ap, first_waits, extra_scale=None,
                 cast_waits=()):
        nb = cw // 16
        srcb = src.rearrange("p (n b) -> p n b", b=16)
        bd.op("vector", "tensor_reduce",
              out=q_bm[0:rc, 0:nb].rearrange("p (n o) -> p n o", o=1),
              in_=srcb, axis=AX.X, op=ALU.max, apply_absolute_value=True,
              waits=list(first_waits) + [(sA, qstate["last_cast"])], incs=[(sE, 1)])
        bd.op("vector", "tensor_scalar", out=q_sbf[0:rc, 0:nb], in0=q_bm[0:rc, 0:nb],
              scalar1=rs6_ap, scalar2=1e-12, op0=ALU.mult, op1=ALU.max,
              waits=[(sE, sE.v)], incs=[(sE, 1)])
        bd.op("vector", "tensor_scalar", out=q_sbf[0:rc, 0:nb], in0=q_sbf[0:rc, 0:nb],
              scalar1=448.0, scalar2=None, op0=ALU.min, waits=[(sE, sE.v)], incs=[(sE, 1)])
        bd.op("vector", "tensor_scalar", out=q_v[0:rc, 0:nb], in0=q_sbf[0:rc, 0:nb],
              scalar1=s_ap, scalar2=None, op0=ALU.mult, waits=[(sE, sE.v)], incs=[(sE, 1)])
        bd.op("vector", "reciprocal", out=q_rv[0:rc, 0:nb], in_=q_v[0:rc, 0:nb],
              waits=[(sE, sE.v)], incs=[(sE, 1)])
        rv_b = q_rv[0:rc, 0:nb].rearrange("p (n o) -> p n o", o=1).broadcast_to((rc, nb, 16))
        v_b = q_v[0:rc, 0:nb].rearrange("p (n o) -> p n o", o=1).broadcast_to((rc, nb, 16))
        av = q_a[0:rc, 0:cw]
        bd.op("vector", "tensor_tensor", av.rearrange("p (n b) -> p n b", b=16), srcb, rv_b,
              ALU.mult, waits=[(sE, sE.v)])
        bd.op("vector", "tensor_scalar", out=q_b[0:rc, 0:cw].bitcast(I32), in0=av.bitcast(I32),
              scalar1=icon[0:rc, 0:1], scalar2=None, op0=ALU.bitwise_and, incs=[(sV, 1)])
        vb = sV.v
        bd.op("scalar", "activation", out=q_l[0:rc, 0:cw], in_=q_b[0:rc, 0:cw],
              func=AF.Identity, bias=fcon[0:rc, 0:1], waits=[(sV, vb)], incs=[(sA, 1)])
        al = sA.v
        bd.op("vector", "tensor_scalar", out=q_m[0:rc, 0:cw].bitcast(I32), in0=av.bitcast(I32),
              scalar1=icon[0:rc, 2:3], scalar2=None, op0=ALU.bitwise_and)
        bd.op("vector", "scalar_tensor_tensor", out=q_m[0:rc, 0:cw], in0=q_m[0:rc, 0:cw],
              scalar=0.25, in1=av, op0=ALU.mult, op1=ALU.add)
        bd.op("vector", "tensor_scalar", out=q_m[0:rc, 0:cw].bitcast(I32),
              in0=q_m[0:rc, 0:cw].bitcast(I32), scalar1=icon[0:rc, 1:2],
              scalar2=icon[0:rc, 0:1], op0=ALU.bitwise_and, op1=ALU.bitwise_and)
        bd.op("vector", "tensor_single_scalar", out=q_b[0:rc, 0:cw].bitcast(I32),
              in_=q_b[0:rc, 0:cw], scalar=1.0, op=ALU.is_lt)
        bd.op("vector", "tensor_scalar", out=q_l[0:rc, 0:cw].bitcast(I32),
              in0=q_l[0:rc, 0:cw].bitcast(I32), scalar1=icon[0:rc, 1:2], scalar2=None,
              op0=ALU.bitwise_and, waits=[(sA, al)])
        bd.op("vector", "tensor_scalar", out=q_l[0:rc, 0:cw], in0=q_l[0:rc, 0:cw],
              scalar1=1.0, scalar2=None, op0=ALU.subtract)
        bd.op("vector", "select", out=q_m[0:rc, 0:cw], mask=q_b[0:rc, 0:cw].bitcast(I32),
              on_true=q_l[0:rc, 0:cw], on_false=q_m[0:rc, 0:cw])
        bd.op("vector", "tensor_tensor", q_m[0:rc, 0:cw].rearrange("p (n b) -> p n b", b=16),
              q_m[0:rc, 0:cw].rearrange("p (n b) -> p n b", b=16), v_b, ALU.mult)
        if extra_scale is not None:
            bd.op("vector", "tensor_scalar", out=q_m[0:rc, 0:cw], in0=q_m[0:rc, 0:cw],
                  scalar1=extra_scale, scalar2=None, op0=ALU.mult)
        bd.op("vector", "scalar_tensor_tensor", out=q_m[0:rc, 0:cw].bitcast(I32),
              in0=src.bitcast(I32), scalar=icon[0:rc, 3:4], in1=q_m[0:rc, 0:cw].bitcast(I32),
              op0=ALU.bitwise_and, op1=ALU.bitwise_or, incs=[(sV, 1)])
        v_done = sV.v
        bd.op("scalar", "activation", out=dst_bf, in_=q_m[0:rc, 0:cw], func=AF.Copy,
              waits=[(sV, v_done)] + list(cast_waits), incs=[(sA, 1)])
        qstate["last_cast"] = sA.v
        return v_done, sA.v

    def pcol_to_srow(col_ap, dstc, first_waits):
        bd.op("scalar", "dma_start", out=ap["sc1"], in_=col_ap,
              waits=list(first_waits) + [(sXc, sXc.v)], incs=[(sS, 16)])
        bd.op("sync", "dma_start",
              out=prow[0:1, 0:128].rearrange("o (a b) -> o a b", a=1),
              in_=ap["sc1"].rearrange("(a p) o -> a o p", a=1),
              waits=[(sS, sS.v)], incs=[(sXc, 16)])
        bd.op("vector", "tensor_reduce", out=srow[0:1, dstc:dstc + 1],
              in_=prow[0:1, 0:128].rearrange("o (a b) -> o a b", a=1),
              axis=AX.X, op=ALU.max, waits=[(sXc, sXc.v)], incs=[(sV, 1)])

    for name, dst in (("ident", ident), ("icon", icon), ("fcon", fcon), ("cmask", cmask),
                      ("ones", ones), ("onesf", onesf), ("msc", msc_sb)):
        bd.op("sync", "dma_start", out=dst[:], in_=ap[name], incs=[(sX, 16)])
    bd.op("vector", "memset", smax[:], 0.0, waits=[(sX, sX.v)], incs=[(sV, 1)])

    with ExitStack() as ph12:
        sbA = lambda n, s, d: ph12.enter_context(nc.sbuf_tensor(n, s, d))
        xsc = sbA("xsc", [128, D], F32)
        lscr = sbA("lscr", [128, D], F32)
        w1buf = sbA("w1buf", [128, 2 * 2048], F32)

        for tt in range(NTT):
            w = [(sV, sV.v)] if tt > 0 else []
            bd.op("sync", "dma_start", out=xsc[:], in_=ap["xp"][tt * 128:(tt + 1) * 128, :],
                  waits=w, incs=[(sXc, 16)])
            ld = sXc.v
            bd.op("vector", "tensor_reduce", out=lnsm[:, 0:1], in_=xsc[:], axis=AX.X,
                  op=ALU.add, waits=[(sXc, ld)], incs=[(sE, 1)])
            bd.op("scalar", "activation", out=lscr[:], in_=xsc[:], func=AF.Square,
                  accum_out=lnsm[:, 1:2], waits=[(sXc, ld)], incs=[(sA, 1)])
            bd.op("vector", "tensor_scalar", out=lnstat[:, 2 * tt:2 * tt + 1], in0=lnsm[:, 0:1],
                  scalar1=1.0 / D, scalar2=None, op0=ALU.mult,
                  waits=[(sA, sA.v), (sE, sE.v)], incs=[(sE, 1)])
            bd.op("vector", "tensor_scalar", out=lnsm[:, 3:4], in0=lnsm[:, 1:2],
                  scalar1=1.0 / D, scalar2=None, op0=ALU.mult, incs=[(sE, 1)])
            bd.op("vector", "tensor_scalar", out=lnsm[:, 4:5], in0=lnstat[:, 2 * tt:2 * tt + 1],
                  scalar1=lnstat[:, 2 * tt:2 * tt + 1], scalar2=None, op0=ALU.mult,
                  waits=[(sE, sE.v)], incs=[(sE, 1)])
            bd.op("vector", "tensor_sub", lnsm[:, 5:6], lnsm[:, 3:4], lnsm[:, 4:5],
                  waits=[(sE, sE.v)], incs=[(sE, 1)])
            bd.op("vector", "tensor_scalar", out=lnsm[:, 5:6], in0=lnsm[:, 5:6],
                  scalar1=1e-5, scalar2=None, op0=ALU.add, waits=[(sE, sE.v)], incs=[(sE, 1)])
            bd.op("vector", "reciprocal", out=lnsm[:, 7:8], in_=lnsm[:, 5:6],
                  waits=[(sE, sE.v)], incs=[(sV, 1)])
            bd.op("scalar", "activation", out=lnstat[:, 2 * tt + 1:2 * tt + 2], in_=lnsm[:, 7:8],
                  func=AF.Sqrt, waits=[(sV, sV.v)], incs=[(sA, 1)])
            bd.op("vector", "tensor_scalar", out=xsc[:], in0=xsc[:],
                  scalar1=lnstat[:, 2 * tt:2 * tt + 1], scalar2=lnstat[:, 2 * tt + 1:2 * tt + 2],
                  op0=ALU.subtract, op1=ALU.mult, waits=[(sA, sA.v)])
            bd.op("vector", "tensor_reduce", out=lnsm[:, 0:1], in_=xsc[:], axis=AX.X,
                  op=ALU.max, apply_absolute_value=True, incs=[(sE, 1)])
            bd.op("vector", "tensor_max", smax[:, 3:4], smax[:, 3:4], lnsm[:, 0:1],
                  waits=[(sE, sE.v)], incs=[(sV, 1)])

        def wtiles(colw):
            out = []
            for name, rows, cols, smcol in (("wg", FSL, D, 0), ("wu", FSL, D, 1), ("wd", WDSL, Fm, 2)):
                r0 = 0
                while r0 < rows:
                    rc = min(128, rows - r0)
                    c0 = 0
                    while c0 < cols:
                        cc = min(colw, cols - c0)
                        out.append((name, r0, rc, c0, cc, smcol))
                        c0 += cc
                    r0 += rc
            return out

        WT1 = wtiles(2048)
        p2red = []
        for i, (name, r0, rc, c0, cc, smcol) in enumerate(WT1):
            buf = w1buf[0:rc, (i % 2) * 2048:(i % 2) * 2048 + cc]
            sp = sXa if i % 2 == 0 else sXb
            w = [] if i < 2 else [(sV, p2red[i - 2])]
            bd.op("sync", "dma_start", out=buf, in_=ap[name][r0:r0 + rc, c0:c0 + cc],
                  waits=w, incs=[(sp, 16)])
            bd.op("vector", "tensor_reduce", out=lnsm[0:rc, 0:1], in_=buf, axis=AX.X,
                  op=ALU.max, apply_absolute_value=True, waits=[(sp, sp.v)], incs=[(sE, 1)])
            bd.op("vector", "tensor_max", smax[0:rc, smcol:smcol + 1],
                  smax[0:rc, smcol:smcol + 1], lnsm[0:rc, 0:1], waits=[(sE, sE.v)], incs=[(sV, 1)])
            p2red.append(sV.v)
        w1_done = sV.v

        for t in range(4):
            pcol_to_srow(smax[:, t:t + 1], t, [(sV, w1_done)] if t == 0 else [])
        bd.op("vector", "memset", srow[0:1, 4:8], 0.0, incs=[(sV, 1)])
        bd.op("scalar", "dma_start", out=ap["mx_c"], in_=srow[0:1, 0:8],
              waits=[(sV, sV.v)], incs=[(sS, 16)])
        bd.wait("gpsimd", (sS, sS.v))
        bd.op("gpsimd", "collective_compute", "AllReduce", ALU.max,
              replica_groups=[list(range(NCORE))], ins=[ap["mx_c"].opt()],
              outs=[ap["mx_g"].opt()], incs=[(sCC[CC_MAX], 1)])
        bd.op("sync", "dma_start", out=srow[0:1, 0:8], in_=ap["mx_g"],
              waits=[(sCC[CC_MAX], 1)], incs=[(sXc, 16)])
        mxg_ld = sXc.v
        if stage < 10:
            bd.op("gpsimd", "dma_start", out=ap["dbg_mx"], in_=ap["mx_g"],
                  waits=[(sCC[CC_MAX], 1)], incs=[(sG, 16)])


    ph57 = stk.enter_context(ExitStack())
    xqfm = ph57.enter_context(nc.sbuf_tensor("xqfm", [128, 32 * TLOC], BF16))
    with ExitStack() as phA:
        sbB = lambda n, s, d: phA.enter_context(nc.sbuf_tensor(n, s, d))
        psB = lambda n, s, d: phA.enter_context(nc.psum_tensor(n, s, d))
        pbc = psB("pbc", [128, 512], F32)
        ptr = psB("ptr", [128, 1024], BF16)
        ptr2 = psB("ptr2", [128, 1024], BF16)
        qkps = [psB(f"qkps{i}", [128, 512], F32) for i in range(4)]
        tps = psB("tps", [64, 512], F32)

        bd.op("tensor", "matmul", pbc[:, 0:4], onesf[:], srow[0:1, 0:4],
              start=True, stop=True, waits=[(sXc, mxg_ld)], incs=[(sT, 1)])
        bd.op("vector", "tensor_scalar", out=scl[:, 0:4], in0=pbc[:, 0:4],
              scalar1=1.0 / 2688.0, scalar2=1e-12, op0=ALU.mult, op1=ALU.max,
              waits=[(sT, sT.v)], incs=[(sE, 1)])
        bd.op("vector", "tensor_scalar", out=scl[:, 4:8], in0=scl[:, 0:4],
              scalar1=6.0, scalar2=None, op0=ALU.mult, waits=[(sE, sE.v)], incs=[(sE, 1)])
        bd.op("vector", "reciprocal", out=scl[:, 4:8], in_=scl[:, 4:8],
              waits=[(sE, sE.v)], incs=[(sV, 1)])
        scales_v = sV.v

        xnfm = sbB("xnfm", [128, 32 * TLOC], BF16)
        phA5 = phA.enter_context(ExitStack())
        xsc2 = phA5.enter_context(nc.sbuf_tensor("xsc2", [128, D], F32))
        xqt = phA5.enter_context(nc.sbuf_tensor("xqt", [128, D], BF16))
        xnt = phA5.enter_context(nc.sbuf_tensor("xnt", [128, D], BF16))
        xnfm3 = xnfm[:].rearrange("p (c t) -> p c t", t=TLOC)
        xqfm3 = xqfm[:].rearrange("p (c t) -> p c t", t=TLOC)
        last_sign_v = scales_v
        prev_cast_a = 0
        tpose_prev = 0
        tb_ev = {0: (sV, 0), 1: (sV, 0)}
        for tt in range(NTT):
            bd.op("sync", "dma_start", out=xsc2[:], in_=ap["xp"][tt * 128:(tt + 1) * 128, :],
                  waits=[(sV, last_sign_v), (sA, prev_cast_a)], incs=[(sXc, 16)])
            ld = sXc.v
            bd.op("vector", "tensor_scalar", out=xsc2[:], in0=xsc2[:],
                  scalar1=lnstat[:, 2 * tt:2 * tt + 1], scalar2=lnstat[:, 2 * tt + 1:2 * tt + 2],
                  op0=ALU.subtract, op1=ALU.mult, waits=[(sXc, ld), (sV, scales_v)])
            a_last = None
            for c0, cw in ((0, 1024), (1024, 1024), (2048, 1024), (3072, 896)):
                vd, a_last = qdq_tile(xsc2[:, c0:c0 + cw], xqt[:, c0:c0 + cw], 128, cw,
                                      scl[:, 3:4], scl[:, 7:8], [],
                                      cast_waits=[(sT, tpose_prev)])
                last_sign_v = vd
            bd.op("scalar", "activation", out=xqt[:, D - 128:D], in_=xsc2[:, D - 128:D],
                  func=AF.Copy, incs=[(sA, 1)])
            bd.op("scalar", "activation", out=xnt[:], in_=xsc2[:], func=AF.Copy, incs=[(sA, 1)])
            cast_a = sA.v
            prev_cast_a = cast_a
            g = 0
            for srcb, dst3 in ((xnt, xnfm3), (xqt, xqfm3)):
                for dc4 in range(8):
                    bank = g % 2
                    pt = ptr if bank == 0 else ptr2
                    for j in range(4):
                        dc = dc4 * 4 + j
                        w = []
                        if j == 0:
                            w.append((sA, cast_a))
                            w.append(tb_ev[bank])
                        bd.op("tensor", "transpose", pt[:, j * 128:(j + 1) * 128],
                              srcb[:, dc * 128:(dc + 1) * 128], ident[:], waits=w,
                              incs=[(sT, 1)] if j == 3 else [])
                    bd.op("vector", "tensor_copy",
                          out=dst3[:, dc4 * 4:dc4 * 4 + 4, tt * 128:tt * 128 + 128],
                          in_=pt[:, 0:512].rearrange("p (a b) -> p a b", a=4),
                          waits=[(sT, sT.v)], incs=[(sV, 1)])
                    tb_ev[bank] = (sV, sV.v)
                    g += 1
            tpose_prev = sT.v
        xq_done_v = sV.v
        xq_done_a = sA.v
        if stage < 10:
            bd.op("scalar", "dma_start", out=ap["dbg_xq"], in_=xqt[:],
                  waits=[(sV, xq_done_v), (sA, xq_done_a)], incs=[(sS, 16)])
            bd.op("scalar", "dma_start", out=ap["dbg_xn"], in_=xnt[:], incs=[(sS, 16)])
        dbg_xq_st = sS.v

        phA5.close()
        phA4 = phA.enter_context(ExitStack())
        w2buf = phA4.enter_context(nc.sbuf_tensor("w2buf", [128, 2 * QW], F32))
        wdq = phA4.enter_context(nc.sbuf_tensor("wdq", [128, QW], BF16))
        wst = phA4.enter_context(nc.sbuf_tensor("wst", [128, 512], BF16))
        WT2 = wtiles(QW)
        CONTRIB = {"wg": "wgt_c", "wu": "wut_c", "wd": "wdt_c"}
        tile_done = []
        store_v = []
        wdq_tp_prev = 0
        for i, (name, r0, rc, c0, cc, smcol) in enumerate(WT2):
            buf = w2buf[0:rc, (i % 2) * QW:(i % 2) * QW + cc]
            sp = sXa if i % 2 == 0 else sXb
            w = [(sV, xq_done_v), (sA, xq_done_a)] if i < 2 else [(sV, tile_done[i - 2])]
            bd.op("sync", "dma_start", out=buf, in_=ap[name][r0:r0 + rc, c0:c0 + cc],
                  waits=w, incs=[(sp, 16)])
            ldv = (sp, sp.v)
            totc = D if name in ("wg", "wu") else Fm
            pt_cols = max(0, (c0 + cc) - (totc - 128))
            qc = cc - pt_cols
            a_last = xq_done_a
            if qc > 0:
                esc = msc_sb[0:rc, r0 // 128:r0 // 128 + 1] if name == "wd" else None
                _, a_last = qdq_tile(buf[:, 0:qc], wdq[0:rc, 0:qc], rc, qc,
                                     scl[0:rc, smcol:smcol + 1], scl[0:rc, 4 + smcol:5 + smcol],
                                     [ldv], extra_scale=esc,
                                     cast_waits=[(sT, wdq_tp_prev)])
            if pt_cols > 0:
                if name == "wd":
                    bd.op("vector", "tensor_scalar", out=wdq[0:rc, qc:cc], in0=buf[:, qc:cc],
                          scalar1=msc_sb[0:rc, r0 // 128:r0 // 128 + 1], scalar2=None,
                          op0=ALU.mult, waits=[ldv])
                else:
                    bd.op("vector", "tensor_copy", out=wdq[0:rc, qc:cc], in_=buf[:, qc:cc],
                          waits=[ldv])
                bd.op("scalar", "activation", out=wdq[0:rc, qc:cc].bitcast(BF16),
                      in_=wdq[0:rc, qc:cc], func=AF.Copy, waits=[(sT, wdq_tp_prev)],
                      incs=[(sA, 1)])
                a_last = sA.v
            bd.op("vector", "tensor_copy", out=lnsm[0:1, 0:1], in_=lnsm[0:1, 0:1], incs=[(sV, 1)])
            tile_done.append(sV.v)
            nch = cc // 128
            cn = CONTRIB[name]
            colw = FSL if name in ("wg", "wu") else WDSL
            for g0 in range(0, nch, 4):
                gn = min(4, nch - g0)
                bank = (g0 // 4) % 2
                pt = ptr if bank == 0 else ptr2
                for j in range(gn):
                    ch = g0 + j
                    w = []
                    if j == 0:
                        w.append((sA, a_last))
                        w.append(tb_ev[bank])
                    bd.op("tensor", "transpose", pt[0:128, j * 128:j * 128 + rc],
                          wdq[0:rc, ch * 128:(ch + 1) * 128], ident[0:rc, 0:rc], waits=w,
                          incs=[(sT, 1)] if j == gn - 1 else [])
                wstw = [(sT, sT.v)]
                if store_v:
                    wstw.append((sS, store_v[-1]))
                bd.op("vector", "tensor_copy",
                      out=wst[:, 0:gn * 128].rearrange("p (a b) -> p a b", a=gn)[:, :, 0:rc],
                      in_=pt[:, 0:gn * 128].rearrange("p (a b) -> p a b", a=gn)[:, :, 0:rc],
                      waits=wstw, incs=[(sV, 1)])
                tb_ev[bank] = (sV, sV.v)
                dst = bass.AP(T[cn], (c0 + g0 * 128) * colw + r0,
                              [[colw, 128], [128 * colw, gn], [1, rc]])
                bd.op("scalar", "dma_start", out=dst,
                      in_=wst[:, 0:gn * 128].rearrange("p (a b) -> p a b", a=gn)[:, :, 0:rc],
                      waits=[(sV, sV.v)], incs=[(sS, 16)])
                store_v.append(sS.v)
            wdq_tp_prev = sT.v
        w2_store = sS.v
        bd.wait("gpsimd", (sS, w2_store))
        for cc_i, cin, cout in ((CC_WG, "wgt_c", "wgt_g"), (CC_WU, "wut_c", "wut_g"),
                                (CC_WD, "wdt_c", "wdt_g")):
            bd.op("gpsimd", "collective_compute", "AllGather", ALU.bypass,
                  replica_groups=[list(range(NCORE))], ins=[ap[cin].opt()],
                  outs=[ap[cout].opt()], incs=[(sCC[cc_i], 1)])
        if stage < 10:
            bd.op("gpsimd", "dma_start", out=ap["dbg_wgt"], in_=ap["wgt_g"][D:D + 256, :],
                  waits=[(sCC[CC_WG], 1)], incs=[(sG, 16)])
            bd.op("gpsimd", "dma_start", out=ap["dbg_wdt"], in_=ap["wdt_g"][Fm:Fm + 256, :],
                  waits=[(sCC[CC_WD], 1)], incs=[(sG, 16)])

        phA4.close()
        trigq = sbB("trigq", [128, TLOC], F32)
        trigk = sbB("trigk", [128, TLOC], F32)
        qkslab = sbB("qkslab", [128, 2 * 4096], BF16)
        ro1 = sbB("ro1", [64, 512], F32)
        ro2 = sbB("ro2", [64, 512], F32)
        qstg = sbB("qstg", [128, 2 * TLOC], BF16)
        for i, (name, dstt) in enumerate((("cosq", trigq[0:64, :]), ("sinq", trigq[64:128, :]),
                                          ("cosk", trigk[0:64, :]), ("sink", trigk[64:128, :]))):
            bd.op("sync", "dma_start", out=dstt, in_=ap[name],
                  waits=([(sV, sV.v), (sA, sA.v)] if i == 0 else []), incs=[(sXc, 16)])
        trig_ld = sXc.v
        oc_ev = []
        st_v = {}
        mm_done = {}
        for oc in range(48):
            par = oc % 2
            sp = sGa if par == 0 else sGb
            slab = qkslab[:, par * 4096:par * 4096 + 4096]
            w = [(sV, xq_done_v), (sA, xq_done_a)] if oc < 2 else [(sT, mm_done[oc - 2])]
            bd.op("gpsimd", "dma_start", out=slab, in_=ap["wq"][oc], waits=w, incs=[(sp, 16)])
            ldg = (sp, sp.v)
            for th in range(2):
                ps = qkps[par * 2 + th]
                for dc in range(32):
                    w = []
                    if dc == 0:
                        w.append(ldg)
                        if oc >= 2:
                            w.append((sV, oc_ev[oc - 2]))
                        if th == 0 and oc == 0:
                            w.append((sV, xq_done_v))
                    bd.op("tensor", "matmul", ps[:], slab[:, dc * 128:(dc + 1) * 128],
                          xnfm3[:, dc, th * 512:th * 512 + 512],
                          start=(dc == 0), stop=(dc == 31), waits=w,
                          incs=[(sT, 1)] if dc == 31 else [])
            mm_done[oc] = sT.v
            stg = qstg[:, par * TLOC:par * TLOC + TLOC]
            wstg = [(sSq[par], st_v[oc - 2])] if oc >= 2 else []
            if oc < 40:
                trg = trigq if oc < 32 else trigk
                for th in range(2):
                    ps = qkps[par * 2 + th]
                    tv = mm_done[oc] - 1 + th
                    t0 = th * 512
                    cosT = trg[0:64, t0:t0 + 512]
                    sinT = trg[64:128, t0:t0 + 512]
                    w0 = [(sT, tv), (sXc, trig_ld)] + (wstg if th == 0 else [])
                    bd.op("vector", "tensor_tensor", ro1[:, 0:512], ps[0:64, :], cosT,
                          ALU.mult, waits=w0)
                    bd.op("vector", "tensor_tensor", ro2[:, 0:512], ps[64:128, :], sinT, ALU.mult)
                    bd.op("vector", "tensor_sub", stg[0:64, t0:t0 + 512],
                          ro1[:, 0:512], ro2[:, 0:512])
                    bd.op("vector", "tensor_tensor", ro1[:, 0:512], ps[0:64, :], sinT, ALU.mult)
                    bd.op("vector", "tensor_tensor", ro2[:, 0:512], ps[64:128, :], cosT, ALU.mult)
                    bd.op("vector", "tensor_add", stg[64:128, t0:t0 + 512], ro1[:, 0:512],
                          ro2[:, 0:512], incs=[(sV, 1)] if th == 1 else [])
            else:
                bd.op("vector", "tensor_copy", out=stg[:, 0:512], in_=qkps[par * 2][:],
                      waits=[(sT, mm_done[oc] - 1)] + wstg)
                bd.op("vector", "tensor_copy", out=stg[:, 512:1024], in_=qkps[par * 2 + 1][:],
                      waits=[(sT, mm_done[oc])], incs=[(sV, 1)])
            oc_ev.append(sV.v)
            if oc < 32:
                row = 768 * (oc // 4) + (oc % 4) * 128
            elif oc < 40:
                row = 768 * (oc - 32) + 512
            else:
                row = 768 * (oc - 40) + 640
            bd.op("scalar", "dma_start", out=ap["qkv_c"][row:row + 128, :], in_=stg[:],
                  waits=[(sV, sV.v)], incs=[(sSq[par], 16)])
            st_v[oc] = sSq[par].v
        lra_sb = sbB("lra_sb", [128, 32 * 64], BF16)
        bd.op("gpsimd", "dma_start", out=lra_sb[:], in_=ap["lra"], incs=[(sGc, 16)])
        lrg = sGc.v
        for th in range(2):
            for dc in range(32):
                if dc == 0:
                    w = ([(sGc, lrg), (sV, xq_done_v), (sV, oc_ev[46])] if th == 0
                         else [(sV, sV.v)])
                else:
                    w = []
                bd.op("tensor", "matmul", tps[:], lra_sb[:, dc * 64:(dc + 1) * 64],
                      xnfm3[:, dc, th * 512:th * 512 + 512],
                      start=(dc == 0), stop=(dc == 31), waits=w,
                      incs=[(sT, 1)] if dc == 31 else [])
            bd.op("vector", "tensor_copy", out=tfm[:, th * 512:th * 512 + 512], in_=tps[:],
                  waits=[(sT, sT.v)], incs=[(sV, 1)])
        qkv_store_w = [(sSq[0], sSq[0].v), (sSq[1], sSq[1].v)]
        pA_done_v = sV.v
        pA_done_t = sT.v
        bd.wait("gpsimd", *qkv_store_w)
        bd.op("gpsimd", "collective_compute", "AllToAll", ALU.bypass,
              replica_groups=[list(range(NCORE))], ins=[ap["qkv_c"].opt()],
              outs=[ap["qkv_x"].opt()], a2a=True, incs=[(sCC[CC_QKV], 1)])
        if stage < 10:
            bd.op("gpsimd", "dma_start", out=ap["dbg_qkv"], in_=ap["qkv_x"][768:1536, :],
                  waits=[(sCC[CC_QKV], 1)], incs=[(sG, 16)])

    if stage < 2:
        return _close(nc, bd, stk, sS, sX, sG, sCC, stage)

    with ExitStack() as phB:
        sbC = lambda n, s, d: phB.enter_context(nc.sbuf_tensor(n, s, d))
        psC = lambda n, s, d: phB.enter_context(nc.psum_tensor(n, s, d))
        gslab = sbC("gslab", [128, 2 * 32 * 512], BF16)
        gact = sbC("gact", [128, 8 * 512], BF16)
        hev = sbC("hev", [128, 4 * 512], F32)
        gps = [psC(f"gps{i}", [128, 512], F32) for i in range(8)]

        FBS = [(r, c0, w) for r in range(NCORE) for (c0, w) in ((0, 512), (512, 512), (1024, 352))]
        bank_free = {tt: (sV, pA_done_v) for tt in range(NTT)}
        hev_store = {}
        mmv = {}
        spill_s = []
        for kp, (r, c0w, ww) in enumerate(FBS):
            for t_i, gname, ccs in ((0, "wgt_g", CC_WG), (1, "wut_g", CC_WU)):
                k = kp * 2 + t_i
                par = k % 2
                sp = sXa if par == 0 else sXb
                slab = gslab[:, par * 16384:par * 16384 + 16384]
                w = [(sCC[ccs], 1), (sT, pA_done_t)]
                if k >= 2:
                    w = [(sT, mmv[k - 2])]
                for sd in range(8):
                    src = bass.AP(T[gname], (r * D + sd * 512) * FSL + c0w,
                                  [[FSL, 128], [128 * FSL, 4], [1, ww]])
                    bd.op("sync", "dma_start",
                          out=slab[:, sd * 2048:sd * 2048 + 2048].rearrange("p (a b) -> p a b", a=4)[:, :, 0:ww],
                          in_=src, waits=(w if sd == 0 else []), incs=[(sp, 16)])
                ldv = (sp, sp.v)
                for dc in range(32):
                    for tt in range(NTT):
                        w2 = []
                        if dc == 0:
                            w2.append(ldv)
                            w2.append(bank_free[tt])
                        bd.op("tensor", "matmul", gps[tt][:, 0:ww],
                              xqfm[:, dc * TLOC + tt * 128:dc * TLOC + tt * 128 + 128],
                              gslab[:, par * 16384 + dc * 512:par * 16384 + dc * 512 + ww],
                              start=(dc == 0), stop=(dc == 31), waits=w2,
                              incs=[(sT, 1)] if dc == 31 else [])
                        if dc == 31:
                            mmv[(k, tt)] = sT.v
                mmv[k] = sT.v
                if t_i == 0:
                    for tt in range(NTT):
                        bd.op("scalar", "activation", out=gact[:, tt * 512:tt * 512 + ww],
                              in_=gps[tt][:, 0:ww], func=AF.Silu,
                              waits=[(sT, mmv[(k, tt)]), bank_free[tt]], incs=[(sA, 1)])
                        bank_free[tt] = (sA, sA.v)
                else:
                    foff = r * FSL + c0w
                    for tt in range(NTT):
                        slot = tt % 4
                        hbuf = hev[:, slot * 512:slot * 512 + ww]
                        w3 = [(sT, mmv[(k, tt)])]
                        if (kp, tt - 4) in hev_store:
                            w3.append((sSh[slot], hev_store[(kp, tt - 4)]))
                        elif kp > 0 and (kp - 1, tt + 4) in hev_store:
                            w3.append((sSh[slot], hev_store[(kp - 1, tt + 4)]))
                        bd.op("vector", "tensor_tensor", hbuf, gps[tt][:, 0:ww],
                              gact[:, tt * 512:tt * 512 + ww], ALU.mult, waits=w3)
                        bd.op("vector", "tensor_reduce", out=lnsm[:, 0:1],
                              in_=hbuf, axis=AX.X, op=ALU.max,
                              apply_absolute_value=True, incs=[(sE, 1)])
                        bd.op("vector", "tensor_max", smax[:, 4:5], smax[:, 4:5],
                              lnsm[:, 0:1], waits=[(sE, sE.v)], incs=[(sV, 1)])
                        bank_free[tt] = (sV, sV.v)
                        bd.op("scalar", "dma_start",
                              out=ap["hsp"][tt * 128:(tt + 1) * 128, foff:foff + ww],
                              in_=hbuf, waits=[(sV, sV.v)], incs=[(sSh[slot], 16)])
                        hev_store[(kp, tt)] = sSh[slot].v
                        spill_s.append(sS.v)
        p7_spill_w = [(sSh[i], sSh[i].v) for i in range(4)]
        p7_done_t = sT.v
        pcol_to_srow(smax[:, 4:5], 8, [bank_free[tt] for tt in range(NTT)])
        bd.op("vector", "memset", srow[0:1, 9:16], 0.0, incs=[(sV, 1)])
        bd.op("scalar", "dma_start", out=ap["hm_c"], in_=srow[0:1, 8:16],
              waits=[(sV, sV.v)], incs=[(sS, 16)])
        bd.wait("gpsimd", (sS, sS.v))
        bd.op("gpsimd", "collective_compute", "AllReduce", ALU.max,
              replica_groups=[list(range(NCORE))], ins=[ap["hm_c"].opt()],
              outs=[ap["hm_g"].opt()], incs=[(sCC[CC_HM], 1)])
        p7_done_v = sV.v
        if stage < 10 and stage >= 2:
            bd.op("gpsimd", "dma_start", out=ap["dbg_h"], in_=ap["hsp"][0:128, :],
                  waits=p7_spill_w, incs=[(sG, 16)])
            bd.op("gpsimd", "dma_start", out=ap["dbg_hm"], in_=ap["hm_g"],
                  waits=[(sCC[CC_HM], 1)], incs=[(sG, 16)])

    ph57.close()

    if stage < 3:
        return _close(nc, bd, stk, sS, sX, sG, sCC, stage)

    with ExitStack() as phC:
        sbD = lambda n, s, d: phC.enter_context(nc.sbuf_tensor(n, s, d))
        psD = lambda n, s, d: phC.enter_context(nc.psum_tensor(n, s, d))
        Kb = sbD("Kb", [128, 2 * S], BF16)
        Vb = sbD("Vb", [128, 2 * S], BF16)
        Qb = sbD("Qb", [128, 2 * S], BF16)
        VT = sbD("VT", [128, S], BF16)
        Ssb = sbD("Ssb", [128, 2 * S], F32)
        Psb = sbD("Psb", [128, 2 * S], BF16)
        PTs = sbD("PTs", [128, 2 * S], BF16)
        otb = sbD("otb", [128, 2 * S], BF16)
        amx = sbD("amx", [128, 8], F32)
        scps = [psD(f"scps{i}", [128, 512], F32) for i in range(2)]
        ptps = [psD(f"ptps{i}", [128, 1024], BF16) for i in range(2)]
        ovps = [psD(f"ovps{i}", [128, 512], F32) for i in range(2)]

        pair_pe_done = {}
        exp_done = {0: 0, 1: 0}
        pv_done = {0: 0, 1: 0}
        scev = {0: (sA, 0), 1: (sA, 0)}
        ptev = {0: (sV, p7_done_v), 1: (sV, p7_done_v)}
        ovev = {0: (sV, p7_done_v), 1: (sV, p7_done_v)}
        otst = {0: 0, 1: 0}
        for p in range(8):
            b, hh = p // 4, p % 4
            par = p % 2
            w = [(sCC[CC_QKV], 1), (sT, p7_done_t), (sV, p7_done_v)] + p7_spill_w
            if p >= 2:
                w = [(sT, pair_pe_done[p - 2])]
            sp = sGa if par == 0 else sGb
            for r2 in range(4):
                base = 768 * (4 * b + r2)
                bd.op("gpsimd", "dma_start", out=Kb[:, par * S + r2 * TLOC:par * S + (r2 + 1) * TLOC],
                      in_=ap["qkv_x"][base + 512:base + 640, :], waits=(w if r2 == 0 else []),
                      incs=[(sp, 16)])
                bd.op("gpsimd", "dma_start", out=Vb[:, par * S + r2 * TLOC:par * S + (r2 + 1) * TLOC],
                      in_=ap["qkv_x"][base + 640:base + 768, :], incs=[(sp, 16)])
                bd.op("gpsimd", "dma_start", out=Qb[:, par * S + r2 * TLOC:par * S + (r2 + 1) * TLOC],
                      in_=ap["qkv_x"][base + hh * 128:base + hh * 128 + 128, :], incs=[(sp, 16)])
            ld = (sp, sp.v)
            v0 = sV.v
            for g4 in range(8):
                pt = ptps[g4 % 2]
                for j in range(4):
                    kt = g4 * 4 + j
                    w2 = []
                    if j == 0:
                        w2.append(ld)
                        if g4 < 2:
                            w2.append(ptev[g4 % 2])
                        else:
                            w2.append((sV, v0 + g4 - 1))
                    bd.op("tensor", "transpose", pt[:, j * 128:(j + 1) * 128],
                          Vb[:, par * S + kt * 128:par * S + (kt + 1) * 128], ident[:],
                          waits=w2, incs=[(sT, 1)] if j == 3 else [])
                bd.op("vector", "tensor_copy",
                      out=VT[:].rearrange("p (c t) -> p c t", t=128)[:, g4 * 4:g4 * 4 + 4, :],
                      in_=pt[:, 0:512].rearrange("p (a b) -> p a b", a=4),
                      waits=[(sT, sT.v)], incs=[(sV, 1)])
                ptev[g4 % 2] = (sV, sV.v)
            def emit_back(ctx):
                gq, qpar, row, resc_v = ctx
                ntk = gq + 1
                gev = []
                for g4 in range(0, ntk, 4):
                    gn = min(4, ntk - g4)
                    bank = (g4 // 4) % 2
                    pt = ptps[bank]
                    for j in range(gn):
                        kt = g4 + j
                        w5 = []
                        if j == 0:
                            w5.append((sV, resc_v))
                            w5.append(ptev[bank])
                        bd.op("tensor", "transpose", pt[:, j * 128:(j + 1) * 128],
                              Psb[:, qpar * S + kt * 128:qpar * S + (kt + 1) * 128],
                              ident[:], waits=w5, incs=[(sT, 1)] if j == gn - 1 else [])
                    bd.op("vector", "tensor_copy",
                          out=PTs[:, qpar * S + g4 * 128:qpar * S + (g4 + gn) * 128].rearrange("p (a b) -> p a b", b=128),
                          in_=pt[:, 0:gn * 128].rearrange("p (a b) -> p a b", a=gn),
                          waits=[(sT, sT.v)], incs=[(sV, 1)])
                    gev.append(sV.v)
                    ptev[bank] = (sV, sV.v)
                for kt in range(ntk):
                    w6 = []
                    if kt % 4 == 0:
                        w6.append((sV, gev[kt // 4]))
                    if kt == 0:
                        w6.append(ovev[qpar])
                    bd.op("tensor", "matmul", ovps[qpar][:, 0:128],
                          VT[:, kt * 128:(kt + 1) * 128],
                          PTs[:, qpar * S + kt * 128:qpar * S + (kt + 1) * 128],
                          start=(kt == 0), stop=(kt == ntk - 1), waits=w6,
                          incs=[(sT, 1)] if kt == ntk - 1 else [])
                pv_done[qpar] = sT.v
                w7 = [(sT, sT.v)]
                if gq < 2:
                    w7.append((sSt2[par], otst[par]))
                bd.op("vector", "tensor_copy", out=otb[:, par * S + gq * 128:par * S + row],
                      in_=ovps[qpar][:, 0:128], waits=w7, incs=[(sV, 1)])
                ovev[qpar] = (sV, sV.v)

            prev_ctx = None
            for gq in range(32):
                qpar = gq % 2
                row = (gq + 1) * 128
                nbk = (row + 511) // 512
                evs = []
                for j in range(nbk):
                    wj = min(512, row - j * 512)
                    w2 = [scev[j % 2]]
                    if p == 0 and gq == 0 and j < 2:
                        w2.append((sV, p7_done_v))
                    bd.op("tensor", "matmul", scps[j % 2][:, 0:wj],
                          Qb[:, par * S + gq * 128:par * S + (gq + 1) * 128],
                          Kb[:, par * S + j * 512:par * S + j * 512 + wj],
                          start=True, stop=True, waits=w2, incs=[(sT, 1)])
                    mmsv = sT.v
                    w3 = [(sT, mmsv)]
                    if gq >= 2 and j == 0:
                        w3.append((sA, exp_done[qpar]))
                    bd.op("scalar", "activation", out=Ssb[:, qpar * S + j * 512:qpar * S + j * 512 + wj],
                          in_=scps[j % 2][:, 0:wj], func=AF.Copy, waits=w3, incs=[(sA, 1)])
                    scev[j % 2] = (sA, sA.v)
                    evs.append(sA.v)
                bd.op("vector", "tensor_add", Ssb[:, qpar * S + gq * 128:qpar * S + row],
                      Ssb[:, qpar * S + gq * 128:qpar * S + row], cmask[:],
                      waits=[(sA, evs[-1])])
                bd.op("vector", "tensor_reduce", out=amx[:, 0:1],
                      in_=Ssb[:, qpar * S:qpar * S + row], axis=AX.X, op=ALU.max,
                      incs=[(sE, 1)])
                bd.op("vector", "tensor_scalar", out=amx[:, 1:2], in0=amx[:, 0:1],
                      scalar1=-1.0, scalar2=None, op0=ALU.mult, waits=[(sE, sE.v)],
                      incs=[(sV, 1)])
                nmv = sV.v
                w4 = [(sV, nmv)]
                if gq >= 2:
                    w4.append((sT, pv_done[qpar]))
                bd.op("scalar", "activation", out=Psb[:, qpar * S:qpar * S + row],
                      in_=Ssb[:, qpar * S:qpar * S + row], func=AF.Exp,
                      bias=amx[:, 1:2], accum_out=amx[:, 2:3], waits=w4, incs=[(sA, 1)])
                exp_done[qpar] = sA.v
                bd.op("vector", "reciprocal", out=amx[:, 3:4], in_=amx[:, 2:3],
                      waits=[(sA, sA.v)], incs=[(sE, 1)])
                bd.op("vector", "tensor_scalar", out=Psb[:, qpar * S:qpar * S + row],
                      in0=Psb[:, qpar * S:qpar * S + row], scalar1=amx[:, 3:4],
                      scalar2=None, op0=ALU.mult, waits=[(sE, sE.v)], incs=[(sV, 1)])
                resc_v = sV.v
                if prev_ctx is not None:
                    emit_back(prev_ctx)
                prev_ctx = (gq, qpar, row, resc_v)
            emit_back(prev_ctx)
            pair_pe_done[p] = sT.v
            ot_v = sV.v
            for r2 in range(4):
                bd.op("scalar", "dma_start",
                      out=ap["at_c"][512 * (4 * b + r2) + hh * 128:512 * (4 * b + r2) + (hh + 1) * 128, :],
                      in_=otb[:, par * S + r2 * TLOC:par * S + (r2 + 1) * TLOC],
                      waits=[(sV, ot_v)], incs=[(sSt2[par], 16)])
            otst[par] = sSt2[par].v
        at_store_w = [(sSt2[0], sSt2[0].v), (sSt2[1], sSt2[1].v)]
        p9_done_t = sT.v
        p9_done_v = sV.v
        bd.wait("gpsimd", *at_store_w)
        bd.op("gpsimd", "collective_compute", "AllToAll", ALU.bypass,
              replica_groups=[list(range(NCORE))], ins=[ap["at_c"].opt()],
              outs=[ap["at_x"].opt()], a2a=True, incs=[(sCC[CC_AT], 1)])
        if stage < 10 and stage >= 3:
            bd.op("gpsimd", "dma_start", out=ap["dbg_at"], in_=ap["at_x"][0:512, :],
                  waits=[(sCC[CC_AT], 1)], incs=[(sG, 16)])

    if stage < 4:
        return _close(nc, bd, stk, sS, sX, sG, sCC, stage)

    with ExitStack() as phD:
        sbE = lambda n, s, d: phD.enter_context(nc.sbuf_tensor(n, s, d))
        psE = lambda n, s, d: phD.enter_context(nc.psum_tensor(n, s, d))
        hbuf = sbE("hbuf", [128, 6144], F32)
        hqt = sbE("hqt", [128, QW], BF16)
        hqfm = sbE("hqfm", [128, FCH * 512], BF16)
        atsb = sbE("atsb", [128, 32 * 512], BF16)
        wdr = sbE("wdr", [128, 4 * 512], BF16)
        wor = sbE("wor", [128, 4 * 512], BF16)
        xrr = sbE("xrr", [128, 4 * 512], F32)
        osb = sbE("osb", [128, 2 * 512], F32)
        lrb_sb = sbE("lrb_sb2", [64, D], BF16)
        fps = [psE(f"fps{i}", [128, 512], F32) for i in range(4)]
        ptF = [psE(f"ptF{i}", [128, 1024], BF16) for i in range(2)]
        pbc2 = psE("pbc2", [128, 512], F32)

        bd.op("gpsimd", "dma_start", out=lrb_sb[:], in_=ap["lrb"],
              waits=[(sT, p9_done_t)] + at_store_w, incs=[(sGc, 16)])
        lrbg = sGc.v
        bd.op("sync", "dma_start", out=srow[0:1, 8:16], in_=ap["hm_g"],
              waits=[(sCC[CC_HM], 1)], incs=[(sXc, 16)])
        bd.op("tensor", "matmul", pbc2[:, 0:1], onesf[:], srow[0:1, 8:9], start=True, stop=True,
              waits=[(sXc, sXc.v), (sT, p9_done_t)], incs=[(sT, 1)])
        bd.op("vector", "tensor_scalar", out=scl[:, 8:9], in0=pbc2[:, 0:1],
              scalar1=1.0 / 2688.0, scalar2=1e-12, op0=ALU.mult, op1=ALU.max,
              waits=[(sT, sT.v), (sV, p9_done_v)], incs=[(sE, 1)])
        bd.op("vector", "tensor_scalar", out=scl[:, 9:10], in0=scl[:, 8:9],
              scalar1=6.0, scalar2=None, op0=ALU.mult, waits=[(sE, sE.v)], incs=[(sE, 1)])
        bd.op("vector", "reciprocal", out=scl[:, 9:10], in_=scl[:, 9:10],
              waits=[(sE, sE.v)], incs=[(sV, 1)])
        hscale_v = sV.v

        HCH = ((0, 1024), (1024, 1024), (2048, 1024), (3072, 1024), (4096, 1024),
               (5120, 1024), (6144, 1024), (7168, 1024), (8192, 1024), (9216, 1024),
               (10240, 768))
        evac_prev = {tr: (sV, hscale_v) for tr in range(4)}
        ptF_ev = {0: (sV, p9_done_v), 1: (sV, p9_done_v)}
        hq_tp_prev = [p9_done_t]
        osb_st = {0: 0, 1: 0}
        prev_wd_done = {i: p9_done_t for i in range(4)}
        prev_wo_done = {i: p9_done_t for i in range(4)}
        for hf in range(2):
            watt = ([(sCC[CC_AT], 1), (sT, p9_done_t)] + at_store_w) if hf == 0 else [(sT, half_done_t)]
            for dc in range(32):
                bd.op("sync", "dma_start", out=atsb[:, dc * 512:(dc + 1) * 512],
                      in_=ap["at_x"][(dc // 4) * 512 + (dc % 4) * 128:(dc // 4) * 512 + (dc % 4) * 128 + 128,
                                     hf * 512:hf * 512 + 512],
                      waits=(watt if dc == 0 else []), incs=[(sXat, 16)])
            at_ld = sXat.v
            hq_v = hscale_v if hf == 0 else hq_v
            for tr in range(4):
                tt = hf * 4 + tr
                for hs0, hw in ((0, 6144), (6144, 4864)):
                    w = [(sV, sV.v), (sA, qstate["last_cast"])] + p7_spill_w + at_store_w
                    if hf == 1 and tr == 0 and hs0 == 0:
                        w.append((sT, half_done_t))
                    bd.op("sync", "dma_start", out=hbuf[:, 0:hw],
                          in_=ap["hsp"][tt * 128:(tt + 1) * 128, hs0:hs0 + hw],
                          waits=w, incs=[(sXh, 16)])
                    hld = (sXh, sXh.v)
                    for (c0, cw) in HCH:
                        if c0 < hs0 or c0 >= hs0 + hw:
                            continue
                        qc = cw - (128 if c0 + cw == Fm else 0)
                        a_last = None
                        if qc > 0:
                            _, a_last = qdq_tile(hbuf[:, c0 - hs0:c0 - hs0 + qc], hqt[:, 0:qc],
                                                 128, qc, scl[:, 8:9], scl[:, 9:10], [hld],
                                                 cast_waits=[(sT, hq_tp_prev[0])])
                        if qc < cw:
                            bd.op("scalar", "activation", out=hqt[:, qc:cw],
                                  in_=hbuf[:, c0 - hs0 + qc:c0 - hs0 + cw],
                                  func=AF.Copy, waits=[(sT, hq_tp_prev[0])], incs=[(sA, 1)])
                            a_last = sA.v
                            qstate["last_cast"] = sA.v
                        nch = cw // 128
                        for g0 in range(0, nch, 4):
                            gn = min(4, nch - g0)
                            bank = (g0 // 4 + c0 // 512) % 2
                            pt = ptF[bank]
                            for j in range(gn):
                                w5 = []
                                if j == 0:
                                    w5.append((sA, a_last))
                                    w5.append(ptF_ev[bank])
                                bd.op("tensor", "transpose", pt[:, j * 128:(j + 1) * 128],
                                      hqt[:, (g0 + j) * 128:(g0 + j + 1) * 128], ident[:],
                                      waits=w5, incs=[(sT, 1)] if j == gn - 1 else [])
                            fc0 = c0 // 128 + g0
                            bd.op("vector", "tensor_copy",
                                  out=hqfm[:].rearrange("p (f q) -> p f q", q=512)[:, fc0:fc0 + gn, tr * 128:tr * 128 + 128],
                                  in_=pt[:, 0:gn * 128].rearrange("p (a b) -> p a b", a=gn),
                                  waits=[(sT, sT.v)], incs=[(sV, 1)])
                            ptF_ev[bank] = (sV, sV.v)
                        hq_tp_prev[0] = sT.v
                hq_v = sV.v
            half_hq_v = sV.v
            for ob in range(8):
                wd_ld = {}
                wo_ld = {}
                for fc in range(4):
                    w = [(sT, prev_wd_done[fc % 4])]
                    if hf == 0 and ob == 0 and fc == 0:
                        w.append((sV, half_hq_v))
                    bd.op("sync", "dma_start", out=wdr[:, fc * 512:(fc + 1) * 512],
                          in_=ap["wdt_g"][ob * Fm + fc * 128:ob * Fm + (fc + 1) * 128, :],
                          waits=w, incs=[(sXw[fc % 4], 16)])
                    wd_ld[fc] = (sXw[fc % 4], sXw[fc % 4].v)
                mm_slot = {}
                for fc in range(FCH):
                    for tr in range(4):
                        w2 = [wd_ld[fc]] if tr == 0 else []
                        if fc == 0:
                            w2.append(evac_prev[tr])
                            if tr == 0:
                                w2.append((sV, half_hq_v))
                        bd.op("tensor", "matmul", fps[tr][:],
                              hqfm[:, fc * 512 + tr * 128:fc * 512 + tr * 128 + 128],
                              wdr[:, (fc % 4) * 512:(fc % 4) * 512 + 512],
                              start=(fc == 0), stop=False, waits=w2, incs=[(sT, 1)] if tr == 3 else [])
                    mm_slot[fc % 4] = sT.v
                    nfc = fc + 4
                    if nfc < FCH:
                        bd.op("sync", "dma_start", out=wdr[:, (nfc % 4) * 512:(nfc % 4) * 512 + 512],
                              in_=ap["wdt_g"][ob * Fm + nfc * 128:ob * Fm + (nfc + 1) * 128, :],
                              waits=[(sT, mm_slot[nfc % 4])], incs=[(sXw[nfc % 4], 16)])
                        wd_ld[nfc] = (sXw[nfc % 4], sXw[nfc % 4].v)
                prev_wd_done = dict(mm_slot)
                for dc in range(4):
                    w = [(sT, prev_wo_done[dc % 4])]
                    if dc == 0:
                        w.append((sXat, at_ld))
                    bd.op("gpsimd", "dma_start", out=wor[:, dc * 512:(dc + 1) * 512],
                          in_=ap["wo"][ob, dc], waits=w, incs=[(sGw[dc % 4], 16)])
                    wo_ld[dc] = (sGw[dc % 4], sGw[dc % 4].v)
                for dc in range(32):
                    for tr in range(4):
                        w2 = [wo_ld[dc]] if tr == 0 else []
                        bd.op("tensor", "matmul", fps[tr][:],
                              atsb[:, dc * 512 + tr * 128:dc * 512 + tr * 128 + 128],
                              wor[:, (dc % 4) * 512:(dc % 4) * 512 + 512],
                              start=False, stop=False, waits=w2, incs=[(sT, 1)] if tr == 3 else [])
                    mm_slot[dc % 4] = sT.v
                    ndc = dc + 4
                    if ndc < 32:
                        bd.op("gpsimd", "dma_start", out=wor[:, (ndc % 4) * 512:(ndc % 4) * 512 + 512],
                              in_=ap["wo"][ob, ndc], waits=[(sT, mm_slot[ndc % 4])],
                              incs=[(sGw[ndc % 4], 16)])
                        wo_ld[ndc] = (sGw[ndc % 4], sGw[ndc % 4].v)
                prev_wo_done = dict(mm_slot)
                for tr in range(4):
                    tt = hf * 4 + tr
                    bd.op("sync", "dma_start", out=xrr[:, tr * 512:(tr + 1) * 512],
                          in_=ap["xres"][tt * 128:(tt + 1) * 128, ob * 512:(ob + 1) * 512],
                          waits=([(sSo[tr % 2], osb_st[tr % 2])] if (hf > 0 or ob > 0) else []),
                          incs=[(sXr, 16)])
                xr_ld = sXr.v
                for tr in range(4):
                    w2 = [(sGc, lrbg)]
                    bd.op("tensor", "matmul", fps[tr][:],
                          tfm[:, hf * 512 + tr * 128:hf * 512 + tr * 128 + 128],
                          lrb_sb[:, ob * 512:(ob + 1) * 512],
                          start=False, stop=True, waits=w2, incs=[(sT, 1)])
                    lr_mm = sT.v
                    bd.op("vector", "tensor_add", osb[:, (tr % 2) * 512:(tr % 2) * 512 + 512],
                          fps[tr][:], xrr[:, tr * 512:(tr + 1) * 512],
                          waits=[(sT, lr_mm), (sXr, xr_ld),
                                 (sSo[tr % 2], osb_st[tr % 2])], incs=[(sV, 1)])
                    evac_prev[tr] = (sV, sV.v)
                    tt = hf * 4 + tr
                    bd.op("scalar", "dma_start",
                          out=ap["out"][tt * 128:(tt + 1) * 128, ob * 512:(ob + 1) * 512],
                          in_=osb[:, (tr % 2) * 512:(tr % 2) * 512 + 512],
                          waits=[(sV, sV.v)], incs=[(sSo[tr % 2], 16)])
                    osb_st[tr % 2] = sSo[tr % 2].v
            half_done_t = sT.v
        out_w = [(sSo[0], sSo[0].v), (sSo[1], sSo[1].v)]

    return _close(nc, bd, stk, sS, sX, sG, sCC, stage, extra=out_w)


def _close(nc, bd, stk, sS, sX, sG, sCC, stage, extra=()):
    bd.wait("sync", (sS, sS.v), (sG, sG.v), *extra)
    with nc.Block() as block:
        bd.emit(block)
    stk.close()
    return nc


def _host_prepare(inputs):
    f32 = np.float32
    x = np.asarray(inputs["L0_attn_x"], f32).reshape(B * S, D)
    Wq = np.asarray(inputs["L0_attn_Wq"], f32)
    Wk = np.asarray(inputs["L0_attn_Wk"], f32)
    Wv = np.asarray(inputs["L0_attn_Wv"], f32)
    Wo = np.asarray(inputs["L0_attn_Wo"], f32)
    wg = np.asarray(inputs["L0_ffn_w_gate"], f32)
    wu = np.asarray(inputs["L0_ffn_w_up"], f32)
    wd = np.asarray(inputs["L0_ffn_w_down"], f32)
    msc = np.asarray(inputs["L0_ffn_mlp_output_scale"], f32)
    lrA = np.asarray(inputs["L0_ffn_lr_A"], f32)
    lrB = np.asarray(inputs["L0_ffn_lr_B"], f32)
    up = np.asarray(inputs["L0_ffn_up_reorder_index"], np.int64)
    dn = np.asarray(inputs["L0_ffn_down_reorder_index"], np.int64)

    rope = np.concatenate([np.concatenate([h * HD + np.arange(0, HD, 2),
                                           h * HD + np.arange(1, HD, 2)]) for h in range(H)])
    rope_kv = rope[:KVH * HD]
    WQKV = np.concatenate([Wq[up][:, rope], Wk[up][:, rope_kv[:KVH * HD]], Wv[up]], axis=1)
    wq_t = np.ascontiguousarray(WQKV.reshape(32, 128, 48, 128).transpose(2, 1, 0, 3))
    wo_t = np.ascontiguousarray(Wo.reshape(32, 128, 8, 512).transpose(2, 0, 1, 3))
    wg_p = wg[dn][:, up]
    wu_p = wu[dn][:, up]
    wd_p = wd[:, dn]
    lra_t = np.ascontiguousarray(lrA[up].reshape(32, 128, 64).transpose(1, 0, 2))

    inv_freq = 1.0 / (10000.0 ** (np.arange(0, HD, 2, dtype=np.float64) / HD))
    ident = np.eye(128, dtype=ml_dtypes.bfloat16)
    icon = np.tile(np.array([[I_ABS, I_TRUNC, I_EXP, I_SIGN]], dtype=np.int32), (128, 1))
    ones_bf = np.ones((1, 128), dtype=ml_dtypes.bfloat16)
    ones_f = np.ones((1, 128), dtype=np.float32)
    qi, ki = np.meshgrid(np.arange(128), np.arange(128), indexing="ij")
    cmask = np.where(ki > qi, np.float32(NEG), np.float32(0.0)).astype(np.float32)

    in_maps = []
    for c in range(NCORE):
        pos = np.arange(1024 * (c % 4), 1024 * (c % 4) + TLOC, dtype=np.float64)
        ang = pos[:, None] * inv_freq[None, :]
        cos = np.cos(ang).T.astype(f32)
        sin = np.sin(ang).T.astype(f32)
        sc = np.float32(1.0 / np.sqrt(HD))
        m = {
            "xp": np.ascontiguousarray(x[c * TLOC:(c + 1) * TLOC][:, up]),
            "xres": np.ascontiguousarray(x[c * TLOC:(c + 1) * TLOC]),
            "wq": wq_t, "wo": wo_t,
            "wg": np.ascontiguousarray(wg_p[c * FSL:(c + 1) * FSL]),
            "wu": np.ascontiguousarray(wu_p[c * FSL:(c + 1) * FSL]),
            "wd": np.ascontiguousarray(wd_p[c * WDSL:(c + 1) * WDSL]),
            "msc": np.ascontiguousarray(msc[c * WDSL:(c + 1) * WDSL].reshape(4, 128).T),
            "lra": lra_t, "lrb": lrB,
            "cosq": np.ascontiguousarray(cos * sc), "sinq": np.ascontiguousarray(sin * sc),
            "cosk": cos, "sink": sin,
            "ident": ident, "icon": icon, "ones": ones_bf, "onesf": ones_f,
            "fcon": np.tile(np.array([[1.25, 0.0, 0.0, 0.0]], dtype=np.float32), (128, 1)),
            "cmask": cmask,
        }
        in_maps.append(m)
    return in_maps


def kernel(**inputs):
    in_maps = _host_prepare(inputs)
    nc = build(stage=10)
    res = run_bass_kernel_spmd(nc, in_maps, core_ids=list(range(NCORE)))
    out = np.concatenate([np.asarray(res.results[c]["out"]) for c in range(NCORE)], axis=0)
    return out.reshape(B, S, D).astype(np.float32)
